# version 16
# baseline (speedup 1.0000x reference)
import numpy as np
import ml_dtypes
from contextlib import ExitStack

import concourse.bass as bass
import concourse.mybir as mybir
from concourse.bass_utils import run_bass_kernel_spmd

F32 = mybir.dt.float32
BF16 = mybir.dt.bfloat16
AF = mybir.ActivationFunctionType
ALU = mybir.AluOpType

D = 2048
T = 1024
TT = 512
NCH = 16
NH = 16
HD = 128
HID = 8192
HCH = 64
EPS = 1e-5
PAD = 512

ENGS = ("tensor", "vector", "scalar", "sync", "gpsimd")


class Sem:
    def __init__(self, stack, nc, name):
        self.h = stack.enter_context(nc.semaphore(name))
        self.n = 0
        self.name = name


class Bank:
    def __init__(self, ap, idx):
        self.ap = ap
        self.idx = idx
        self.free_evt = None


class Builder:
    def __init__(self, nc, stack):
        self.nc = nc
        self.stack = stack
        self.plan = {e: [] for e in ENGS}
        self.wm = {e: {} for e in ENGS}
        self.sem = {k: Sem(stack, nc, k) for k in
                    ("tE", "vE", "sE", "gE", "cc")}
        self.chans = {}
        self.cur_scope = "p0"
        self.ln_slot = [0, 0]
        self.tmp_slot = [(None, 0), (None, 0)]
        self.rb_consumed = 0

    def scope(self, name):
        self.cur_scope = name

    def em(self, eng, fn):
        self.plan[eng].append((self.cur_scope, fn))

    def wait(self, eng, sem, val):
        if val is None or val <= 0:
            return
        if self.wm[eng].get(sem.name, 0) >= val:
            return
        self.wm[eng][sem.name] = val
        self.plan[eng].append((self.cur_scope, lambda e, s=sem.h, v=val: e.wait_ge(s, v)))

    def waite(self, eng, evt):
        if evt is not None and evt[0] is not None:
            self.wait(eng, evt[0], evt[1])

    def chan(self, name):
        if name not in self.chans:
            self.chans[name] = Sem(self.stack, self.nc, "ch_" + name)
        return self.chans[name]

    def dma(self, eng, chname, fn):
        ch = self.chan(chname)
        ch.n += 16
        v = ch.n
        self.plan[eng].append((self.cur_scope, lambda e, f=fn, s=ch.h: f(e).then_inc(s, 16)))
        return (ch, v)

    ENGSEM = {"vector": "vE", "scalar": "sE", "gpsimd": "gE"}

    def op(self, eng, fn):
        sem = self.sem[self.ENGSEM[eng]]
        if sem.n > 0:
            self.wait(eng, sem, sem.n)
        sem.n += 1
        v = sem.n
        self.plan[eng].append((self.cur_scope, lambda e, f=fn, s=sem.h: f(e).then_inc(s, 1)))
        return v

    def inc(self, eng, sem, fn, k=1):
        sem.n += k
        v = sem.n
        self.plan[eng].append((self.cur_scope, lambda e, f=fn, s=sem.h, kk=k: f(e).then_inc(s, kk)))
        return v

    def run(self, eng, fn):
        self.plan[eng].append((self.cur_scope, fn))

    def flush(self, block):
        nc = self.nc
        for eng in ENGS:
            lst = self.plan[eng]
            if not lst:
                continue
            def body(e, lst=lst):
                cur = None
                cur_id = None
                for sc, fn in lst:
                    if sc != cur:
                        if cur is not None:
                            nc.leave_named_scope(cur, cur_id, False)
                        if sc is not None:
                            cur_id, _ = nc.enter_named_scope(sc, False)
                        cur = sc
                    fn(e)
                if cur is not None:
                    nc.leave_named_scope(cur, cur_id, False)
            getattr(block, eng)(body)

    def init_psum(self):
        self.banks = []
        for i in range(8):
            t = self.stack.enter_context(
                self.nc.psum_tensor("ps%d" % i, [128, TT], F32))
            self.banks.append(Bank(t, i))
        self._rr = 0

    def bank(self):
        bk = self.banks[self._rr]
        self._rr = (self._rr + 1) % 8
        return bk


def build_program(stop=None):
    nc = bass.Bass("TRN2", target_bir_lowering=False)
    stack = ExitStack()
    b = Builder(nc, stack)
    S = b.sem

    dt = {}
    def din(name, shape, dtype=F32):
        dt[name] = nc.dram_tensor(name, shape, dtype, kind="ExternalInput")
        return dt[name]
    def dint(name, shape, dtype=BF16):
        dt[name] = nc.dram_tensor(name, shape, dtype)
        return dt[name]

    x0h = din("x0", [D, T])
    din("in_w", [D, D]); din("out_w", [D, D])
    din("w1a", [D, HID]); din("w2a", [HID, D])
    din("ssm_mats", [128, NH * 5 * HD])
    din("wq", [D, D]); din("wkv", [D, D]); din("wo", [D, D])
    din("fw1", [D, HID]); din("fw2", [HID, D]); din("fw3", [D, HID])
    din("vcolsin", [128, 7 * 16])
    din("b1colsin", [128, 64])
    din("maskc", [128, 8])
    din("consts_b", [128, 10 * 128], BF16)
    din("onescol_f", [128, 1])

    y_out = nc.dram_tensor("y", [D, T], F32, kind="ExternalOutput")
    dbg = nc.dram_tensor("dbg", [D, T], F32, kind="ExternalOutput")

    ssmb = dint("ssm_bf", [128, NH * 5 * HD])
    w1T = dint("w1T", [D, HID]); w3T = dint("w3T", [D, HID])
    pw_dram = dint("pw_dram", [NH * HD, 9 * 128])
    u_dram = dint("u_dram", [D, T])
    s_dram = dint("s_dram", [D, T])
    w2bf = dint("w2bf", [HID, D])
    w2t3 = dint("w2t3", [HID, D])
    w1abf = dint("w1abf", [D, HID]); w2abf = dint("w2abf", [HID, D])
    cc1_in = nc.dram_tensor("cc1_in", [128, 16], F32)
    cc1_out = nc.dram_tensor("cc1_out", [8 * 128, 16], F32)
    cc2_in = nc.dram_tensor("cc2_in", [128, 16], F32)
    cc2_out = nc.dram_tensor("cc2_out", [8 * 128, 16], F32)

    sbuf = lambda name, shape, dtype: stack.enter_context(
        nc.sbuf_tensor(name, shape, dtype))

    St = sbuf("S", [128, NCH * T], F32)
    BB1 = sbuf("BB1", [128, NCH * T], BF16)
    TBu = sbuf("TBu", [128, NCH * T], BF16)
    NB = sbuf("NB", [128, NCH * TT], BF16)
    WA = sbuf("WA", [128, 4096], BF16)
    UH = sbuf("UH", [128, 2 * T], BF16)
    SST = sbuf("SST", [128, 2 * T], BF16)
    XBs = sbuf("XBs", [128, 2 * TT], BF16)
    XQs = sbuf("XQs", [128, 2 * TT], BF16)
    TMP = sbuf("TMP", [128, 2 * TT], F32)
    XT = sbuf("XT", [128, 2 * TT], F32)
    ROWS = sbuf("ROWS", [1, 4 * TT], F32)
    RB = sbuf("RB", [1, 2 * TT], BF16)
    SSMW = sbuf("SSMW", [128, 46 * 128], BF16)
    XS0 = sbuf("XS0", [128, PAD + T], BF16)
    XS1 = sbuf("XS1", [128, PAD + T], BF16)
    VBt = sbuf("VBt", [128, T], BF16)
    CAR = sbuf("CAR", [128, NH], F32)
    CARB = sbuf("CARB", [128, NH], BF16)
    GG = sbuf("GG", [128, 8 * NH], F32)
    HIN = sbuf("HIN", [128, NH], F32)
    VCOLS = sbuf("VCOLS", [128, 7 * 16], F32)
    B1C = sbuf("B1C", [128, 64], F32)
    MASK = sbuf("MASK", [128, 8], F32)
    CONS = sbuf("CONS", [128, 10 * 128], BF16)
    ONEC = sbuf("ONEC", [128, 1], F32)
    CONSTF = sbuf("CONSTF", [128, 4], F32)

    b.init_psum()

    def cons(i):
        return CONS[:, i * 128:(i + 1) * 128]
    ONESROW_B = cons(0)[0:1, :]
    ONESCOL_B = cons(0)[:, 0:1]
    I128 = cons(1)
    H128 = cons(2)
    H64BD = cons(3)
    DFTC = cons(4)
    DFTS = cons(5)
    IDFTC = cons(6)
    IDFTS = cons(7)

    VKEYS = ["ln1w", "ln1b", "ln2w", "ln2b", "in_b", "out_b", "b2"]
    def vcol(key, c):
        i = VKEYS.index(key)
        return VCOLS[:, i * 16 + c: i * 16 + c + 1]
    def b1col(c):
        return B1C[:, c:c + 1]

    x0_lds = []
    for xp in range(4):
        x0_lds.append(b.dma("sync", "x0_%d" % xp, lambda e, xp=xp: e.dma_start(
            out=St[:, xp * 4 * T:(xp + 1) * 4 * T].rearrange("p (c t) -> p c t", c=4),
            in_=x0h[xp * 512:(xp + 1) * 512, :].rearrange("(c p) t -> p c t", p=128))))
    x0_ld = x0_lds[-1]
    b.dma("sync", "cons", lambda e: e.dma_start(
        out=CONS[:, :], in_=dt["consts_b"].ap()))
    b.dma("sync", "cons", lambda e: e.dma_start(
        out=MASK[:, :], in_=dt["maskc"].ap()))
    b.dma("sync", "cons", lambda e: e.dma_start(
        out=ONEC[:, :], in_=dt["onescol_f"].ap()))
    b.dma("sync", "cons", lambda e: e.dma_start(
        out=VCOLS[:, :], in_=dt["vcolsin"].ap()))
    cons_grp = b.dma("sync", "cons", lambda e: e.dma_start(
        out=B1C[:, :], in_=dt["b1colsin"].ap()))
    b.op("vector", lambda e: e.memset(CONSTF[:, 0:1], float(EPS)))
    b.op("vector", lambda e: e.memset(CONSTF[:, 1:2], 0.0))
    constf_ev = b.op("vector", lambda e: e.memset(CONSTF[:, 2:3], 1.0))
    EPSC = CONSTF[0:1, 0:1]

    cast_evts = {}
    cast_piece_evts = {}
    cast_queue = []
    cast_ch_state = {"i": 0, "last": [None] * 6}
    def cast_chunked(name, src, dst, rows, piece):
        for r0 in range(0, rows, piece):
            r1 = min(rows, r0 + piece)
            cast_queue.append((name, src, dst, r0, r1))

    def cast_pump(n=1, pace=True):
        for _ in range(n):
            if not cast_queue:
                return
            name, srct, dstt, r0, r1 = cast_queue.pop(0)
            if pace:
                b.wait("gpsimd", S["tE"], S["tE"].n)
            ci = cast_ch_state["i"]; cast_ch_state["i"] = (ci + 1) % 6
            b.waite("gpsimd", cast_ch_state["last"][ci])
            ev = b.dma("gpsimd", "cast%d" % ci,
                       lambda e, r0=r0, r1=r1, srct=srct, dstt=dstt:
                       e.dma_start(out=dstt[r0:r1, :], in_=srct[r0:r1, :]))
            cast_ch_state["last"][ci] = ev
            cast_evts[name] = ev
            cast_piece_evts.setdefault(name, []).append(ev)

    def cast_flush(names):
        while any(q[0] in names for q in cast_queue):
            cast_pump(1)

    cast_chunked("ssm", dt["ssm_mats"], ssmb, 128, 128)
    cast_pump(1, pace=False)
    for _name, _s, _d, _r, _p in (
            ("w1a", dt["w1a"], w1abf, D, 256),
            ("w2a", dt["w2a"], w2abf, HID, 256),
            ("w1", dt["fw1"], w1T, D, 256),
            ("w3", dt["fw3"], w3T, D, 256),
            ("w2", dt["fw2"], w2bf, HID, 256)):
        cast_chunked(_name, _s, _d, _r, _p)

    def emit_ln(src_ch, dst_ch, w_key=None, src_f32=True, src_gate=None,
                dst_gate=None):
        stats_g = [None] * NCH
        bm = b.bank(); bs = b.bank()
        b.waite("tensor", bm.free_evt); b.waite("tensor", bs.free_evt)
        for c in range(NCH):
            sl = c % 2
            xb_ap = XBs[:, sl * TT:(sl + 1) * TT]
            xq_ap = XQs[:, sl * TT:(sl + 1) * TT]
            if c >= 2:
                b.wait("vector", S["tE"], stats_g[c - 2])
            else:
                b.wait("vector", S["tE"], b.ln_slot[sl])
            if src_gate is not None:
                b.waite("vector", src_gate(c))
            if src_f32:
                b.op("vector", lambda e, c=c, xb=xb_ap: e.tensor_copy(xb, src_ch(c)))
                ev = b.op("vector", lambda e, xb=xb_ap, xq=xq_ap:
                           e.tensor_tensor(xq, xb, xb, ALU.mult))
                stats_src = xb_ap
            else:
                ev = b.op("vector", lambda e, c=c, xq=xq_ap:
                           e.tensor_tensor(xq, src_ch(c), src_ch(c), ALU.mult))
                stats_src = src_ch(c)
            b.wait("tensor", S["vE"], ev)
            first = (c == 0); last = (c == NCH - 1)
            b.run("tensor", lambda e, s=stats_src, bm=bm, f=first, l=last:
                  e.matmul(bm.ap[0:1, :], ONESCOL_B, s, start=f, stop=l))
            stats_g[c] = b.inc("tensor", S["tE"],
                               lambda e, xq=xq_ap, bs=bs, f=first, l=last:
                               e.matmul(bs.ap[0:1, :], ONESCOL_B, xq, start=f, stop=l))
        b.ln_slot[0] = stats_g[NCH - 2]
        b.ln_slot[1] = stats_g[NCH - 1]
        g_stats = stats_g[NCH - 1]

        m_row = ROWS[0:1, 0:TT]; r_row = ROWS[0:1, TT:2 * TT]
        q_row = ROWS[0:1, 2 * TT:3 * TT]; s_row = ROWS[0:1, 3 * TT:4 * TT]
        b.wait("vector", S["tE"], g_stats)
        b.op("vector", lambda e, bm=bm: e.tensor_scalar_mul(m_row, bm.ap[0:1, 0:TT], 1.0 / D))
        b.op("vector", lambda e, bs=bs: e.tensor_scalar_mul(s_row, bs.ap[0:1, 0:TT], 1.0 / D))
        b.op("vector", lambda e: e.tensor_tensor(q_row, m_row, m_row, ALU.mult))
        ev = b.op("vector", lambda e: e.tensor_tensor(s_row, s_row, q_row, ALU.subtract))
        bm.free_evt = (S["vE"], ev); bs.free_evt = (S["vE"], ev)
        b.wait("scalar", S["vE"], max(ev, constf_ev))
        ev = b.op("scalar", lambda e: e.activation(
            s_row, s_row, AF.Sqrt, bias=EPSC))
        b.wait("vector", S["sE"], ev)
        b.op("vector", lambda e: e.reciprocal(r_row, s_row))
        b.op("vector", lambda e: e.tensor_tensor(q_row, m_row, r_row, ALU.mult))
        rb_row = RB[0:1, 0:TT]; qb_row = RB[0:1, TT:2 * TT]
        b.wait("vector", S["tE"], b.rb_consumed)
        b.op("vector", lambda e: e.tensor_copy(rb_row, r_row))
        ev = b.op("vector", lambda e: e.tensor_copy(qb_row, q_row))

        br = b.bank(); bq = b.bank()
        b.waite("tensor", br.free_evt); b.waite("tensor", bq.free_evt)
        b.wait("tensor", S["vE"], ev)
        b.run("tensor", lambda e, br=br: e.matmul(br.ap[:, :], ONESROW_B, rb_row,
                                                  start=True, stop=True))
        g_bc = b.inc("tensor", S["tE"], lambda e, bq=bq: e.matmul(
            bq.ap[:, :], ONESROW_B, qb_row, start=True, stop=True))
        b.rb_consumed = g_bc
        b.wait("vector", S["tE"], g_bc)
        if dst_gate is not None:
            b.waite("vector", dst_gate)
            if w_key is not None:
                b.waite("scalar", dst_gate)
        last_ev = None
        for c in range(NCH):
            sl = c % 2
            t1 = TMP[:, sl * TT:(sl + 1) * TT]
            b.waite("vector", b.tmp_slot[sl])
            b.op("vector", lambda e, c=c, t1=t1, br=br:
                  e.tensor_tensor(t1, src_ch(c), br.ap[:, :], ALU.mult))
            if w_key is None:
                ev2 = b.op("vector", lambda e, c=c, t1=t1, bq=bq:
                            e.tensor_tensor(dst_ch(c), t1, bq.ap[:, :], ALU.subtract))
                b.tmp_slot[sl] = (S["vE"], ev2)
                last = (S["vE"], ev2)
            else:
                ev2 = b.op("vector", lambda e, c=c, t1=t1, bq=bq:
                            e.tensor_tensor(t1, t1, bq.ap[:, :], ALU.subtract))
                b.wait("scalar", S["vE"], ev2)
                ev3 = b.op("scalar", lambda e, c=c, t1=t1:
                            e.activation(dst_ch(c), t1, AF.Identity,
                                         bias=vcol(w_key[1], c), scale=vcol(w_key[0], c)))
                b.tmp_slot[sl] = (S["sE"], ev3)
                last = (S["sE"], ev3)
            last_ev = last
        br.free_evt = last_ev if w_key is None else (S["vE"], ev2)
        bq.free_evt = br.free_evt
        return last_ev

    stopped = [False]

    def dump_bf16_full(buf, done_evt):
        b.waite("vector", done_evt)
        st_ev = [None, None]
        for c in range(NCH):
            for tau in range(2):
                sl = (2 * c + tau) % 2
                xt = XT[:, sl * TT:(sl + 1) * TT]
                b.waite("vector", st_ev[sl])
                ev = b.op("vector", lambda e, c=c, tau=tau, xt=xt:
                           e.tensor_copy(xt, buf[:, c * T + tau * TT: c * T + (tau + 1) * TT]))
                b.wait("scalar", S["vE"], ev)
                st_ev[sl] = b.dma("scalar", "dmp%d" % sl, lambda e, c=c, tau=tau, xt=xt:
                              e.dma_start(out=dbg[c * 128:(c + 1) * 128,
                                                  tau * TT:(tau + 1) * TT], in_=xt))
        b.waite("scalar", st_ev[0])
        b.waite("scalar", st_ev[1])
        stopped[0] = True

    def dump_f32_full(buf, done_evt):
        b.waite("sync", done_evt)
        ev = b.dma("sync", "out", lambda e: e.dma_start(
            out=dbg.ap().rearrange("(c p) t -> p c t", p=128),
            in_=buf[:, :].rearrange("p (c t) -> p c t", c=NCH)))
        b.waite("sync", ev)
        stopped[0] = True

    def finish():
        with nc.Block() as block:
            b.flush(block)
        stack.close()
        return nc

    b.scope('p1_ln0')
    b.waite("tensor", cons_grp)
    for tau in range(2):
        h1_done = emit_ln(
            src_ch=lambda c, tau=tau: St[:, c * T + tau * TT: c * T + (tau + 1) * TT],
            dst_ch=lambda c, tau=tau: BB1[:, c * T + tau * TT: c * T + (tau + 1) * TT],
            src_f32=True, src_gate=lambda c: x0_lds[c // 4])
    if stop == "h1":
        dump_bf16_full(BB1, h1_done)
        return finish()

    b.scope('p2_ln1')
    for tau in range(2):
        t_done = emit_ln(
            src_ch=lambda c, tau=tau: BB1[:, c * T + tau * TT: c * T + (tau + 1) * TT],
            dst_ch=lambda c, tau=tau: TBu[:, c * T + tau * TT: c * T + (tau + 1) * TT],
            w_key=("ln1w", "ln1b"), src_f32=False)
    if stop == "t":
        dump_bf16_full(TBu, t_done)
        return finish()

    proj_state = {"wa_evt": [None, None], "ld_i": 0}

    def emit_proj(w_hbm, src_ch, evac, n_k=NCH, n_j=NCH, src_evt=None,
                  taus=(0, 1), shared_tau=True, mode="f32", w_ready=None):
        ps = proj_state
        if isinstance(w_ready, str):
            for pe_ in cast_piece_evts[w_ready]:
                b.waite("sync", pe_)
        elif w_ready is not None:
            b.waite("sync", w_ready)
        jg_size = 2 if shared_tau else 4
        tau_groups = [list(taus)] if shared_tau else [[t] for t in taus]
        gend = None
        for tg in tau_groups:
            for jg in range(n_j // jg_size):
                cast_pump(1)
                banks = {}
                for jj in range(jg_size):
                    for tau in tg:
                        bk = b.bank()
                        b.waite("tensor", bk.free_evt)
                        banks[(jj, tau)] = bk
                if src_evt is not None:
                    b.waite("tensor", src_evt)
                jw = jg_size * 128
                if mode in ("bf16", "f32cast"):
                    assert n_k % 4 == 0 and jw in (256, 512)
                    eng_q = "sync" if mode == "bf16" else "gpsimd"
                    chp = "wa" if mode == "bf16" else "wacast"
                    for k4 in range(n_k // 4):
                        sl = ps["ld_i"] % 2
                        ps["ld_i"] += 1
                        wa = WA[:, sl * 2048: sl * 2048 + 4 * jw]
                        b.waite(eng_q, ps["wa_evt"][sl])
                        ld = b.dma(eng_q, chp + "%d" % sl,
                                   lambda e, k4=k4, jg=jg, wa=wa, jw=jw:
                                   e.dma_start(
                                       out=wa.rearrange("p (k c) -> p k c", k=4),
                                       in_=w_hbm[k4 * 512:(k4 + 1) * 512,
                                                 jg * jw:(jg + 1) * jw]
                                       .rearrange("(k p) c -> p k c", p=128)))
                        b.waite("tensor", ld)
                        gk = None
                        for dk in range(4):
                            k = k4 * 4 + dk
                            first = (k == 0); last = (k == n_k - 1)
                            for jj in range(jg_size):
                                for tau in tg:
                                    bk = banks[(jj, tau)]
                                    fn = (lambda e, bk=bk, wa=wa, jj=jj, k=k, dk=dk,
                                          tau=tau, f=first, l=last:
                                          e.matmul(bk.ap[:, :],
                                                   wa[:, dk * jw + jj * 128: dk * jw + (jj + 1) * 128],
                                                   src_ch(k, tau), start=f, stop=l))
                                    if dk == 3 and jj == jg_size - 1 and tau == tg[-1]:
                                        gk = b.inc("tensor", S["tE"], fn)
                                    else:
                                        b.run("tensor", fn)
                        ps["wa_evt"][sl] = (S["tE"], gk)
                else:
                    for k in range(n_k):
                        sl = ps["ld_i"] % 2
                        ps["ld_i"] += 1
                        wa = WA[:, sl * 2048: sl * 2048 + jw]
                        b.waite("sync", ps["wa_evt"][sl])
                        ld = b.dma("sync", "wa%d" % sl, lambda e, k=k, jg=jg, wa=wa, jw=jw:
                                   e.dma_start(out=wa,
                                               in_=w_hbm[jg * jw:(jg + 1) * jw,
                                                         k * 128:(k + 1) * 128],
                                               transpose=True))
                        b.waite("tensor", ld)
                        first = (k == 0); last = (k == n_k - 1)
                        gk = None
                        for jj in range(jg_size):
                            for tau in tg:
                                bk = banks[(jj, tau)]
                                fn = (lambda e, bk=bk, wa=wa, jj=jj, k=k, tau=tau,
                                      f=first, l=last:
                                      e.matmul(bk.ap[:, :], wa[:, jj * 128:(jj + 1) * 128],
                                               src_ch(k, tau), start=f, stop=l))
                                if jj == jg_size - 1 and tau == tg[-1]:
                                    gk = b.inc("tensor", S["tE"], fn)
                                else:
                                    b.run("tensor", fn)
                        ps["wa_evt"][sl] = (S["tE"], gk)
                gend = gk
                for jj in range(jg_size):
                    for tau in tg:
                        evac(jg * jg_size + jj, tau, banks[(jj, tau)], gend)
        return gend

    b.scope('p3_inproj')
    u_st = [None]
    def evac_inproj(j, tau, bk, gend):
        sl = j % 2
        dst = UH[:, sl * T + tau * TT: sl * T + (tau + 1) * TT]
        b.wait("scalar", S["tE"], gend)
        b.waite("scalar", u_st[0])
        ev = b.op("scalar", lambda e, bk=bk, dst=dst, j=j:
                   e.activation(dst, bk.ap[:, :], AF.Identity, bias=vcol("in_b", j)))
        bk.free_evt = (S["sE"], ev)
        if tau == 1:
            b.wait("scalar", S["sE"], ev)
            u_st[0] = b.dma("scalar", "ust", lambda e, j=j, sl=sl:
                            e.dma_start(out=u_dram[j * 128:(j + 1) * 128, :],
                                        in_=UH[:, sl * T:(sl + 1) * T]))

    emit_proj(dt["in_w"],
              lambda k, tau: TBu[:, k * T + tau * TT: k * T + (tau + 1) * TT],
              evac_inproj, src_evt=t_done, mode="f32cast")
    if stop == "u":
        b.waite("sync", u_st[0])
        ld = b.dma("sync", "out", lambda e: e.dma_start(
            out=BB1[:, :].rearrange("p (c t) -> p c t", c=NCH),
            in_=u_dram.ap().rearrange("(c p) t -> p c t", p=128)))
        b.waite("vector", ld)
        dump_bf16_full(BB1, None)
        return finish()

    cast_pump(40, pace=False)

    b.scope('p4_scan')
    def pwp(par, i):
        off = 23 * par
        return SSMW[:, (off + i) * 128:(off + i + 1) * 128]

    ssmw_used = [0, 0]
    pw_store_ref = [None]
    xs_store = [None]

    b.op("vector", lambda e: e.memset(XS0[:, 0:PAD], 0.0))
    pad_ev = b.op("vector", lambda e: e.memset(XS1[:, 0:PAD], 0.0))
    b.wait("tensor", S["vE"], pad_ev)

    def ssm_head_loads(h, which):
        par = h % 2
        b.wait("scalar", S["tE"], ssmw_used[par])
        b.waite("scalar", cast_evts["ssm"])
        chn = "ssmw%d" % par
        off = 23 * par
        evs = []
        evs.append(b.dma("scalar", chn, lambda e, h=h, off=off: e.dma_start(
            out=SSMW[:, off * 128:(off + 5) * 128],
            in_=ssmb[:, h * 640:(h + 1) * 640])))
        if which == "corr":
            evs.append(b.dma("scalar", chn, lambda e, h=h, off=off: e.dma_start(
                out=SSMW[:, (off + 5) * 128:(off + 14) * 128],
                in_=pw_dram[h * 128:(h + 1) * 128, :])))
        return evs[-1]

    def ssm_powers(h, ld_ev):
        par = h % 2
        pw = lambda i: pwp(par, i)
        b.waite("tensor", ld_ev)
        if pw_store_ref[0] is not None:
            b.waite("vector", pw_store_ref[0][par])
        prev_ev = None
        for k in range(1, 10):
            bka = b.bank(); bkb = b.bank()
            b.waite("tensor", bka.free_evt); b.waite("tensor", bkb.free_evt)
            if prev_ev is not None:
                b.wait("tensor", S["vE"], prev_ev)
            qp = pw(0) if k == 1 else pw(13 + k - 1)
            pp = pw(1) if k == 1 else pw(4 + k - 1)
            b.run("tensor", lambda e, bka=bka, qp=qp, pp=pp: e.matmul(
                bka.ap[:, 0:128], qp, pp, start=True, stop=True))
            g = b.inc("tensor", S["tE"], lambda e, bkb=bkb, qp=qp, pp=pp: e.matmul(
                bkb.ap[:, 0:128], pp, qp, start=True, stop=True))
            b.wait("vector", S["tE"], g)
            b.op("vector", lambda e, bka=bka, k=k, pw=pw: e.tensor_copy(pw(4 + k), bka.ap[:, 0:128]))
            prev_ev = b.op("vector", lambda e, bkb=bkb, k=k, pw=pw:
                            e.tensor_copy(pw(13 + k), bkb.ap[:, 0:128]))
            bka.free_evt = (S["vE"], prev_ev); bkb.free_evt = (S["vE"], prev_ev)
        return prev_ev

    def load_uh(h):
        sl = h % 2
        b.wait("scalar", S["tE"], ssmw_used[sl])
        b.waite("scalar", u_st[0])
        ev = b.dma("scalar", "uh%d" % sl, lambda e, h=h, sl=sl: e.dma_start(
            out=UH[:, sl * T:(sl + 1) * T], in_=u_dram[h * 128:(h + 1) * 128, :]))
        return sl, ev

    pw_store = [None, None]
    pw_store_ref[0] = pw_store
    ldh = {0: ssm_head_loads(0, "scan")}
    uhd = {0: load_uh(0)}
    pwev = {0: ssm_powers(0, ldh[0])}
    b.wait("scalar", S["vE"], pwev[0])
    pw_store[0] = b.dma("scalar", "pwst0", lambda e: e.dma_start(
        out=pw_dram[0:128, :], in_=SSMW[:, 5 * 128:14 * 128]))
    for h in range(NH):
        par = h % 2
        pw = lambda i, par=par: pwp(par, i)
        usl, u_ev = uhd[h]
        ev_v = ev_s = None
        for tau in range(2):
            bk = b.bank(); b.waite("tensor", bk.free_evt)
            b.waite("tensor", u_ev)
            b.waite("tensor", ldh[h])
            g = b.inc("tensor", S["tE"], lambda e, bk=bk, usl=usl, tau=tau, pw=pw: e.matmul(
                bk.ap[:, :], pw(2), UH[:, usl * T + tau * TT: usl * T + (tau + 1) * TT],
                start=True, stop=True))
            eng = "vector" if tau == 0 else "scalar"
            sem = S["vE"] if tau == 0 else S["sE"]
            b.wait(eng, S["tE"], g)
            b.waite(eng, xs_store[0])
            if tau == 0:
                ev_v = b.op(eng, lambda e, bk=bk, tau=tau: e.tensor_copy(
                    XS0[:, PAD + tau * TT: PAD + (tau + 1) * TT], bk.ap[:, :]))
                bk.free_evt = (S["vE"], ev_v)
            else:
                ev_s = b.op(eng, lambda e, bk=bk, tau=tau: e.activation(
                    XS0[:, PAD + tau * TT: PAD + (tau + 1) * TT], bk.ap[:, :], AF.Identity))
                bk.free_evt = (S["sE"], ev_s)
        if h + 1 < NH:
            ldh[h + 1] = ssm_head_loads(h + 1, "scan")
            uhd[h + 1] = load_uh(h + 1)
            pwev[h + 1] = ssm_powers(h + 1, ldh[h + 1])
            b.wait("scalar", S["vE"], pwev[h + 1])
            pw_store[(h + 1) % 2] = b.dma("scalar", "pwst%d" % ((h + 1) % 2),
                lambda e, h1=h + 1: e.dma_start(
                    out=pw_dram[h1 * 128:(h1 + 1) * 128, :],
                    in_=SSMW[:, (23 * ((h1) % 2) + 5) * 128:(23 * ((h1) % 2) + 14) * 128]))
        cast_pump(2)
        xsrc, xdst = XS0, XS1
        for k in range(10):
            sh = 1 << k
            nev_v = nev_s = None
            for tau in range(2):
                bk = b.bank(); b.waite("tensor", bk.free_evt)
                b.wait("tensor", S["vE"], ev_v)
                b.wait("tensor", S["sE"], ev_s)
                b.run("tensor", lambda e, bk=bk, xsrc=xsrc, tau=tau: e.matmul(
                    bk.ap[:, :], I128,
                    xsrc[:, PAD + tau * TT: PAD + (tau + 1) * TT], start=True, stop=False))
                g = b.inc("tensor", S["tE"], lambda e, bk=bk, xsrc=xsrc, tau=tau, k=k, sh=sh, pw=pw:
                          e.matmul(bk.ap[:, :], pw(1) if k == 0 else pw(4 + k),
                                   xsrc[:, PAD + tau * TT - sh: PAD + (tau + 1) * TT - sh],
                                   start=False, stop=True))
                if tau == 0:
                    b.wait("vector", S["tE"], g)
                    nev_v = b.op("vector", lambda e, bk=bk, xdst=xdst, tau=tau:
                               e.tensor_copy(xdst[:, PAD + tau * TT: PAD + (tau + 1) * TT],
                                             bk.ap[:, :]))
                    bk.free_evt = (S["vE"], nev_v)
                else:
                    b.wait("scalar", S["tE"], g)
                    nev_s = b.op("scalar", lambda e, bk=bk, xdst=xdst, tau=tau:
                               e.activation(xdst[:, PAD + tau * TT: PAD + (tau + 1) * TT],
                                            bk.ap[:, :], AF.Identity))
                    bk.free_evt = (S["sE"], nev_s)
            ev_v, ev_s = nev_v, nev_s
            xsrc, xdst = xdst, xsrc
        ssmw_used[par] = S["tE"].n
        b.wait("vector", S["sE"], ev_s)
        car_ev = b.op("vector", lambda e, h=h: e.tensor_copy(
            CAR[:, h:h + 1], XS0[:, PAD + T - 1: PAD + T]))
        b.wait("scalar", S["vE"], car_ev)
        xs_store[0] = b.dma("scalar", "xst", lambda e, h=h: e.dma_start(
            out=s_dram[h * 128:(h + 1) * 128, :], in_=XS0[:, PAD:PAD + T]))
    if stop == "states":
        b.waite("sync", xs_store[0])
        ld = b.dma("sync", "out", lambda e: e.dma_start(
            out=BB1[:, :].rearrange("p (c t) -> p c t", c=NCH),
            in_=s_dram.ap().rearrange("(c p) t -> p c t", p=128)))
        b.waite("vector", ld)
        dump_bf16_full(BB1, None)
        return finish()

    b.scope('p4b_cc')
    b.wait("gpsimd", S["vE"], car_ev)
    b.wait("sync", S["vE"], car_ev)
    cin_ev = b.dma("sync", "ccio", lambda e: e.dma_start(
        out=cc1_in.ap(), in_=CAR[:, :]))
    b.waite("gpsimd", cin_ev)
    cc_ev = b.inc("gpsimd", S["cc"], lambda e: e.collective_compute(
        "AllGather", ALU.bypass, replica_groups=[list(range(8))],
        ins=[cc1_in.ap().opt()], outs=[cc1_out.ap().opt()]))
    b.wait("gpsimd", S["cc"], cc_ev)
    b.wait("sync", S["cc"], cc_ev)
    gg_ev = b.dma("sync", "ccio", lambda e: e.dma_start(
        out=GG[:, :].rearrange("p (r h) -> p r h", r=8),
        in_=cc1_out.ap().rearrange("(r i) h -> i r h", i=128)))
    b.waite("vector", gg_ev)
    b.waite("vector", cons_grp)
    b.op("vector", lambda e: e.tensor_scalar(
        HIN[:, :], GG[:, 0:NH], MASK[:, 0:1], None, ALU.mult))
    for r in range(1, 8):
        b.op("vector", lambda e, r=r: e.scalar_tensor_tensor(
            HIN[:, :], GG[:, r * NH:(r + 1) * NH], MASK[:, r:r + 1], HIN[:, :],
            ALU.mult, ALU.add))
    hinb_ev = b.op("vector", lambda e: e.tensor_copy(CARB[:, :], HIN[:, :]))

    b.scope('p4c_corr')
    sst_used = [0]
    VB2 = [VBt[:, 0:T], XQs[:, 0:2 * TT]]
    _bar4c = (S["tE"], S["tE"].n)
    vb_used = [_bar4c, _bar4c]

    def emit_vchain(h):
        par = h % 2
        pw = lambda i: pwp(par, i)
        vb = VB2[par]
        bk = b.bank(); b.waite("tensor", bk.free_evt)
        b.waite("tensor", ldh[h])
        b.wait("tensor", S["vE"], hinb_ev)
        g = b.inc("tensor", S["tE"], lambda e, bk=bk, h=h, pw=pw: e.matmul(
            bk.ap[:, 0:1], pw(1), CARB[:, h:h + 1], start=True, stop=True))
        b.wait("vector", S["tE"], g)
        b.waite("vector", vb_used[par])
        vev = b.op("vector", lambda e, bk=bk, vb=vb: e.tensor_copy(
            vb[:, 0:1], bk.ap[:, 0:1]))
        bk.free_evt = (S["vE"], vev)
        for k in range(10):
            w = 1 << k
            bk = b.bank(); b.waite("tensor", bk.free_evt)
            b.wait("tensor", S["vE"], vev)
            g = b.inc("tensor", S["tE"], lambda e, bk=bk, k=k, w=w, pw=pw, vb=vb: e.matmul(
                bk.ap[:, 0:w], pw(1) if k == 0 else pw(4 + k), vb[:, 0:w],
                start=True, stop=True))
            b.wait("vector", S["tE"], g)
            vev = b.op("vector", lambda e, bk=bk, w=w, vb=vb: e.tensor_copy(
                vb[:, w:2 * w], bk.ap[:, 0:w]))
            bk.free_evt = (S["vE"], vev)
        return vev

    b.waite("scalar", pw_store[0]); b.waite("scalar", pw_store[1])
    ldh = {0: ssm_head_loads(0, "corr")}
    uhd = {0: load_uh(0)}
    vch = {0: emit_vchain(0)}
    std = {}
    for h in range(NH):
        par = h % 2
        pw = lambda i, par=par: pwp(par, i)
        if h == 0:
            sl = 0
            b.wait("scalar", S["tE"], sst_used[0])
            b.waite("scalar", xs_store[0])
            std[0] = b.dma("scalar", "sst0", lambda e: e.dma_start(
                out=SST[:, 0:T], in_=s_dram[0:128, :]))
        st_ld = std[h]
        usl, u_ev = uhd[h]
        sl = h % 2
        if h + 1 < NH:
            ldh[h + 1] = ssm_head_loads(h + 1, "corr")
            uhd[h + 1] = load_uh(h + 1)
            sl2 = (h + 1) % 2
            b.wait("scalar", S["tE"], sst_used[0])
            std[h + 1] = b.dma("scalar", "sst%d" % sl2, lambda e, h=h, sl2=sl2: e.dma_start(
                out=SST[:, sl2 * T:(sl2 + 1) * T],
                in_=s_dram[(h + 1) * 128:(h + 2) * 128, :]))
            vch[h + 1] = emit_vchain(h + 1)
        cast_pump(2)
        b.waite("vector", st_ld)
        b.wait("vector", S["vE"], vch[h])
        add_ev = b.op("vector", lambda e, sl=sl: e.tensor_tensor(
            SST[:, sl * T:(sl + 1) * T], SST[:, sl * T:(sl + 1) * T],
            VB2[sl][:, 0:T], ALU.add))
        vb_used[sl] = (S["vE"], add_ev)
        for tau in range(2):
            bk = b.bank(); b.waite("tensor", bk.free_evt)
            b.wait("tensor", S["vE"], add_ev)
            b.waite("tensor", u_ev)
            b.run("tensor", lambda e, bk=bk, sl=sl, tau=tau, pw=pw: e.matmul(
                bk.ap[:, :], pw(3), SST[:, sl * T + tau * TT: sl * T + (tau + 1) * TT],
                start=True, stop=False))
            g = b.inc("tensor", S["tE"], lambda e, bk=bk, usl=usl, tau=tau, pw=pw: e.matmul(
                bk.ap[:, :], pw(4), UH[:, usl * T + tau * TT: usl * T + (tau + 1) * TT],
                start=False, stop=True))
            if tau == 0:
                b.wait("vector", S["tE"], g)
                yev = b.op("vector", lambda e, bk=bk, h=h, tau=tau: e.tensor_copy(
                    St[:, h * T + tau * TT: h * T + (tau + 1) * TT], bk.ap[:, :]))
                bk.free_evt = (S["vE"], yev)
            else:
                b.wait("scalar", S["tE"], g)
                yev = b.op("scalar", lambda e, bk=bk, h=h, tau=tau: e.activation(
                    St[:, h * T + tau * TT: h * T + (tau + 1) * TT], bk.ap[:, :], AF.Identity))
                bk.free_evt = (S["sE"], yev)
        ssmw_used[par] = S["tE"].n
        sst_used[0] = S["tE"].n
    b.ln_slot = [S["tE"].n, S["tE"].n]
    y_done = (S["vE"], S["vE"].n)
    y_done_s = (S["sE"], S["sE"].n)
    if stop == "yss":
        dump_f32_full(St, y_done)
        return finish()

    b.scope('p5_outproj')
    z_done = None
    for tau in range(2):
        cast_ev = None
        for c in range(NCH):
            b.waite("vector", y_done)
            b.wait("vector", S["sE"], y_done_s[1])
            cast_ev = b.op("vector", lambda e, c=c, tau=tau: e.tensor_copy(
                NB[:, c * TT:(c + 1) * TT], St[:, c * T + tau * TT: c * T + (tau + 1) * TT]))
        def evac_z(j, t2, bk, gend, tau=tau):
            b.wait("vector", S["tE"], gend)
            ev = b.op("vector", lambda e, bk=bk, j=j, tau=tau:
                       e.scalar_tensor_tensor(
                           St[:, j * T + tau * TT: j * T + (tau + 1) * TT],
                           bk.ap[:, :], vcol("out_b", j),
                           BB1[:, j * T + tau * TT: j * T + (tau + 1) * TT],
                           ALU.add, ALU.add))
            bk.free_evt = (S["vE"], ev)
            return ev
        evz = []
        def evac_z_wrap(j, t2, bk, gend):
            evz.append(evac_z(j, t2, bk, gend))
        emit_proj(dt["out_w"], lambda k, t2, tau=tau: NB[:, k * TT:(k + 1) * TT],
                  evac_z_wrap, src_evt=(S["vE"], cast_ev), taus=(tau,),
                  shared_tau=False, mode="f32cast")
        z_done = (S["vE"], evz[-1])
    if stop == "z":
        dump_f32_full(St, z_done)
        return finish()

    b.scope('p68_ffnA')
    def hidtile(c):
        buf = BB1 if c < 32 else TBu
        return buf[:, (c % 32) * TT:(c % 32 + 1) * TT]

    xt_used = [None, None]

    def emit_x0_add(tau):
        ev = None
        for c in range(NCH):
            sl = c % 2
            xt = XT[:, sl * TT:(sl + 1) * TT]
            b.waite("sync", xt_used[sl])
            ld = b.dma("sync", "xt%d" % sl, lambda e, c=c, tau=tau, xt=xt:
                       e.dma_start(out=xt, in_=x0h[c * 128:(c + 1) * 128,
                                                   tau * TT:(tau + 1) * TT]))
            b.waite("vector", ld)
            ev = b.op("vector", lambda e, c=c, tau=tau, xt=xt: e.tensor_tensor(
                St[:, c * T + tau * TT: c * T + (tau + 1) * TT],
                St[:, c * T + tau * TT: c * T + (tau + 1) * TT], xt, ALU.add))
            xt_used[sl] = (S["vE"], ev)
        return ev

    ln2_evs = {}
    ln2_evs[0] = emit_ln(
        src_ch=lambda c: St[:, c * T: c * T + TT],
        dst_ch=lambda c: NB[:, c * TT:(c + 1) * TT],
        w_key=("ln2w", "ln2b"), src_f32=True)
    emit_x0_add(0)
    x1_ev = [None]
    for tau in range(2):
        gelu_ev = {}
        def evac_gelu(j, t2, bk, gend):
            b.wait("scalar", S["tE"], gend)
            ev = b.op("scalar", lambda e, bk=bk, j=j: e.activation(
                hidtile(j), bk.ap[:, :], AF.Gelu, bias=b1col(j)))
            bk.free_evt = (S["sE"], ev)
            gelu_ev[j] = ev
        w1a_gend = emit_proj(w1abf, lambda k, t2: NB[:, k * TT:(k + 1) * TT],
                  evac_gelu, n_k=NCH, n_j=HCH, src_evt=ln2_evs[tau],
                  taus=(tau,), shared_tau=False, mode="bf16", w_ready="w1a")
        if tau == 0:
            ln2_evs[1] = emit_ln(
                src_ch=lambda c: St[:, c * T + TT: c * T + 2 * TT],
                dst_ch=lambda c: NB[:, c * TT:(c + 1) * TT],
                w_key=("ln2w", "ln2b"), src_f32=True,
                dst_gate=(S["tE"], w1a_gend))
            emit_x0_add(1)
        def evac_w2a(j, t2, bk, gend, tau=tau):
            b.wait("vector", S["tE"], gend)
            ev = b.op("vector", lambda e, bk=bk, j=j, tau=tau:
                      e.scalar_tensor_tensor(
                          St[:, j * T + tau * TT: j * T + (tau + 1) * TT],
                          bk.ap[:, :], vcol("b2", j),
                          St[:, j * T + tau * TT: j * T + (tau + 1) * TT],
                          ALU.add, ALU.add))
            bk.free_evt = (S["vE"], ev)
            x1_ev[0] = ev
        emit_proj(w2abf, lambda k, t2: hidtile(k),
                  evac_w2a, n_k=HCH, n_j=NCH, src_evt=(S["sE"], gelu_ev[HCH - 1]),
                  taus=(tau,), shared_tau=False, mode="bf16", w_ready="w2a")
    x1_done = (S["vE"], x1_ev[0])
    if stop == "x1":
        dump_f32_full(St, x1_done)
        return finish()

    b.scope('p_w2prep')
    prep_in_used = [None, None, None, None]
    prep_out_used = [None, None, None, None]
    PREP_OUT = [WA[:, 0:2048], WA[:, 2048:4096], UH[:, 0:2048], SST[:, 0:2048]]

    def prep_pass():
        st_ev = None
        for i in range(HCH):
            sl = i % 4
            inslot = NB[:, sl * 2048:(sl + 1) * 2048]
            chn = "prep_in%d" % sl
            b.waite("sync", prep_in_used[sl])
            ld = b.dma("sync", chn, lambda e, i=i, inslot=inslot: e.dma_start(
                out=inslot, in_=w2bf[i * 128:(i + 1) * 128, :]))
            b.waite("tensor", ld)
            out_wa = PREP_OUT[sl]
            ev_ev = None
            for dtile in range(4):
                bk = b.bank(); b.waite("tensor", bk.free_evt)
                g = b.inc("tensor", S["tE"], lambda e, bk=bk, inslot=inslot,
                          dtile=dtile: e.matmul(
                    bk.ap[:, :], H128, inslot[:, dtile * TT:(dtile + 1) * TT],
                    start=True, stop=True))
                eng = "vector" if dtile % 2 == 0 else "scalar"
                b.wait(eng, S["tE"], g)
                if dtile == 0:
                    b.waite("vector", prep_out_used[sl])
                if dtile == 1:
                    b.waite("scalar", prep_out_used[sl])
                if eng == "vector":
                    ev_ev = b.op(eng, lambda e, bk=bk, dtile=dtile, out_wa=out_wa:
                                 e.tensor_copy(out_wa[:, dtile * TT:(dtile + 1) * TT],
                                               bk.ap[:, :]))
                    bk.free_evt = (S["vE"], ev_ev)
                else:
                    ev_s = b.op(eng, lambda e, bk=bk, dtile=dtile, out_wa=out_wa:
                                e.activation(out_wa[:, dtile * TT:(dtile + 1) * TT],
                                             bk.ap[:, :], AF.Identity))
                    bk.free_evt = (S["sE"], ev_s)
            prep_in_used[sl] = (S["tE"], S["tE"].n)
            b.wait("scalar", S["vE"], ev_ev)
            b.wait("scalar", S["sE"], ev_s)
            st = b.dma("scalar", "prep_st%d" % sl, lambda e, i=i, out_wa=out_wa:
                       e.dma_start(
                out=bass.AP(w2t3, i * D, [[128 * D, 64], [64 * D, 2], [1, D]]),
                in_=out_wa))
            prep_out_used[sl] = st
            st_ev = st
        return st_ev

    cast_flush(["w2"])
    for pe_ in cast_piece_evts["w2"]:
        b.waite("sync", pe_)
    prep_bar = (S["tE"], S["tE"].n)
    for _q in range(4):
        prep_in_used[_q] = prep_bar
    b.wait("sync", S["tE"], S["tE"].n)
    b.wait("vector", S["tE"], S["tE"].n)
    b.wait("scalar", S["tE"], S["tE"].n)
    prep_pass()
    for _q in range(4):
        b.waite("sync", prep_out_used[_q])
    p2_end = prep_out_used[3]
    b.wait("vector", S["tE"], S["tE"].n)
    b.wait("scalar", S["tE"], S["tE"].n)
    proj_state["wa_evt"] = [prep_out_used[0], prep_out_used[1]]

    def emit_ffn(x_stop=None):
        out_ev = [None]
        ln_evs = {}
        ln_evs[0] = emit_ln(
            src_ch=lambda c: St[:, c * T: c * T + TT],
            dst_ch=lambda c: NB[:, c * TT:(c + 1) * TT],
            w_key=None, src_f32=True)
        for tau in range(2):
            silu_ev = {}
            h64_ev = {}
            def evac_silu(j, t2, bk, gend):
                b.wait("scalar", S["tE"], gend)
                ev = b.op("scalar", lambda e, bk=bk, j=j: e.activation(
                    hidtile(j), bk.ap[:, :], AF.Silu))
                bk.free_evt = (S["sE"], ev)
                silu_ev[j] = ev
            emit_proj(w1T, lambda k, t2: NB[:, k * TT:(k + 1) * TT],
                      evac_silu, n_k=NCH, n_j=HCH, src_evt=ln_evs[tau],
                      taus=(tau,), shared_tau=False, mode="bf16",
                      w_ready="w1")
            def evac_lin(j, t2, bk, gend):
                b.wait("vector", S["tE"], gend)
                b.wait("vector", S["sE"], silu_ev[j])
                ev = b.op("vector", lambda e, bk=bk, j=j: e.tensor_tensor(
                    hidtile(j), hidtile(j), bk.ap[:, :], ALU.mult))
                bk.free_evt = (S["vE"], ev)
                silu_ev[j] = ev
                bh = b.bank(); b.waite("tensor", bh.free_evt)
                b.wait("tensor", S["vE"], ev)
                g2 = b.inc("tensor", S["tE"], lambda e, bh=bh, j=j: e.matmul(
                    bh.ap[:, :], H64BD, hidtile(j), start=True, stop=True))
                b.wait("scalar", S["tE"], g2)
                ev2 = b.op("scalar", lambda e, bh=bh, j=j: e.activation(
                    hidtile(j), bh.ap[:, :], AF.Identity))
                bh.free_evt = (S["sE"], ev2)
                h64_ev[j] = ev2
            w3_gend = emit_proj(w3T, lambda k, t2: NB[:, k * TT:(k + 1) * TT],
                      evac_lin, n_k=NCH, n_j=HCH, src_evt=ln_evs[tau],
                      taus=(tau,), shared_tau=False, mode="bf16",
                      w_ready="w3")
            if tau == 0:
                ln_evs[1] = emit_ln(
                    src_ch=lambda c: St[:, c * T + TT: c * T + 2 * TT],
                    dst_ch=lambda c: NB[:, c * TT:(c + 1) * TT],
                    w_key=None, src_f32=True,
                    dst_gate=(S["tE"], w3_gend))
            def evac_out(j, t2, bk, gend, tau=tau):
                b.wait("vector", S["tE"], gend)
                ev = b.op("vector", lambda e, bk=bk, j=j, tau=tau: e.tensor_tensor(
                    St[:, j * T + tau * TT: j * T + (tau + 1) * TT],
                    St[:, j * T + tau * TT: j * T + (tau + 1) * TT],
                    bk.ap[:, :], ALU.add))
                bk.free_evt = (S["vE"], ev)
                out_ev[0] = ev
            emit_proj(w2t3, lambda k, t2: hidtile(k),
                      evac_out, n_k=HCH, n_j=NCH,
                      src_evt=(S["sE"], h64_ev[HCH - 1]),
                      taus=(tau,), shared_tau=False, mode="bf16",
                      w_ready=p2_end)
        return (S["vE"], out_ev[0])

    b.scope('p_ffn1')
    x2_done = emit_ffn()
    if stop == "x2":
        dump_f32_full(St, x2_done)
        return finish()

    b.scope('p_attnb_proj')
    ones_ev = b.op("vector", lambda e: e.memset(VBt[:, :], 1.0))

    qr_ev = [None]
    for tau in range(2):
        ln_ev = emit_ln(
            src_ch=lambda c, tau=tau: St[:, c * T + tau * TT: c * T + (tau + 1) * TT],
            dst_ch=lambda c: NB[:, c * TT:(c + 1) * TT],
            w_key=None, src_f32=True)
        for (whbm, dstbuf) in ((dt["wq"], TBu), (dt["wkv"], BB1)):
            def evac_raw(j, t2, bk, gend, dstbuf=dstbuf, tau=tau):
                b.wait("vector", S["tE"], gend)
                ev = b.op("vector", lambda e, bk=bk, j=j, tau=tau, dstbuf=dstbuf:
                          e.tensor_copy(dstbuf[:, j * T + tau * TT: j * T + (tau + 1) * TT],
                                        bk.ap[:, :]))
                bk.free_evt = (S["vE"], ev)
                qr_ev[0] = ev
            emit_proj(whbm, lambda k, t2: NB[:, k * TT:(k + 1) * TT],
                      evac_raw, n_k=NCH, n_j=NCH, src_evt=ln_ev,
                      taus=(tau,), shared_tau=False, mode="f32cast")

    b.scope('p_attnb_norm')
    nrm_state = {"i": 0, "sq": [0, 0], "srow": [0, 0, 0, 0],
                 "rb": [0, 0], "xb": [0, 0]}

    def emit_normalize(buf, h, tau):
        ns = nrm_state
        i = ns["i"]; ns["i"] += 1
        tile = buf[:, h * T + tau * TT: h * T + (tau + 1) * TT]
        sq = XQs[:, (i % 2) * TT:(i % 2 + 1) * TT]
        b.wait("gpsimd", S["tE"], ns["sq"][i % 2])
        if ns.get("src_ev") is not None:
            b.wait("gpsimd", S["vE"], ns["src_ev"])
        sq_ev = b.op("gpsimd", lambda e, sq=sq, tile=tile:
                     e.tensor_tensor(sq, tile, tile, ALU.mult))
        bk = b.bank(); b.waite("tensor", bk.free_evt)
        b.wait("tensor", S["gE"], sq_ev)
        g = b.inc("tensor", S["tE"], lambda e, bk=bk, sq=sq: e.matmul(
            bk.ap[0:1, :], ONESCOL_B, sq, start=True, stop=True))
        ns["sq"][i % 2] = g
        s_row = ROWS[0:1, (i % 4) * TT:(i % 4 + 1) * TT]
        b.wait("scalar", S["tE"], g)
        b.wait("scalar", S["vE"], ns["srow"][i % 4])
        sc_ev = b.op("scalar", lambda e, bk=bk, s_row=s_row: e.activation(
            s_row, bk.ap[0:1, :], AF.Sqrt, scale=128.0))
        bk.free_evt = (S["sE"], sc_ev)
        b.wait("vector", S["sE"], sc_ev)
        b.op("vector", lambda e, s_row=s_row: e.reciprocal(s_row, s_row))
        rb_row = RB[0:1, (i % 2) * TT:(i % 2 + 1) * TT]
        b.wait("vector", S["tE"], ns["rb"][i % 2])
        rb_ev = b.op("vector", lambda e, rb_row=rb_row, s_row=s_row:
                     e.tensor_copy(rb_row, s_row))
        ns["srow"][i % 4] = rb_ev
        bc = b.bank(); b.waite("tensor", bc.free_evt)
        b.wait("tensor", S["vE"], rb_ev)
        g2 = b.inc("tensor", S["tE"], lambda e, bc=bc, rb_row=rb_row: e.matmul(
            bc.ap[:, :], ONESROW_B, rb_row, start=True, stop=True))
        ns["rb"][i % 2] = g2
        b.wait("scalar", S["tE"], g2)
        xb = XBs[:, (i % 2) * TT:(i % 2 + 1) * TT]
        b.waite("scalar", ns["xb_used"][i % 2])
        cp_ev = b.op("scalar", lambda e, bc=bc, xb=xb: e.activation(
            xb, bc.ap[:, :], AF.Identity))
        bc.free_evt = (S["sE"], cp_ev)
        b.wait("vector", S["sE"], cp_ev)
        nrm_ev = b.op("vector", lambda e, tile=tile, xb=xb:
                      e.tensor_tensor(tile, tile, xb, ALU.mult))
        ns["xb_used"][i % 2] = (S["vE"], nrm_ev)
        return nrm_ev

    nrm_state["sq"] = [b.ln_slot[0], b.ln_slot[1]]
    nrm_state["rb"] = [b.rb_consumed, b.rb_consumed]
    nrm_state["srow"] = [S["vE"].n] * 4
    nrm_state["xb_used"] = [(S["vE"], S["vE"].n), (S["vE"], S["vE"].n)]
    nrm_state["src_ev"] = qr_ev[0]
    kv_norm_ev = None
    for h in range(NH):
        for tau in range(2):
            emit_normalize(TBu, h, tau)
            kv_norm_ev = emit_normalize(BB1, h, tau)
    b.ln_slot = [max(nrm_state["sq"]), max(nrm_state["sq"])]
    b.rb_consumed = max(nrm_state["rb"])

    b.waite("vector", cin_ev)
    scan_evs = {}
    for h in range(NH):
        sc_ev = b.op("vector", lambda e, h=h: e.tensor_tensor_scan(
            BB1[:, h * T:(h + 1) * T], VBt[:, :], BB1[:, h * T:(h + 1) * T],
            0.0, ALU.mult, ALU.add))
        car_ev = b.op("vector", lambda e, h=h: e.tensor_copy(
            CAR[:, h:h + 1], BB1[:, h * T + T - 1: h * T + T]))
        scan_evs[h] = car_ev

    b.wait("gpsimd", S["vE"], scan_evs[NH - 1])
    b.wait("gpsimd", S["vE"], hinb_ev)
    b.wait("sync", S["vE"], scan_evs[NH - 1])
    cin2 = b.dma("sync", "ccio", lambda e: e.dma_start(
        out=cc2_in.ap(), in_=CAR[:, :]))
    b.waite("gpsimd", cin2)
    cc2_ev = b.inc("gpsimd", S["cc"], lambda e: e.collective_compute(
        "AllGather", ALU.bypass, replica_groups=[list(range(8))],
        ins=[cc2_in.ap().opt()], outs=[cc2_out.ap().opt()]))
    b.wait("gpsimd", S["cc"], cc2_ev)
    b.wait("sync", S["cc"], cc2_ev)
    b.wait("sync", S["vE"], hinb_ev)
    gg2 = b.dma("sync", "ccio", lambda e: e.dma_start(
        out=GG[:, :].rearrange("p (r h) -> p r h", r=8),
        in_=cc2_out.ap().rearrange("(r i) h -> i r h", i=128)))
    b.waite("vector", gg2)
    b.op("vector", lambda e: e.tensor_scalar(
        HIN[:, :], GG[:, 0:NH], MASK[:, 0:1], None, ALU.mult))
    for r in range(1, 8):
        b.op("vector", lambda e, r=r: e.scalar_tensor_tensor(
            HIN[:, :], GG[:, r * NH:(r + 1) * NH], MASK[:, r:r + 1], HIN[:, :],
            ALU.mult, ALU.add))
    addc_ev = {}
    for h in range(NH):
        addc_ev[h] = b.op("vector", lambda e, h=h: e.tensor_scalar(
            BB1[:, h * T:(h + 1) * T], BB1[:, h * T:(h + 1) * T],
            HIN[:, h:h + 1], None, ALU.add))

    b.scope('p_attnb_fft')
    val_ev = None
    for h in range(NH):
        for tau in range(2):
            qt = TBu[:, h * T + tau * TT: h * T + (tau + 1) * TT]
            kt = BB1[:, h * T + tau * TT: h * T + (tau + 1) * TT]
            fs = [XS0[:, 0:TT], XS0[:, TT:2 * TT], XS0[:, 2 * TT:3 * TT],
                  XS1[:, 0:TT], XS1[:, TT:2 * TT]]
            banks = []
            b.wait("tensor", S["vE"], addc_ev[h])
            for (lhs, src) in ((DFTC, qt), (DFTS, qt), (DFTC, kt), (DFTS, kt)):
                bk = b.bank(); b.waite("tensor", bk.free_evt)
                g = b.inc("tensor", S["tE"], lambda e, bk=bk, lhs=lhs, src=src:
                          e.matmul(bk.ap[:, :], lhs, src, start=True, stop=True))
                banks.append((bk, g))
            for i, (bk, g) in enumerate(banks):
                b.wait("vector", S["tE"], g)
                ev = b.op("vector", lambda e, bk=bk, i=i: e.tensor_copy(
                    fs[i], bk.ap[:, :]))
                bk.free_evt = (S["vE"], ev)
            b.op("vector", lambda e: e.tensor_tensor(fs[4], fs[1], fs[2], ALU.mult))
            b.op("vector", lambda e: e.tensor_tensor(fs[2], fs[0], fs[2], ALU.mult))
            b.op("vector", lambda e: e.tensor_tensor(fs[0], fs[0], fs[3], ALU.mult))
            b.op("vector", lambda e: e.tensor_tensor(fs[3], fs[1], fs[3], ALU.mult))
            b.op("vector", lambda e: e.tensor_tensor(fs[2], fs[2], fs[3], ALU.subtract))
            pr_ev = b.op("vector", lambda e: e.tensor_tensor(fs[0], fs[0], fs[4], ALU.add))
            bk = b.bank(); b.waite("tensor", bk.free_evt)
            b.wait("tensor", S["vE"], pr_ev)
            b.run("tensor", lambda e, bk=bk: e.matmul(
                bk.ap[:, :], IDFTC, fs[2], start=True, stop=False))
            g = b.inc("tensor", S["tE"], lambda e, bk=bk: e.matmul(
                bk.ap[:, :], IDFTS, fs[0], start=False, stop=True))
            b.wait("vector", S["tE"], g)
            val_ev = b.op("vector", lambda e, bk=bk, qt=qt: e.tensor_copy(
                qt, bk.ap[:, :]))
            bk.free_evt = (S["vE"], val_ev)

    b.scope('p_attnb_wo')
    x3_ev = [None]
    for tau in range(2):
        def evac_wo(j, t2, bk, gend, tau=tau):
            b.wait("vector", S["tE"], gend)
            ev = b.op("vector", lambda e, bk=bk, j=j, tau=tau: e.tensor_tensor(
                St[:, j * T + tau * TT: j * T + (tau + 1) * TT],
                St[:, j * T + tau * TT: j * T + (tau + 1) * TT],
                bk.ap[:, :], ALU.add))
            bk.free_evt = (S["vE"], ev)
            x3_ev[0] = ev
        emit_proj(dt["wo"], lambda k, t2, tau=tau: TBu[:, k * T + tau * TT: k * T + (tau + 1) * TT],
                  evac_wo, n_k=NCH, n_j=NCH, src_evt=(S["vE"], val_ev),
                  taus=(tau,), shared_tau=False, mode="f32cast")
    x3_done = (S["vE"], x3_ev[0])
    if stop == "x3":
        dump_f32_full(St, x3_done)
        return finish()

    b.scope('p_ffn2')
    x4_done = emit_ffn()
    if stop == "x4":
        dump_f32_full(St, x4_done)
        return finish()

    b.scope('p_out')
    b.waite("sync", x4_done)
    out_ev = b.dma("sync", "out", lambda e: e.dma_start(
        out=y_out.ap().rearrange("(c p) t -> p c t", p=128),
        in_=St[:, :].rearrange("p (c t) -> p c t", c=NCH)))
    b.waite("sync", out_ev)
    return finish()


def _consts_b():
    blocks = []
    blocks.append(np.ones((128, 128), np.float32))
    blocks.append(np.eye(128, dtype=np.float32))
    def hadamard(n):
        h = np.array([[1.0]], np.float32)
        while h.shape[0] < n:
            h = np.block([[h, h], [h, -h]])
        return h
    blocks.append(hadamard(128))
    h64 = hadamard(64) / np.sqrt(8192.0)
    bd = np.zeros((128, 128), np.float32)
    bd[:64, :64] = h64; bd[64:, 64:] = h64
    blocks.append(bd)
    j = np.arange(128)
    ang = 2.0 * np.pi * np.outer(j, j) / 128.0
    C = np.cos(ang).astype(np.float32); Sm = np.sin(ang).astype(np.float32)
    blocks.append(C); blocks.append(Sm)
    blocks.append(C / 128.0); blocks.append(Sm / 128.0)
    blocks.append(np.zeros((128, 128), np.float32))
    blocks.append(np.zeros((128, 128), np.float32))
    return np.concatenate(blocks, 1).astype(ml_dtypes.bfloat16)


def _hid_perm():
    j = np.arange(HID)
    return (j % 64) * 128 + 2 * (j // 128) + (j % 128) // 64


def make_inputs(full):
    g = lambda k: np.asarray(full[k])
    x = g("L0_attn_a_x")
    vkeys = [g("L0_attn_a_ln1_w"), g("L0_attn_a_ln1_b"),
             g("L0_attn_a_ln2_w"), g("L0_attn_a_ln2_b"),
             g("L0_attn_a_in_b"), g("L0_attn_a_out_b"), g("L0_attn_a_b2")]
    vcols = np.concatenate(
        [np.asarray(v, np.float32).reshape(16, 128).T for v in vkeys], 1)
    b1cols = np.ascontiguousarray(
        g("L0_attn_a_b1").astype(np.float32).reshape(64, 128).T)
    A_ = g("L0_attn_a_A"); Bm_ = g("L0_attn_a_Bm")
    Cm_ = g("L0_attn_a_Cm"); Dm_ = g("L0_attn_a_Dm")
    blocks = []
    for h in range(NH):
        blocks += [A_[h], A_[h].T, Bm_[h].T, Cm_[h].T, Dm_[h].T]
    ssm = np.ascontiguousarray(np.concatenate(blocks, 1), np.float32)
    shared = dict(
        in_w=np.ascontiguousarray(g("L0_attn_a_in_w"), np.float32),
        out_w=np.ascontiguousarray(g("L0_attn_a_out_w"), np.float32),
        w1a=np.ascontiguousarray(g("L0_attn_a_w1"), np.float32),
        w2a=np.ascontiguousarray(g("L0_attn_a_w2"), np.float32),
        ssm_mats=ssm,
        wq=np.ascontiguousarray(g("L0_attn_b_W_q").T, np.float32),
        wkv=np.ascontiguousarray(g("L0_attn_b_W_kv").T, np.float32),
        wo=np.ascontiguousarray(g("L0_attn_b_W_o").T, np.float32),
        fw1=np.ascontiguousarray(g("L0_ffn_W1").T[:, _hid_perm()], np.float32),
        fw2=np.ascontiguousarray(g("L0_ffn_W2").T, np.float32),
        fw3=np.ascontiguousarray(g("L0_ffn_W3").T[:, _hid_perm()], np.float32),
        vcolsin=np.ascontiguousarray(vcols, np.float32),
        b1colsin=b1cols, consts_b=_consts_b(),
        onescol_f=np.ones((128, 1), np.float32),
    )
    in_maps = []
    for core in range(8):
        bb, half = core // 2, core % 2
        xs = np.ascontiguousarray(x[bb, half * T:(half + 1) * T, :].T.astype(np.float32))
        maskc = np.zeros((128, 8), np.float32)
        if half == 1:
            maskc[:, core - 1] = 1.0
        m = dict(shared)
        m["x0"] = xs
        m["maskc"] = maskc
        in_maps.append(m)
    return in_maps


def kernel(**inputs):
    in_maps = make_inputs(inputs)
    nc = build_program(stop=None)
    res = run_bass_kernel_spmd(nc, in_maps, core_ids=list(range(8)))
    out = np.empty((4, 2048, 2048), np.float32)
    for core in range(8):
        bb, half = core // 2, core % 2
        out[bb, half * T:(half + 1) * T, :] = res.results[core]["y"].T
    return out


# version 17 (final)
# speedup vs baseline: 1.0201x; 1.0201x over previous
import numpy as np
import ml_dtypes
from contextlib import ExitStack

import concourse.bass as bass
import concourse.mybir as mybir
from concourse.bass_utils import run_bass_kernel_spmd

F32 = mybir.dt.float32
BF16 = mybir.dt.bfloat16
AF = mybir.ActivationFunctionType
ALU = mybir.AluOpType

D = 2048
T = 1024
TT = 512
NCH = 16
NH = 16
HD = 128
HID = 8192
HCH = 64
EPS = 1e-5
PAD = 512

ENGS = ("tensor", "vector", "scalar", "sync", "gpsimd")


class Sem:
    def __init__(self, stack, nc, name):
        self.h = stack.enter_context(nc.semaphore(name))
        self.n = 0
        self.name = name


class Bank:
    def __init__(self, ap, idx):
        self.ap = ap
        self.idx = idx
        self.free_evt = None


class Builder:
    def __init__(self, nc, stack):
        self.nc = nc
        self.stack = stack
        self.plan = {e: [] for e in ENGS}
        self.wm = {e: {} for e in ENGS}
        self.sem = {k: Sem(stack, nc, k) for k in
                    ("tE", "vE", "sE", "gE", "cc")}
        self.chans = {}
        self.cur_scope = "p0"
        self.ln_slot = [0, 0]
        self.tmp_slot = [(None, 0), (None, 0)]
        self.rb_consumed = 0

    def scope(self, name):
        self.cur_scope = name

    def em(self, eng, fn):
        self.plan[eng].append((self.cur_scope, fn))

    def wait(self, eng, sem, val):
        if val is None or val <= 0:
            return
        if self.wm[eng].get(sem.name, 0) >= val:
            return
        self.wm[eng][sem.name] = val
        self.plan[eng].append((self.cur_scope, lambda e, s=sem.h, v=val: e.wait_ge(s, v)))

    def waite(self, eng, evt):
        if evt is not None and evt[0] is not None:
            self.wait(eng, evt[0], evt[1])

    def chan(self, name):
        if name not in self.chans:
            self.chans[name] = Sem(self.stack, self.nc, "ch_" + name)
        return self.chans[name]

    def dma(self, eng, chname, fn):
        ch = self.chan(chname)
        ch.n += 16
        v = ch.n
        self.plan[eng].append((self.cur_scope, lambda e, f=fn, s=ch.h: f(e).then_inc(s, 16)))
        return (ch, v)

    ENGSEM = {"vector": "vE", "scalar": "sE", "gpsimd": "gE"}

    def op(self, eng, fn):
        sem = self.sem[self.ENGSEM[eng]]
        if sem.n > 0:
            self.wait(eng, sem, sem.n)
        sem.n += 1
        v = sem.n
        self.plan[eng].append((self.cur_scope, lambda e, f=fn, s=sem.h: f(e).then_inc(s, 1)))
        return v

    def inc(self, eng, sem, fn, k=1):
        sem.n += k
        v = sem.n
        self.plan[eng].append((self.cur_scope, lambda e, f=fn, s=sem.h, kk=k: f(e).then_inc(s, kk)))
        return v

    def run(self, eng, fn):
        self.plan[eng].append((self.cur_scope, fn))

    def flush(self, block):
        nc = self.nc
        for eng in ENGS:
            lst = self.plan[eng]
            if not lst:
                continue
            def body(e, lst=lst):
                cur = None
                cur_id = None
                for sc, fn in lst:
                    if sc != cur:
                        if cur is not None:
                            nc.leave_named_scope(cur, cur_id, False)
                        if sc is not None:
                            cur_id, _ = nc.enter_named_scope(sc, False)
                        cur = sc
                    fn(e)
                if cur is not None:
                    nc.leave_named_scope(cur, cur_id, False)
            getattr(block, eng)(body)

    def init_psum(self):
        self.banks = []
        for i in range(8):
            t = self.stack.enter_context(
                self.nc.psum_tensor("ps%d" % i, [128, TT], F32))
            self.banks.append(Bank(t, i))
        self._rr = 0

    def bank(self):
        bk = self.banks[self._rr]
        self._rr = (self._rr + 1) % 8
        return bk


def build_program(stop=None):
    nc = bass.Bass("TRN2", target_bir_lowering=False)
    stack = ExitStack()
    b = Builder(nc, stack)
    S = b.sem

    dt = {}
    def din(name, shape, dtype=F32):
        dt[name] = nc.dram_tensor(name, shape, dtype, kind="ExternalInput")
        return dt[name]
    def dint(name, shape, dtype=BF16):
        dt[name] = nc.dram_tensor(name, shape, dtype)
        return dt[name]

    x0h = din("x0", [D, T])
    din("in_w", [D, D]); din("out_w", [D, D])
    din("w1a", [D, HID]); din("w2a", [HID, D])
    din("ssm_mats", [128, NH * 5 * HD])
    din("wq", [D, D]); din("wkv", [D, D]); din("wo", [D, D])
    din("fw1", [D, HID]); din("fw2", [HID, D]); din("fw3", [D, HID])
    din("vcolsin", [128, 7 * 16])
    din("b1colsin", [128, 64])
    din("maskc", [128, 8])
    din("consts_b", [128, 10 * 128], BF16)
    din("onescol_f", [128, 1])

    y_out = nc.dram_tensor("y", [D, T], F32, kind="ExternalOutput")
    dbg = nc.dram_tensor("dbg", [D, T], F32, kind="ExternalOutput")

    ssmb = dint("ssm_bf", [128, NH * 5 * HD])
    w1T = dint("w1T", [D, HID]); w3T = dint("w3T", [D, HID])
    pw_dram = dint("pw_dram", [NH * HD, 9 * 128])
    u_dram = dint("u_dram", [D, T])
    s_dram = dint("s_dram", [D, T])
    w2bf = dint("w2bf", [HID, D])
    w2t3 = dint("w2t3", [HID, D])
    w1abf = dint("w1abf", [D, HID]); w2abf = dint("w2abf", [HID, D])
    cc1_in = nc.dram_tensor("cc1_in", [128, 16], F32)
    cc1_out = nc.dram_tensor("cc1_out", [8 * 128, 16], F32)
    cc2_in = nc.dram_tensor("cc2_in", [128, 16], F32)
    cc2_out = nc.dram_tensor("cc2_out", [8 * 128, 16], F32)

    sbuf = lambda name, shape, dtype: stack.enter_context(
        nc.sbuf_tensor(name, shape, dtype))

    St = sbuf("S", [128, NCH * T], F32)
    BB1 = sbuf("BB1", [128, NCH * T], BF16)
    TBu = sbuf("TBu", [128, NCH * T], BF16)
    NB = sbuf("NB", [128, NCH * TT], BF16)
    WA = sbuf("WA", [128, 4096], BF16)
    UH = sbuf("UH", [128, 2 * T], BF16)
    SST = sbuf("SST", [128, 2 * T], BF16)
    XBs = sbuf("XBs", [128, 2 * TT], BF16)
    XQs = sbuf("XQs", [128, 2 * TT], BF16)
    TMP = sbuf("TMP", [128, 2 * TT], F32)
    XT = sbuf("XT", [128, 2 * TT], F32)
    ROWS = sbuf("ROWS", [1, 4 * TT], F32)
    RB = sbuf("RB", [1, 2 * TT], BF16)
    SSMW = sbuf("SSMW", [128, 46 * 128], BF16)
    XS0 = sbuf("XS0", [128, PAD + T], BF16)
    XS1 = sbuf("XS1", [128, PAD + T], BF16)
    VBt = sbuf("VBt", [128, T], BF16)
    CAR = sbuf("CAR", [128, NH], F32)
    CARB = sbuf("CARB", [128, NH], BF16)
    GG = sbuf("GG", [128, 8 * NH], F32)
    HIN = sbuf("HIN", [128, NH], F32)
    VCOLS = sbuf("VCOLS", [128, 7 * 16], F32)
    B1C = sbuf("B1C", [128, 64], F32)
    MASK = sbuf("MASK", [128, 8], F32)
    CONS = sbuf("CONS", [128, 10 * 128], BF16)
    ONEC = sbuf("ONEC", [128, 1], F32)
    CONSTF = sbuf("CONSTF", [128, 4], F32)

    b.init_psum()

    def cons(i):
        return CONS[:, i * 128:(i + 1) * 128]
    ONESROW_B = cons(0)[0:1, :]
    ONESCOL_B = cons(0)[:, 0:1]
    I128 = cons(1)
    H128 = cons(2)
    H64BD = cons(3)
    DFTC = cons(4)
    DFTS = cons(5)
    IDFTC = cons(6)
    IDFTS = cons(7)

    VKEYS = ["ln1w", "ln1b", "ln2w", "ln2b", "in_b", "out_b", "b2"]
    def vcol(key, c):
        i = VKEYS.index(key)
        return VCOLS[:, i * 16 + c: i * 16 + c + 1]
    def b1col(c):
        return B1C[:, c:c + 1]

    x0_lds = []
    for xp in range(4):
        x0_lds.append(b.dma("sync", "x0_%d" % xp, lambda e, xp=xp: e.dma_start(
            out=St[:, xp * 4 * T:(xp + 1) * 4 * T].rearrange("p (c t) -> p c t", c=4),
            in_=x0h[xp * 512:(xp + 1) * 512, :].rearrange("(c p) t -> p c t", p=128))))
    x0_ld = x0_lds[-1]
    b.dma("sync", "cons", lambda e: e.dma_start(
        out=CONS[:, :], in_=dt["consts_b"].ap()))
    b.dma("sync", "cons", lambda e: e.dma_start(
        out=MASK[:, :], in_=dt["maskc"].ap()))
    b.dma("sync", "cons", lambda e: e.dma_start(
        out=ONEC[:, :], in_=dt["onescol_f"].ap()))
    b.dma("sync", "cons", lambda e: e.dma_start(
        out=VCOLS[:, :], in_=dt["vcolsin"].ap()))
    cons_grp = b.dma("sync", "cons", lambda e: e.dma_start(
        out=B1C[:, :], in_=dt["b1colsin"].ap()))
    b.op("vector", lambda e: e.memset(CONSTF[:, 0:1], float(EPS)))
    b.op("vector", lambda e: e.memset(CONSTF[:, 1:2], 0.0))
    constf_ev = b.op("vector", lambda e: e.memset(CONSTF[:, 2:3], 1.0))
    EPSC = CONSTF[0:1, 0:1]

    cast_evts = {}
    cast_piece_evts = {}
    cast_queue = []
    cast_ch_state = {"i": 0, "last": [None] * 6}
    def cast_chunked(name, src, dst, rows, piece):
        for r0 in range(0, rows, piece):
            r1 = min(rows, r0 + piece)
            cast_queue.append((name, src, dst, r0, r1))

    def cast_pump(n=1, pace=True):
        for _ in range(n):
            if not cast_queue:
                return
            name, srct, dstt, r0, r1 = cast_queue.pop(0)
            if pace:
                b.wait("gpsimd", S["tE"], S["tE"].n)
            ci = cast_ch_state["i"]; cast_ch_state["i"] = (ci + 1) % 6
            b.waite("gpsimd", cast_ch_state["last"][ci])
            ev = b.dma("gpsimd", "cast%d" % ci,
                       lambda e, r0=r0, r1=r1, srct=srct, dstt=dstt:
                       e.dma_start(out=dstt[r0:r1, :], in_=srct[r0:r1, :]))
            cast_ch_state["last"][ci] = ev
            cast_evts[name] = ev
            cast_piece_evts.setdefault(name, []).append(ev)

    def cast_flush(names):
        while any(q[0] in names for q in cast_queue):
            cast_pump(1)

    cast_chunked("ssm", dt["ssm_mats"], ssmb, 128, 128)
    cast_pump(1, pace=False)
    for _name, _s, _d, _r, _p in (
            ("w1a", dt["w1a"], w1abf, D, 256),
            ("w2a", dt["w2a"], w2abf, HID, 256),
            ("w1", dt["fw1"], w1T, D, 256),
            ("w3", dt["fw3"], w3T, D, 256),
            ("w2", dt["fw2"], w2bf, HID, 256)):
        cast_chunked(_name, _s, _d, _r, _p)

    def emit_ln(src_ch, dst_ch, w_key=None, src_f32=True, src_gate=None,
                dst_gate=None):
        stats_g = [None] * NCH
        bm = b.bank(); bs = b.bank()
        b.waite("tensor", bm.free_evt); b.waite("tensor", bs.free_evt)
        for c in range(NCH):
            sl = c % 2
            xb_ap = XBs[:, sl * TT:(sl + 1) * TT]
            xq_ap = XQs[:, sl * TT:(sl + 1) * TT]
            if c >= 2:
                b.wait("vector", S["tE"], stats_g[c - 2])
            else:
                b.wait("vector", S["tE"], b.ln_slot[sl])
            if src_gate is not None:
                b.waite("vector", src_gate(c))
            if src_f32:
                b.op("vector", lambda e, c=c, xb=xb_ap: e.tensor_copy(xb, src_ch(c)))
                ev = b.op("vector", lambda e, xb=xb_ap, xq=xq_ap:
                           e.tensor_tensor(xq, xb, xb, ALU.mult))
                stats_src = xb_ap
            else:
                ev = b.op("vector", lambda e, c=c, xq=xq_ap:
                           e.tensor_tensor(xq, src_ch(c), src_ch(c), ALU.mult))
                stats_src = src_ch(c)
            b.wait("tensor", S["vE"], ev)
            first = (c == 0); last = (c == NCH - 1)
            b.run("tensor", lambda e, s=stats_src, bm=bm, f=first, l=last:
                  e.matmul(bm.ap[0:1, :], ONESCOL_B, s, start=f, stop=l))
            stats_g[c] = b.inc("tensor", S["tE"],
                               lambda e, xq=xq_ap, bs=bs, f=first, l=last:
                               e.matmul(bs.ap[0:1, :], ONESCOL_B, xq, start=f, stop=l))
        b.ln_slot[0] = stats_g[NCH - 2]
        b.ln_slot[1] = stats_g[NCH - 1]
        g_stats = stats_g[NCH - 1]

        m_row = ROWS[0:1, 0:TT]; r_row = ROWS[0:1, TT:2 * TT]
        q_row = ROWS[0:1, 2 * TT:3 * TT]; s_row = ROWS[0:1, 3 * TT:4 * TT]
        b.wait("vector", S["tE"], g_stats)
        b.op("vector", lambda e, bm=bm: e.tensor_scalar_mul(m_row, bm.ap[0:1, 0:TT], 1.0 / D))
        b.op("vector", lambda e, bs=bs: e.tensor_scalar_mul(s_row, bs.ap[0:1, 0:TT], 1.0 / D))
        b.op("vector", lambda e: e.tensor_tensor(q_row, m_row, m_row, ALU.mult))
        ev = b.op("vector", lambda e: e.tensor_tensor(s_row, s_row, q_row, ALU.subtract))
        bm.free_evt = (S["vE"], ev); bs.free_evt = (S["vE"], ev)
        b.wait("scalar", S["vE"], max(ev, constf_ev))
        ev = b.op("scalar", lambda e: e.activation(
            s_row, s_row, AF.Sqrt, bias=EPSC))
        b.wait("vector", S["sE"], ev)
        b.op("vector", lambda e: e.reciprocal(r_row, s_row))
        b.op("vector", lambda e: e.tensor_tensor(q_row, m_row, r_row, ALU.mult))
        rb_row = RB[0:1, 0:TT]; qb_row = RB[0:1, TT:2 * TT]
        b.wait("vector", S["tE"], b.rb_consumed)
        b.op("vector", lambda e: e.tensor_copy(rb_row, r_row))
        ev = b.op("vector", lambda e: e.tensor_copy(qb_row, q_row))

        br = b.bank(); bq = b.bank()
        b.waite("tensor", br.free_evt); b.waite("tensor", bq.free_evt)
        b.wait("tensor", S["vE"], ev)
        b.run("tensor", lambda e, br=br: e.matmul(br.ap[:, :], ONESROW_B, rb_row,
                                                  start=True, stop=True))
        g_bc = b.inc("tensor", S["tE"], lambda e, bq=bq: e.matmul(
            bq.ap[:, :], ONESROW_B, qb_row, start=True, stop=True))
        b.rb_consumed = g_bc
        b.wait("vector", S["tE"], g_bc)
        if dst_gate is not None:
            b.waite("vector", dst_gate)
            if w_key is not None:
                b.waite("scalar", dst_gate)
        last_ev = None
        for c in range(NCH):
            sl = c % 2
            t1 = TMP[:, sl * TT:(sl + 1) * TT]
            b.waite("vector", b.tmp_slot[sl])
            b.op("vector", lambda e, c=c, t1=t1, br=br:
                  e.tensor_tensor(t1, src_ch(c), br.ap[:, :], ALU.mult))
            if w_key is None:
                ev2 = b.op("vector", lambda e, c=c, t1=t1, bq=bq:
                            e.tensor_tensor(dst_ch(c), t1, bq.ap[:, :], ALU.subtract))
                b.tmp_slot[sl] = (S["vE"], ev2)
                last = (S["vE"], ev2)
            else:
                ev2 = b.op("vector", lambda e, c=c, t1=t1, bq=bq:
                            e.tensor_tensor(t1, t1, bq.ap[:, :], ALU.subtract))
                b.wait("scalar", S["vE"], ev2)
                ev3 = b.op("scalar", lambda e, c=c, t1=t1:
                            e.activation(dst_ch(c), t1, AF.Identity,
                                         bias=vcol(w_key[1], c), scale=vcol(w_key[0], c)))
                b.tmp_slot[sl] = (S["sE"], ev3)
                last = (S["sE"], ev3)
            last_ev = last
        br.free_evt = last_ev if w_key is None else (S["vE"], ev2)
        bq.free_evt = br.free_evt
        return last_ev

    stopped = [False]

    def dump_bf16_full(buf, done_evt):
        b.waite("vector", done_evt)
        st_ev = [None, None]
        for c in range(NCH):
            for tau in range(2):
                sl = (2 * c + tau) % 2
                xt = XT[:, sl * TT:(sl + 1) * TT]
                b.waite("vector", st_ev[sl])
                ev = b.op("vector", lambda e, c=c, tau=tau, xt=xt:
                           e.tensor_copy(xt, buf[:, c * T + tau * TT: c * T + (tau + 1) * TT]))
                b.wait("scalar", S["vE"], ev)
                st_ev[sl] = b.dma("scalar", "dmp%d" % sl, lambda e, c=c, tau=tau, xt=xt:
                              e.dma_start(out=dbg[c * 128:(c + 1) * 128,
                                                  tau * TT:(tau + 1) * TT], in_=xt))
        b.waite("scalar", st_ev[0])
        b.waite("scalar", st_ev[1])
        stopped[0] = True

    def dump_f32_full(buf, done_evt):
        b.waite("sync", done_evt)
        ev = b.dma("sync", "out", lambda e: e.dma_start(
            out=dbg.ap().rearrange("(c p) t -> p c t", p=128),
            in_=buf[:, :].rearrange("p (c t) -> p c t", c=NCH)))
        b.waite("sync", ev)
        stopped[0] = True

    def finish():
        with nc.Block() as block:
            b.flush(block)
        stack.close()
        return nc

    b.scope('p1_ln0')
    b.waite("tensor", cons_grp)
    for tau in range(2):
        h1_done = emit_ln(
            src_ch=lambda c, tau=tau: St[:, c * T + tau * TT: c * T + (tau + 1) * TT],
            dst_ch=lambda c, tau=tau: BB1[:, c * T + tau * TT: c * T + (tau + 1) * TT],
            src_f32=True, src_gate=lambda c: x0_lds[c // 4])
    if stop == "h1":
        dump_bf16_full(BB1, h1_done)
        return finish()

    b.scope('p2_ln1')
    for tau in range(2):
        t_done = emit_ln(
            src_ch=lambda c, tau=tau: BB1[:, c * T + tau * TT: c * T + (tau + 1) * TT],
            dst_ch=lambda c, tau=tau: TBu[:, c * T + tau * TT: c * T + (tau + 1) * TT],
            w_key=("ln1w", "ln1b"), src_f32=False)
    if stop == "t":
        dump_bf16_full(TBu, t_done)
        return finish()

    proj_state = {"wa_evt": [None, None], "ld_i": 0}

    def emit_proj(w_hbm, src_ch, evac, n_k=NCH, n_j=NCH, src_evt=None,
                  taus=(0, 1), shared_tau=True, mode="f32", w_ready=None):
        ps = proj_state
        if isinstance(w_ready, str):
            for pe_ in cast_piece_evts[w_ready]:
                b.waite("sync", pe_)
        elif w_ready is not None:
            b.waite("sync", w_ready)
        jg_size = 2 if shared_tau else 4
        tau_groups = [list(taus)] if shared_tau else [[t] for t in taus]
        gend = None
        for tg in tau_groups:
            for jg in range(n_j // jg_size):
                cast_pump(1)
                banks = {}
                for jj in range(jg_size):
                    for tau in tg:
                        bk = b.bank()
                        b.waite("tensor", bk.free_evt)
                        banks[(jj, tau)] = bk
                if src_evt is not None:
                    b.waite("tensor", src_evt)
                jw = jg_size * 128
                if mode in ("bf16", "f32cast"):
                    assert n_k % 4 == 0 and jw in (256, 512)
                    eng_q = "sync" if mode == "bf16" else "gpsimd"
                    chp = "wa" if mode == "bf16" else "wacast"
                    for k4 in range(n_k // 4):
                        sl = ps["ld_i"] % 2
                        ps["ld_i"] += 1
                        wa = WA[:, sl * 2048: sl * 2048 + 4 * jw]
                        b.waite(eng_q, ps["wa_evt"][sl])
                        ld = b.dma(eng_q, chp + "%d" % sl,
                                   lambda e, k4=k4, jg=jg, wa=wa, jw=jw:
                                   e.dma_start(
                                       out=wa.rearrange("p (k c) -> p k c", k=4),
                                       in_=w_hbm[k4 * 512:(k4 + 1) * 512,
                                                 jg * jw:(jg + 1) * jw]
                                       .rearrange("(k p) c -> p k c", p=128)))
                        b.waite("tensor", ld)
                        gk = None
                        for dk in range(4):
                            k = k4 * 4 + dk
                            first = (k == 0); last = (k == n_k - 1)
                            for jj in range(jg_size):
                                for tau in tg:
                                    bk = banks[(jj, tau)]
                                    fn = (lambda e, bk=bk, wa=wa, jj=jj, k=k, dk=dk,
                                          tau=tau, f=first, l=last:
                                          e.matmul(bk.ap[:, :],
                                                   wa[:, dk * jw + jj * 128: dk * jw + (jj + 1) * 128],
                                                   src_ch(k, tau), start=f, stop=l))
                                    if dk == 3 and jj == jg_size - 1 and tau == tg[-1]:
                                        gk = b.inc("tensor", S["tE"], fn)
                                    else:
                                        b.run("tensor", fn)
                        ps["wa_evt"][sl] = (S["tE"], gk)
                else:
                    for k in range(n_k):
                        sl = ps["ld_i"] % 2
                        ps["ld_i"] += 1
                        wa = WA[:, sl * 2048: sl * 2048 + jw]
                        b.waite("sync", ps["wa_evt"][sl])
                        ld = b.dma("sync", "wa%d" % sl, lambda e, k=k, jg=jg, wa=wa, jw=jw:
                                   e.dma_start(out=wa,
                                               in_=w_hbm[jg * jw:(jg + 1) * jw,
                                                         k * 128:(k + 1) * 128],
                                               transpose=True))
                        b.waite("tensor", ld)
                        first = (k == 0); last = (k == n_k - 1)
                        gk = None
                        for jj in range(jg_size):
                            for tau in tg:
                                bk = banks[(jj, tau)]
                                fn = (lambda e, bk=bk, wa=wa, jj=jj, k=k, tau=tau,
                                      f=first, l=last:
                                      e.matmul(bk.ap[:, :], wa[:, jj * 128:(jj + 1) * 128],
                                               src_ch(k, tau), start=f, stop=l))
                                if jj == jg_size - 1 and tau == tg[-1]:
                                    gk = b.inc("tensor", S["tE"], fn)
                                else:
                                    b.run("tensor", fn)
                        ps["wa_evt"][sl] = (S["tE"], gk)
                gend = gk
                for jj in range(jg_size):
                    for tau in tg:
                        evac(jg * jg_size + jj, tau, banks[(jj, tau)], gend)
        return gend

    b.scope('p3_inproj')
    u_st = [None]
    def evac_inproj(j, tau, bk, gend):
        sl = j % 2
        dst = UH[:, sl * T + tau * TT: sl * T + (tau + 1) * TT]
        b.wait("scalar", S["tE"], gend)
        b.waite("scalar", u_st[0])
        ev = b.op("scalar", lambda e, bk=bk, dst=dst, j=j:
                   e.activation(dst, bk.ap[:, :], AF.Identity, bias=vcol("in_b", j)))
        bk.free_evt = (S["sE"], ev)
        if tau == 1:
            b.wait("scalar", S["sE"], ev)
            u_st[0] = b.dma("scalar", "ust", lambda e, j=j, sl=sl:
                            e.dma_start(out=u_dram[j * 128:(j + 1) * 128, :],
                                        in_=UH[:, sl * T:(sl + 1) * T]))

    emit_proj(dt["in_w"],
              lambda k, tau: TBu[:, k * T + tau * TT: k * T + (tau + 1) * TT],
              evac_inproj, src_evt=t_done, mode="f32cast")
    if stop == "u":
        b.waite("sync", u_st[0])
        ld = b.dma("sync", "out", lambda e: e.dma_start(
            out=BB1[:, :].rearrange("p (c t) -> p c t", c=NCH),
            in_=u_dram.ap().rearrange("(c p) t -> p c t", p=128)))
        b.waite("vector", ld)
        dump_bf16_full(BB1, None)
        return finish()

    cast_pump(40, pace=False)

    b.scope('p4_scan')
    def pwp(par, i):
        off = 23 * par
        return SSMW[:, (off + i) * 128:(off + i + 1) * 128]

    ssmw_used = [0, 0]
    pw_store_ref = [None]
    xs_store = [None]

    b.op("vector", lambda e: e.memset(XS0[:, 0:PAD], 0.0))
    pad_ev = b.op("vector", lambda e: e.memset(XS1[:, 0:PAD], 0.0))
    b.wait("tensor", S["vE"], pad_ev)

    def ssm_head_loads(h, which):
        par = h % 2
        b.wait("scalar", S["tE"], ssmw_used[par])
        b.waite("scalar", cast_evts["ssm"])
        chn = "ssmw%d" % par
        off = 23 * par
        evs = []
        evs.append(b.dma("scalar", chn, lambda e, h=h, off=off: e.dma_start(
            out=SSMW[:, off * 128:(off + 5) * 128],
            in_=ssmb[:, h * 640:(h + 1) * 640])))
        if which == "corr":
            evs.append(b.dma("scalar", chn, lambda e, h=h, off=off: e.dma_start(
                out=SSMW[:, (off + 5) * 128:(off + 14) * 128],
                in_=pw_dram[h * 128:(h + 1) * 128, :])))
        return evs[-1]

    def ssm_powers_stepper(h, ld_ev):
        par = h % 2
        pw = lambda i: pwp(par, i)
        st = {"k": 1, "prev_ev": None}

        def step(n=1):
            for _ in range(n):
                k = st["k"]
                if k > 9:
                    return st["prev_ev"]
                st["k"] += 1
                if k == 1:
                    b.waite("tensor", ld_ev)
                    if pw_store_ref[0] is not None:
                        b.waite("vector", pw_store_ref[0][par])
                bka = b.bank(); bkb = b.bank()
                b.waite("tensor", bka.free_evt); b.waite("tensor", bkb.free_evt)
                if st["prev_ev"] is not None:
                    b.wait("tensor", S["vE"], st["prev_ev"])
                qp = pw(0) if k == 1 else pw(13 + k - 1)
                pp = pw(1) if k == 1 else pw(4 + k - 1)
                b.run("tensor", lambda e, bka=bka, qp=qp, pp=pp: e.matmul(
                    bka.ap[:, 0:128], qp, pp, start=True, stop=True))
                g = b.inc("tensor", S["tE"], lambda e, bkb=bkb, qp=qp, pp=pp: e.matmul(
                    bkb.ap[:, 0:128], pp, qp, start=True, stop=True))
                b.wait("vector", S["tE"], g)
                b.op("vector", lambda e, bka=bka, k=k, pw=pw: e.tensor_copy(pw(4 + k), bka.ap[:, 0:128]))
                st["prev_ev"] = b.op("vector", lambda e, bkb=bkb, k=k, pw=pw:
                                e.tensor_copy(pw(13 + k), bkb.ap[:, 0:128]))
                bka.free_evt = (S["vE"], st["prev_ev"]); bkb.free_evt = (S["vE"], st["prev_ev"])
            return st["prev_ev"]
        return step

    def ssm_powers(h, ld_ev):
        return ssm_powers_stepper(h, ld_ev)(9)

    def load_uh(h):
        sl = h % 2
        b.wait("scalar", S["tE"], ssmw_used[sl])
        b.waite("scalar", u_st[0])
        ev = b.dma("scalar", "uh%d" % sl, lambda e, h=h, sl=sl: e.dma_start(
            out=UH[:, sl * T:(sl + 1) * T], in_=u_dram[h * 128:(h + 1) * 128, :]))
        return sl, ev

    pw_store = [None, None]
    pw_store_ref[0] = pw_store
    ldh = {0: ssm_head_loads(0, "scan")}
    uhd = {0: load_uh(0)}
    pwev = {0: ssm_powers(0, ldh[0])}
    b.wait("scalar", S["vE"], pwev[0])
    pw_store[0] = b.dma("scalar", "pwst0", lambda e: e.dma_start(
        out=pw_dram[0:128, :], in_=SSMW[:, 5 * 128:14 * 128]))
    for h in range(NH):
        par = h % 2
        pw = lambda i, par=par: pwp(par, i)
        usl, u_ev = uhd[h]
        ev_v = ev_s = None
        for tau in range(2):
            bk = b.bank(); b.waite("tensor", bk.free_evt)
            b.waite("tensor", u_ev)
            b.waite("tensor", ldh[h])
            g = b.inc("tensor", S["tE"], lambda e, bk=bk, usl=usl, tau=tau, pw=pw: e.matmul(
                bk.ap[:, :], pw(2), UH[:, usl * T + tau * TT: usl * T + (tau + 1) * TT],
                start=True, stop=True))
            eng = "vector" if tau == 0 else "scalar"
            sem = S["vE"] if tau == 0 else S["sE"]
            b.wait(eng, S["tE"], g)
            b.waite(eng, xs_store[0])
            if tau == 0:
                ev_v = b.op(eng, lambda e, bk=bk, tau=tau: e.tensor_copy(
                    XS0[:, PAD + tau * TT: PAD + (tau + 1) * TT], bk.ap[:, :]))
                bk.free_evt = (S["vE"], ev_v)
            else:
                ev_s = b.op(eng, lambda e, bk=bk, tau=tau: e.activation(
                    XS0[:, PAD + tau * TT: PAD + (tau + 1) * TT], bk.ap[:, :], AF.Identity))
                bk.free_evt = (S["sE"], ev_s)
        pstep = None
        if h + 1 < NH:
            ldh[h + 1] = ssm_head_loads(h + 1, "scan")
            uhd[h + 1] = load_uh(h + 1)
            pstep = ssm_powers_stepper(h + 1, ldh[h + 1])
        cast_pump(2)
        PSCHED = [0, 0, 1, 1, 1, 1, 2, 2, 1, 0]
        pw_ev_last = None
        xsrc, xdst = XS0, XS1
        for k in range(10):
            sh = 1 << k
            nev_v = nev_s = None
            for tau in range(2):
                bk = b.bank(); b.waite("tensor", bk.free_evt)
                b.wait("tensor", S["vE"], ev_v)
                b.wait("tensor", S["sE"], ev_s)
                b.run("tensor", lambda e, bk=bk, xsrc=xsrc, tau=tau: e.matmul(
                    bk.ap[:, :], I128,
                    xsrc[:, PAD + tau * TT: PAD + (tau + 1) * TT], start=True, stop=False))
                g = b.inc("tensor", S["tE"], lambda e, bk=bk, xsrc=xsrc, tau=tau, k=k, sh=sh, pw=pw:
                          e.matmul(bk.ap[:, :], pw(1) if k == 0 else pw(4 + k),
                                   xsrc[:, PAD + tau * TT - sh: PAD + (tau + 1) * TT - sh],
                                   start=False, stop=True))
                if tau == 0:
                    b.wait("vector", S["tE"], g)
                    nev_v = b.op("vector", lambda e, bk=bk, xdst=xdst, tau=tau:
                               e.tensor_copy(xdst[:, PAD + tau * TT: PAD + (tau + 1) * TT],
                                             bk.ap[:, :]))
                    bk.free_evt = (S["vE"], nev_v)
                else:
                    b.wait("scalar", S["tE"], g)
                    nev_s = b.op("scalar", lambda e, bk=bk, xdst=xdst, tau=tau:
                               e.activation(xdst[:, PAD + tau * TT: PAD + (tau + 1) * TT],
                                            bk.ap[:, :], AF.Identity))
                    bk.free_evt = (S["sE"], nev_s)
            ev_v, ev_s = nev_v, nev_s
            xsrc, xdst = xdst, xsrc
            if pstep is not None:
                pw_ev_last = pstep(PSCHED[k])
        if h + 1 < NH:
            pwev[h + 1] = pw_ev_last
            b.wait("scalar", S["vE"], pwev[h + 1])
            pw_store[(h + 1) % 2] = b.dma("scalar", "pwst%d" % ((h + 1) % 2),
                lambda e, h1=h + 1: e.dma_start(
                    out=pw_dram[h1 * 128:(h1 + 1) * 128, :],
                    in_=SSMW[:, (23 * ((h1) % 2) + 5) * 128:(23 * ((h1) % 2) + 14) * 128]))
        ssmw_used[par] = S["tE"].n
        b.wait("vector", S["sE"], ev_s)
        car_ev = b.op("vector", lambda e, h=h: e.tensor_copy(
            CAR[:, h:h + 1], XS0[:, PAD + T - 1: PAD + T]))
        b.wait("scalar", S["vE"], car_ev)
        xs_store[0] = b.dma("scalar", "xst", lambda e, h=h: e.dma_start(
            out=s_dram[h * 128:(h + 1) * 128, :], in_=XS0[:, PAD:PAD + T]))
    if stop == "states":
        b.waite("sync", xs_store[0])
        ld = b.dma("sync", "out", lambda e: e.dma_start(
            out=BB1[:, :].rearrange("p (c t) -> p c t", c=NCH),
            in_=s_dram.ap().rearrange("(c p) t -> p c t", p=128)))
        b.waite("vector", ld)
        dump_bf16_full(BB1, None)
        return finish()

    b.scope('p4b_cc')
    b.wait("gpsimd", S["vE"], car_ev)
    b.wait("sync", S["vE"], car_ev)
    cin_ev = b.dma("sync", "ccio", lambda e: e.dma_start(
        out=cc1_in.ap(), in_=CAR[:, :]))
    b.waite("gpsimd", cin_ev)
    cc_ev = b.inc("gpsimd", S["cc"], lambda e: e.collective_compute(
        "AllGather", ALU.bypass, replica_groups=[list(range(8))],
        ins=[cc1_in.ap().opt()], outs=[cc1_out.ap().opt()]))
    b.wait("gpsimd", S["cc"], cc_ev)
    b.wait("sync", S["cc"], cc_ev)
    gg_ev = b.dma("sync", "ccio", lambda e: e.dma_start(
        out=GG[:, :].rearrange("p (r h) -> p r h", r=8),
        in_=cc1_out.ap().rearrange("(r i) h -> i r h", i=128)))
    b.waite("vector", gg_ev)
    b.waite("vector", cons_grp)
    b.op("vector", lambda e: e.tensor_scalar(
        HIN[:, :], GG[:, 0:NH], MASK[:, 0:1], None, ALU.mult))
    for r in range(1, 8):
        b.op("vector", lambda e, r=r: e.scalar_tensor_tensor(
            HIN[:, :], GG[:, r * NH:(r + 1) * NH], MASK[:, r:r + 1], HIN[:, :],
            ALU.mult, ALU.add))
    hinb_ev = b.op("vector", lambda e: e.tensor_copy(CARB[:, :], HIN[:, :]))

    b.scope('p4c_corr')
    sst_used = [0]
    VB2 = [VBt[:, 0:T], XQs[:, 0:2 * TT]]
    _bar4c = (S["tE"], S["tE"].n)
    vb_used = [_bar4c, _bar4c]

    def emit_vchain(h):
        par = h % 2
        pw = lambda i: pwp(par, i)
        vb = VB2[par]
        bk = b.bank(); b.waite("tensor", bk.free_evt)
        b.waite("tensor", ldh[h])
        b.wait("tensor", S["vE"], hinb_ev)
        g = b.inc("tensor", S["tE"], lambda e, bk=bk, h=h, pw=pw: e.matmul(
            bk.ap[:, 0:1], pw(1), CARB[:, h:h + 1], start=True, stop=True))
        b.wait("vector", S["tE"], g)
        b.waite("vector", vb_used[par])
        vev = b.op("vector", lambda e, bk=bk, vb=vb: e.tensor_copy(
            vb[:, 0:1], bk.ap[:, 0:1]))
        bk.free_evt = (S["vE"], vev)
        for k in range(10):
            w = 1 << k
            bk = b.bank(); b.waite("tensor", bk.free_evt)
            b.wait("tensor", S["vE"], vev)
            g = b.inc("tensor", S["tE"], lambda e, bk=bk, k=k, w=w, pw=pw, vb=vb: e.matmul(
                bk.ap[:, 0:w], pw(1) if k == 0 else pw(4 + k), vb[:, 0:w],
                start=True, stop=True))
            b.wait("vector", S["tE"], g)
            vev = b.op("vector", lambda e, bk=bk, w=w, vb=vb: e.tensor_copy(
                vb[:, w:2 * w], bk.ap[:, 0:w]))
            bk.free_evt = (S["vE"], vev)
        return vev

    b.waite("scalar", pw_store[0]); b.waite("scalar", pw_store[1])
    ldh = {0: ssm_head_loads(0, "corr")}
    uhd = {0: load_uh(0)}
    vch = {0: emit_vchain(0)}
    std = {}
    for h in range(NH):
        par = h % 2
        pw = lambda i, par=par: pwp(par, i)
        if h == 0:
            sl = 0
            b.wait("scalar", S["tE"], sst_used[0])
            b.waite("scalar", xs_store[0])
            std[0] = b.dma("scalar", "sst0", lambda e: e.dma_start(
                out=SST[:, 0:T], in_=s_dram[0:128, :]))
        st_ld = std[h]
        usl, u_ev = uhd[h]
        sl = h % 2
        if h + 1 < NH:
            ldh[h + 1] = ssm_head_loads(h + 1, "corr")
            uhd[h + 1] = load_uh(h + 1)
            sl2 = (h + 1) % 2
            b.wait("scalar", S["tE"], sst_used[0])
            std[h + 1] = b.dma("scalar", "sst%d" % sl2, lambda e, h=h, sl2=sl2: e.dma_start(
                out=SST[:, sl2 * T:(sl2 + 1) * T],
                in_=s_dram[(h + 1) * 128:(h + 2) * 128, :]))
            vch[h + 1] = emit_vchain(h + 1)
        cast_pump(2)
        b.waite("vector", st_ld)
        b.wait("vector", S["vE"], vch[h])
        add_ev = b.op("vector", lambda e, sl=sl: e.tensor_tensor(
            SST[:, sl * T:(sl + 1) * T], SST[:, sl * T:(sl + 1) * T],
            VB2[sl][:, 0:T], ALU.add))
        vb_used[sl] = (S["vE"], add_ev)
        for tau in range(2):
            bk = b.bank(); b.waite("tensor", bk.free_evt)
            b.wait("tensor", S["vE"], add_ev)
            b.waite("tensor", u_ev)
            b.run("tensor", lambda e, bk=bk, sl=sl, tau=tau, pw=pw: e.matmul(
                bk.ap[:, :], pw(3), SST[:, sl * T + tau * TT: sl * T + (tau + 1) * TT],
                start=True, stop=False))
            g = b.inc("tensor", S["tE"], lambda e, bk=bk, usl=usl, tau=tau, pw=pw: e.matmul(
                bk.ap[:, :], pw(4), UH[:, usl * T + tau * TT: usl * T + (tau + 1) * TT],
                start=False, stop=True))
            if tau == 0:
                b.wait("vector", S["tE"], g)
                yev = b.op("vector", lambda e, bk=bk, h=h, tau=tau: e.tensor_copy(
                    St[:, h * T + tau * TT: h * T + (tau + 1) * TT], bk.ap[:, :]))
                bk.free_evt = (S["vE"], yev)
            else:
                b.wait("scalar", S["tE"], g)
                yev = b.op("scalar", lambda e, bk=bk, h=h, tau=tau: e.activation(
                    St[:, h * T + tau * TT: h * T + (tau + 1) * TT], bk.ap[:, :], AF.Identity))
                bk.free_evt = (S["sE"], yev)
        ssmw_used[par] = S["tE"].n
        sst_used[0] = S["tE"].n
    b.ln_slot = [S["tE"].n, S["tE"].n]
    y_done = (S["vE"], S["vE"].n)
    y_done_s = (S["sE"], S["sE"].n)
    if stop == "yss":
        dump_f32_full(St, y_done)
        return finish()

    b.scope('p5_outproj')
    z_done = None
    for tau in range(2):
        cast_ev = None
        for c in range(NCH):
            b.waite("vector", y_done)
            b.wait("vector", S["sE"], y_done_s[1])
            cast_ev = b.op("vector", lambda e, c=c, tau=tau: e.tensor_copy(
                NB[:, c * TT:(c + 1) * TT], St[:, c * T + tau * TT: c * T + (tau + 1) * TT]))
        def evac_z(j, t2, bk, gend, tau=tau):
            b.wait("vector", S["tE"], gend)
            ev = b.op("vector", lambda e, bk=bk, j=j, tau=tau:
                       e.scalar_tensor_tensor(
                           St[:, j * T + tau * TT: j * T + (tau + 1) * TT],
                           bk.ap[:, :], vcol("out_b", j),
                           BB1[:, j * T + tau * TT: j * T + (tau + 1) * TT],
                           ALU.add, ALU.add))
            bk.free_evt = (S["vE"], ev)
            return ev
        evz = []
        def evac_z_wrap(j, t2, bk, gend):
            evz.append(evac_z(j, t2, bk, gend))
        emit_proj(dt["out_w"], lambda k, t2, tau=tau: NB[:, k * TT:(k + 1) * TT],
                  evac_z_wrap, src_evt=(S["vE"], cast_ev), taus=(tau,),
                  shared_tau=False, mode="f32cast")
        z_done = (S["vE"], evz[-1])
    if stop == "z":
        dump_f32_full(St, z_done)
        return finish()

    b.scope('p68_ffnA')
    def hidtile(c):
        buf = BB1 if c < 32 else TBu
        return buf[:, (c % 32) * TT:(c % 32 + 1) * TT]

    xt_used = [None, None]

    def emit_x0_add(tau):
        ev = None
        for c in range(NCH):
            sl = c % 2
            xt = XT[:, sl * TT:(sl + 1) * TT]
            b.waite("sync", xt_used[sl])
            ld = b.dma("sync", "xt%d" % sl, lambda e, c=c, tau=tau, xt=xt:
                       e.dma_start(out=xt, in_=x0h[c * 128:(c + 1) * 128,
                                                   tau * TT:(tau + 1) * TT]))
            b.waite("vector", ld)
            ev = b.op("vector", lambda e, c=c, tau=tau, xt=xt: e.tensor_tensor(
                St[:, c * T + tau * TT: c * T + (tau + 1) * TT],
                St[:, c * T + tau * TT: c * T + (tau + 1) * TT], xt, ALU.add))
            xt_used[sl] = (S["vE"], ev)
        return ev

    ln2_evs = {}
    ln2_evs[0] = emit_ln(
        src_ch=lambda c: St[:, c * T: c * T + TT],
        dst_ch=lambda c: NB[:, c * TT:(c + 1) * TT],
        w_key=("ln2w", "ln2b"), src_f32=True)
    emit_x0_add(0)
    x1_ev = [None]
    for tau in range(2):
        gelu_ev = {}
        def evac_gelu(j, t2, bk, gend):
            b.wait("scalar", S["tE"], gend)
            ev = b.op("scalar", lambda e, bk=bk, j=j: e.activation(
                hidtile(j), bk.ap[:, :], AF.Gelu, bias=b1col(j)))
            bk.free_evt = (S["sE"], ev)
            gelu_ev[j] = ev
        w1a_gend = emit_proj(w1abf, lambda k, t2: NB[:, k * TT:(k + 1) * TT],
                  evac_gelu, n_k=NCH, n_j=HCH, src_evt=ln2_evs[tau],
                  taus=(tau,), shared_tau=False, mode="bf16", w_ready="w1a")
        if tau == 0:
            ln2_evs[1] = emit_ln(
                src_ch=lambda c: St[:, c * T + TT: c * T + 2 * TT],
                dst_ch=lambda c: NB[:, c * TT:(c + 1) * TT],
                w_key=("ln2w", "ln2b"), src_f32=True,
                dst_gate=(S["tE"], w1a_gend))
            emit_x0_add(1)
        def evac_w2a(j, t2, bk, gend, tau=tau):
            b.wait("vector", S["tE"], gend)
            ev = b.op("vector", lambda e, bk=bk, j=j, tau=tau:
                      e.scalar_tensor_tensor(
                          St[:, j * T + tau * TT: j * T + (tau + 1) * TT],
                          bk.ap[:, :], vcol("b2", j),
                          St[:, j * T + tau * TT: j * T + (tau + 1) * TT],
                          ALU.add, ALU.add))
            bk.free_evt = (S["vE"], ev)
            x1_ev[0] = ev
        emit_proj(w2abf, lambda k, t2: hidtile(k),
                  evac_w2a, n_k=HCH, n_j=NCH, src_evt=(S["sE"], gelu_ev[HCH - 1]),
                  taus=(tau,), shared_tau=False, mode="bf16", w_ready="w2a")
    x1_done = (S["vE"], x1_ev[0])
    if stop == "x1":
        dump_f32_full(St, x1_done)
        return finish()

    b.scope('p_w2prep')
    prep_in_used = [None, None, None, None]
    prep_out_used = [None, None, None, None]
    PREP_OUT = [WA[:, 0:2048], WA[:, 2048:4096], UH[:, 0:2048], SST[:, 0:2048]]

    def prep_pass():
        st_ev = None
        for i in range(HCH):
            sl = i % 4
            inslot = NB[:, sl * 2048:(sl + 1) * 2048]
            chn = "prep_in%d" % sl
            b.waite("sync", prep_in_used[sl])
            ld = b.dma("sync", chn, lambda e, i=i, inslot=inslot: e.dma_start(
                out=inslot, in_=w2bf[i * 128:(i + 1) * 128, :]))
            b.waite("tensor", ld)
            out_wa = PREP_OUT[sl]
            ev_ev = None
            for dtile in range(4):
                bk = b.bank(); b.waite("tensor", bk.free_evt)
                g = b.inc("tensor", S["tE"], lambda e, bk=bk, inslot=inslot,
                          dtile=dtile: e.matmul(
                    bk.ap[:, :], H128, inslot[:, dtile * TT:(dtile + 1) * TT],
                    start=True, stop=True))
                eng = "vector" if dtile % 2 == 0 else "scalar"
                b.wait(eng, S["tE"], g)
                if dtile == 0:
                    b.waite("vector", prep_out_used[sl])
                if dtile == 1:
                    b.waite("scalar", prep_out_used[sl])
                if eng == "vector":
                    ev_ev = b.op(eng, lambda e, bk=bk, dtile=dtile, out_wa=out_wa:
                                 e.tensor_copy(out_wa[:, dtile * TT:(dtile + 1) * TT],
                                               bk.ap[:, :]))
                    bk.free_evt = (S["vE"], ev_ev)
                else:
                    ev_s = b.op(eng, lambda e, bk=bk, dtile=dtile, out_wa=out_wa:
                                e.activation(out_wa[:, dtile * TT:(dtile + 1) * TT],
                                             bk.ap[:, :], AF.Identity))
                    bk.free_evt = (S["sE"], ev_s)
            prep_in_used[sl] = (S["tE"], S["tE"].n)
            b.wait("scalar", S["vE"], ev_ev)
            b.wait("scalar", S["sE"], ev_s)
            st = b.dma("scalar", "prep_st%d" % sl, lambda e, i=i, out_wa=out_wa:
                       e.dma_start(
                out=bass.AP(w2t3, i * D, [[128 * D, 64], [64 * D, 2], [1, D]]),
                in_=out_wa))
            prep_out_used[sl] = st
            st_ev = st
        return st_ev

    cast_flush(["w2"])
    for pe_ in cast_piece_evts["w2"]:
        b.waite("sync", pe_)
    prep_bar = (S["tE"], S["tE"].n)
    for _q in range(4):
        prep_in_used[_q] = prep_bar
    b.wait("sync", S["tE"], S["tE"].n)
    b.wait("vector", S["tE"], S["tE"].n)
    b.wait("scalar", S["tE"], S["tE"].n)
    prep_pass()
    for _q in range(4):
        b.waite("sync", prep_out_used[_q])
    p2_end = prep_out_used[3]
    b.wait("vector", S["tE"], S["tE"].n)
    b.wait("scalar", S["tE"], S["tE"].n)
    proj_state["wa_evt"] = [prep_out_used[0], prep_out_used[1]]

    def emit_ffn(x_stop=None):
        out_ev = [None]
        ln_evs = {}
        ln_evs[0] = emit_ln(
            src_ch=lambda c: St[:, c * T: c * T + TT],
            dst_ch=lambda c: NB[:, c * TT:(c + 1) * TT],
            w_key=None, src_f32=True)
        for tau in range(2):
            silu_ev = {}
            h64_ev = {}
            def evac_silu(j, t2, bk, gend):
                b.wait("scalar", S["tE"], gend)
                ev = b.op("scalar", lambda e, bk=bk, j=j: e.activation(
                    hidtile(j), bk.ap[:, :], AF.Silu))
                bk.free_evt = (S["sE"], ev)
                silu_ev[j] = ev
            emit_proj(w1T, lambda k, t2: NB[:, k * TT:(k + 1) * TT],
                      evac_silu, n_k=NCH, n_j=HCH, src_evt=ln_evs[tau],
                      taus=(tau,), shared_tau=False, mode="bf16",
                      w_ready="w1")
            def evac_lin(j, t2, bk, gend):
                b.wait("vector", S["tE"], gend)
                b.wait("vector", S["sE"], silu_ev[j])
                ev = b.op("vector", lambda e, bk=bk, j=j: e.tensor_tensor(
                    hidtile(j), hidtile(j), bk.ap[:, :], ALU.mult))
                bk.free_evt = (S["vE"], ev)
                silu_ev[j] = ev
                bh = b.bank(); b.waite("tensor", bh.free_evt)
                b.wait("tensor", S["vE"], ev)
                g2 = b.inc("tensor", S["tE"], lambda e, bh=bh, j=j: e.matmul(
                    bh.ap[:, :], H64BD, hidtile(j), start=True, stop=True))
                b.wait("scalar", S["tE"], g2)
                ev2 = b.op("scalar", lambda e, bh=bh, j=j: e.activation(
                    hidtile(j), bh.ap[:, :], AF.Identity))
                bh.free_evt = (S["sE"], ev2)
                h64_ev[j] = ev2
            w3_gend = emit_proj(w3T, lambda k, t2: NB[:, k * TT:(k + 1) * TT],
                      evac_lin, n_k=NCH, n_j=HCH, src_evt=ln_evs[tau],
                      taus=(tau,), shared_tau=False, mode="bf16",
                      w_ready="w3")
            if tau == 0:
                ln_evs[1] = emit_ln(
                    src_ch=lambda c: St[:, c * T + TT: c * T + 2 * TT],
                    dst_ch=lambda c: NB[:, c * TT:(c + 1) * TT],
                    w_key=None, src_f32=True,
                    dst_gate=(S["tE"], w3_gend))
            def evac_out(j, t2, bk, gend, tau=tau):
                b.wait("vector", S["tE"], gend)
                ev = b.op("vector", lambda e, bk=bk, j=j, tau=tau: e.tensor_tensor(
                    St[:, j * T + tau * TT: j * T + (tau + 1) * TT],
                    St[:, j * T + tau * TT: j * T + (tau + 1) * TT],
                    bk.ap[:, :], ALU.add))
                bk.free_evt = (S["vE"], ev)
                out_ev[0] = ev
            emit_proj(w2t3, lambda k, t2: hidtile(k),
                      evac_out, n_k=HCH, n_j=NCH,
                      src_evt=(S["sE"], h64_ev[HCH - 1]),
                      taus=(tau,), shared_tau=False, mode="bf16",
                      w_ready=p2_end)
        return (S["vE"], out_ev[0])

    b.scope('p_ffn1')
    x2_done = emit_ffn()
    if stop == "x2":
        dump_f32_full(St, x2_done)
        return finish()

    b.scope('p_attnb_proj')
    ones_ev = b.op("vector", lambda e: e.memset(VBt[:, :], 1.0))

    qr_ev = [None]
    for tau in range(2):
        ln_ev = emit_ln(
            src_ch=lambda c, tau=tau: St[:, c * T + tau * TT: c * T + (tau + 1) * TT],
            dst_ch=lambda c: NB[:, c * TT:(c + 1) * TT],
            w_key=None, src_f32=True)
        for (whbm, dstbuf) in ((dt["wq"], TBu), (dt["wkv"], BB1)):
            def evac_raw(j, t2, bk, gend, dstbuf=dstbuf, tau=tau):
                b.wait("vector", S["tE"], gend)
                ev = b.op("vector", lambda e, bk=bk, j=j, tau=tau, dstbuf=dstbuf:
                          e.tensor_copy(dstbuf[:, j * T + tau * TT: j * T + (tau + 1) * TT],
                                        bk.ap[:, :]))
                bk.free_evt = (S["vE"], ev)
                qr_ev[0] = ev
            emit_proj(whbm, lambda k, t2: NB[:, k * TT:(k + 1) * TT],
                      evac_raw, n_k=NCH, n_j=NCH, src_evt=ln_ev,
                      taus=(tau,), shared_tau=False, mode="f32cast")

    b.scope('p_attnb_norm')
    nrm_state = {"i": 0, "sq": [0, 0], "srow": [0, 0, 0, 0],
                 "rb": [0, 0], "xb": [0, 0]}

    def emit_normalize(buf, h, tau):
        ns = nrm_state
        i = ns["i"]; ns["i"] += 1
        tile = buf[:, h * T + tau * TT: h * T + (tau + 1) * TT]
        sq = XQs[:, (i % 2) * TT:(i % 2 + 1) * TT]
        b.wait("gpsimd", S["tE"], ns["sq"][i % 2])
        if ns.get("src_ev") is not None:
            b.wait("gpsimd", S["vE"], ns["src_ev"])
        sq_ev = b.op("gpsimd", lambda e, sq=sq, tile=tile:
                     e.tensor_tensor(sq, tile, tile, ALU.mult))
        bk = b.bank(); b.waite("tensor", bk.free_evt)
        b.wait("tensor", S["gE"], sq_ev)
        g = b.inc("tensor", S["tE"], lambda e, bk=bk, sq=sq: e.matmul(
            bk.ap[0:1, :], ONESCOL_B, sq, start=True, stop=True))
        ns["sq"][i % 2] = g
        s_row = ROWS[0:1, (i % 4) * TT:(i % 4 + 1) * TT]
        b.wait("scalar", S["tE"], g)
        b.wait("scalar", S["vE"], ns["srow"][i % 4])
        sc_ev = b.op("scalar", lambda e, bk=bk, s_row=s_row: e.activation(
            s_row, bk.ap[0:1, :], AF.Sqrt, scale=128.0))
        bk.free_evt = (S["sE"], sc_ev)
        b.wait("vector", S["sE"], sc_ev)
        b.op("vector", lambda e, s_row=s_row: e.reciprocal(s_row, s_row))
        rb_row = RB[0:1, (i % 2) * TT:(i % 2 + 1) * TT]
        b.wait("vector", S["tE"], ns["rb"][i % 2])
        rb_ev = b.op("vector", lambda e, rb_row=rb_row, s_row=s_row:
                     e.tensor_copy(rb_row, s_row))
        ns["srow"][i % 4] = rb_ev
        bc = b.bank(); b.waite("tensor", bc.free_evt)
        b.wait("tensor", S["vE"], rb_ev)
        g2 = b.inc("tensor", S["tE"], lambda e, bc=bc, rb_row=rb_row: e.matmul(
            bc.ap[:, :], ONESROW_B, rb_row, start=True, stop=True))
        ns["rb"][i % 2] = g2
        b.wait("scalar", S["tE"], g2)
        xb = XBs[:, (i % 2) * TT:(i % 2 + 1) * TT]
        b.waite("scalar", ns["xb_used"][i % 2])
        cp_ev = b.op("scalar", lambda e, bc=bc, xb=xb: e.activation(
            xb, bc.ap[:, :], AF.Identity))
        bc.free_evt = (S["sE"], cp_ev)
        b.wait("vector", S["sE"], cp_ev)
        nrm_ev = b.op("vector", lambda e, tile=tile, xb=xb:
                      e.tensor_tensor(tile, tile, xb, ALU.mult))
        ns["xb_used"][i % 2] = (S["vE"], nrm_ev)
        return nrm_ev

    nrm_state["sq"] = [b.ln_slot[0], b.ln_slot[1]]
    nrm_state["rb"] = [b.rb_consumed, b.rb_consumed]
    nrm_state["srow"] = [S["vE"].n] * 4
    nrm_state["xb_used"] = [(S["vE"], S["vE"].n), (S["vE"], S["vE"].n)]
    nrm_state["src_ev"] = qr_ev[0]
    kv_norm_ev = None
    for h in range(NH):
        for tau in range(2):
            emit_normalize(TBu, h, tau)
            kv_norm_ev = emit_normalize(BB1, h, tau)
    b.ln_slot = [max(nrm_state["sq"]), max(nrm_state["sq"])]
    b.rb_consumed = max(nrm_state["rb"])

    b.waite("vector", cin_ev)
    scan_evs = {}
    for h in range(NH):
        sc_ev = b.op("vector", lambda e, h=h: e.tensor_tensor_scan(
            BB1[:, h * T:(h + 1) * T], VBt[:, :], BB1[:, h * T:(h + 1) * T],
            0.0, ALU.mult, ALU.add))
        car_ev = b.op("vector", lambda e, h=h: e.tensor_copy(
            CAR[:, h:h + 1], BB1[:, h * T + T - 1: h * T + T]))
        scan_evs[h] = car_ev

    b.wait("gpsimd", S["vE"], scan_evs[NH - 1])
    b.wait("gpsimd", S["vE"], hinb_ev)
    b.wait("sync", S["vE"], scan_evs[NH - 1])
    cin2 = b.dma("sync", "ccio", lambda e: e.dma_start(
        out=cc2_in.ap(), in_=CAR[:, :]))
    b.waite("gpsimd", cin2)
    cc2_ev = b.inc("gpsimd", S["cc"], lambda e: e.collective_compute(
        "AllGather", ALU.bypass, replica_groups=[list(range(8))],
        ins=[cc2_in.ap().opt()], outs=[cc2_out.ap().opt()]))
    b.wait("gpsimd", S["cc"], cc2_ev)
    b.wait("sync", S["cc"], cc2_ev)
    b.wait("sync", S["vE"], hinb_ev)
    gg2 = b.dma("sync", "ccio", lambda e: e.dma_start(
        out=GG[:, :].rearrange("p (r h) -> p r h", r=8),
        in_=cc2_out.ap().rearrange("(r i) h -> i r h", i=128)))
    b.waite("vector", gg2)
    b.op("vector", lambda e: e.tensor_scalar(
        HIN[:, :], GG[:, 0:NH], MASK[:, 0:1], None, ALU.mult))
    for r in range(1, 8):
        b.op("vector", lambda e, r=r: e.scalar_tensor_tensor(
            HIN[:, :], GG[:, r * NH:(r + 1) * NH], MASK[:, r:r + 1], HIN[:, :],
            ALU.mult, ALU.add))
    addc_ev = {}
    for h in range(NH):
        addc_ev[h] = b.op("vector", lambda e, h=h: e.tensor_scalar(
            BB1[:, h * T:(h + 1) * T], BB1[:, h * T:(h + 1) * T],
            HIN[:, h:h + 1], None, ALU.add))

    b.scope('p_attnb_fft')
    val_ev = None
    for h in range(NH):
        for tau in range(2):
            qt = TBu[:, h * T + tau * TT: h * T + (tau + 1) * TT]
            kt = BB1[:, h * T + tau * TT: h * T + (tau + 1) * TT]
            fs = [XS0[:, 0:TT], XS0[:, TT:2 * TT], XS0[:, 2 * TT:3 * TT],
                  XS1[:, 0:TT], XS1[:, TT:2 * TT]]
            banks = []
            b.wait("tensor", S["vE"], addc_ev[h])
            for (lhs, src) in ((DFTC, qt), (DFTS, qt), (DFTC, kt), (DFTS, kt)):
                bk = b.bank(); b.waite("tensor", bk.free_evt)
                g = b.inc("tensor", S["tE"], lambda e, bk=bk, lhs=lhs, src=src:
                          e.matmul(bk.ap[:, :], lhs, src, start=True, stop=True))
                banks.append((bk, g))
            for i, (bk, g) in enumerate(banks):
                b.wait("vector", S["tE"], g)
                ev = b.op("vector", lambda e, bk=bk, i=i: e.tensor_copy(
                    fs[i], bk.ap[:, :]))
                bk.free_evt = (S["vE"], ev)
            b.op("vector", lambda e: e.tensor_tensor(fs[4], fs[1], fs[2], ALU.mult))
            b.op("vector", lambda e: e.tensor_tensor(fs[2], fs[0], fs[2], ALU.mult))
            b.op("vector", lambda e: e.tensor_tensor(fs[0], fs[0], fs[3], ALU.mult))
            b.op("vector", lambda e: e.tensor_tensor(fs[3], fs[1], fs[3], ALU.mult))
            b.op("vector", lambda e: e.tensor_tensor(fs[2], fs[2], fs[3], ALU.subtract))
            pr_ev = b.op("vector", lambda e: e.tensor_tensor(fs[0], fs[0], fs[4], ALU.add))
            bk = b.bank(); b.waite("tensor", bk.free_evt)
            b.wait("tensor", S["vE"], pr_ev)
            b.run("tensor", lambda e, bk=bk: e.matmul(
                bk.ap[:, :], IDFTC, fs[2], start=True, stop=False))
            g = b.inc("tensor", S["tE"], lambda e, bk=bk: e.matmul(
                bk.ap[:, :], IDFTS, fs[0], start=False, stop=True))
            b.wait("vector", S["tE"], g)
            val_ev = b.op("vector", lambda e, bk=bk, qt=qt: e.tensor_copy(
                qt, bk.ap[:, :]))
            bk.free_evt = (S["vE"], val_ev)

    b.scope('p_attnb_wo')
    x3_ev = [None]
    for tau in range(2):
        def evac_wo(j, t2, bk, gend, tau=tau):
            b.wait("vector", S["tE"], gend)
            ev = b.op("vector", lambda e, bk=bk, j=j, tau=tau: e.tensor_tensor(
                St[:, j * T + tau * TT: j * T + (tau + 1) * TT],
                St[:, j * T + tau * TT: j * T + (tau + 1) * TT],
                bk.ap[:, :], ALU.add))
            bk.free_evt = (S["vE"], ev)
            x3_ev[0] = ev
        emit_proj(dt["wo"], lambda k, t2, tau=tau: TBu[:, k * T + tau * TT: k * T + (tau + 1) * TT],
                  evac_wo, n_k=NCH, n_j=NCH, src_evt=(S["vE"], val_ev),
                  taus=(tau,), shared_tau=False, mode="f32cast")
    x3_done = (S["vE"], x3_ev[0])
    if stop == "x3":
        dump_f32_full(St, x3_done)
        return finish()

    b.scope('p_ffn2')
    x4_done = emit_ffn()
    if stop == "x4":
        dump_f32_full(St, x4_done)
        return finish()

    b.scope('p_out')
    b.waite("sync", x4_done)
    out_ev = b.dma("sync", "out", lambda e: e.dma_start(
        out=y_out.ap().rearrange("(c p) t -> p c t", p=128),
        in_=St[:, :].rearrange("p (c t) -> p c t", c=NCH)))
    b.waite("sync", out_ev)
    return finish()


def _consts_b():
    blocks = []
    blocks.append(np.ones((128, 128), np.float32))
    blocks.append(np.eye(128, dtype=np.float32))
    def hadamard(n):
        h = np.array([[1.0]], np.float32)
        while h.shape[0] < n:
            h = np.block([[h, h], [h, -h]])
        return h
    blocks.append(hadamard(128))
    h64 = hadamard(64) / np.sqrt(8192.0)
    bd = np.zeros((128, 128), np.float32)
    bd[:64, :64] = h64; bd[64:, 64:] = h64
    blocks.append(bd)
    j = np.arange(128)
    ang = 2.0 * np.pi * np.outer(j, j) / 128.0
    C = np.cos(ang).astype(np.float32); Sm = np.sin(ang).astype(np.float32)
    blocks.append(C); blocks.append(Sm)
    blocks.append(C / 128.0); blocks.append(Sm / 128.0)
    blocks.append(np.zeros((128, 128), np.float32))
    blocks.append(np.zeros((128, 128), np.float32))
    return np.concatenate(blocks, 1).astype(ml_dtypes.bfloat16)


def _hid_perm():
    j = np.arange(HID)
    return (j % 64) * 128 + 2 * (j // 128) + (j % 128) // 64


def make_inputs(full):
    g = lambda k: np.asarray(full[k])
    x = g("L0_attn_a_x")
    vkeys = [g("L0_attn_a_ln1_w"), g("L0_attn_a_ln1_b"),
             g("L0_attn_a_ln2_w"), g("L0_attn_a_ln2_b"),
             g("L0_attn_a_in_b"), g("L0_attn_a_out_b"), g("L0_attn_a_b2")]
    vcols = np.concatenate(
        [np.asarray(v, np.float32).reshape(16, 128).T for v in vkeys], 1)
    b1cols = np.ascontiguousarray(
        g("L0_attn_a_b1").astype(np.float32).reshape(64, 128).T)
    A_ = g("L0_attn_a_A"); Bm_ = g("L0_attn_a_Bm")
    Cm_ = g("L0_attn_a_Cm"); Dm_ = g("L0_attn_a_Dm")
    blocks = []
    for h in range(NH):
        blocks += [A_[h], A_[h].T, Bm_[h].T, Cm_[h].T, Dm_[h].T]
    ssm = np.ascontiguousarray(np.concatenate(blocks, 1), np.float32)
    shared = dict(
        in_w=np.ascontiguousarray(g("L0_attn_a_in_w"), np.float32),
        out_w=np.ascontiguousarray(g("L0_attn_a_out_w"), np.float32),
        w1a=np.ascontiguousarray(g("L0_attn_a_w1"), np.float32),
        w2a=np.ascontiguousarray(g("L0_attn_a_w2"), np.float32),
        ssm_mats=ssm,
        wq=np.ascontiguousarray(g("L0_attn_b_W_q").T, np.float32),
        wkv=np.ascontiguousarray(g("L0_attn_b_W_kv").T, np.float32),
        wo=np.ascontiguousarray(g("L0_attn_b_W_o").T, np.float32),
        fw1=np.ascontiguousarray(g("L0_ffn_W1").T[:, _hid_perm()], np.float32),
        fw2=np.ascontiguousarray(g("L0_ffn_W2").T, np.float32),
        fw3=np.ascontiguousarray(g("L0_ffn_W3").T[:, _hid_perm()], np.float32),
        vcolsin=np.ascontiguousarray(vcols, np.float32),
        b1colsin=b1cols, consts_b=_consts_b(),
        onescol_f=np.ones((128, 1), np.float32),
    )
    in_maps = []
    for core in range(8):
        bb, half = core // 2, core % 2
        xs = np.ascontiguousarray(x[bb, half * T:(half + 1) * T, :].T.astype(np.float32))
        maskc = np.zeros((128, 8), np.float32)
        if half == 1:
            maskc[:, core - 1] = 1.0
        m = dict(shared)
        m["x0"] = xs
        m["maskc"] = maskc
        in_maps.append(m)
    return in_maps


def kernel(**inputs):
    in_maps = make_inputs(inputs)
    nc = build_program(stop=None)
    res = run_bass_kernel_spmd(nc, in_maps, core_ids=list(range(8)))
    out = np.empty((4, 2048, 2048), np.float32)
    for core in range(8):
        bb, half = core // 2, core % 2
        out[bb, half * T:(half + 1) * T, :] = res.results[core]["y"].T
    return out


# version 18
# speedup vs baseline: 1.0286x; 1.0083x over previous
import numpy as np
import ml_dtypes
from contextlib import ExitStack

import concourse.bass as bass
import concourse.mybir as mybir
from concourse.bass_utils import run_bass_kernel_spmd

F32 = mybir.dt.float32
BF16 = mybir.dt.bfloat16
AF = mybir.ActivationFunctionType
ALU = mybir.AluOpType

D = 2048
T = 1024
TT = 512
NCH = 16
NH = 16
HD = 128
HID = 8192
HCH = 64
EPS = 1e-5
PAD = 512

ENGS = ("tensor", "vector", "scalar", "sync", "gpsimd")


class Sem:
    def __init__(self, stack, nc, name):
        self.h = stack.enter_context(nc.semaphore(name))
        self.n = 0
        self.name = name


class Bank:
    def __init__(self, ap, idx):
        self.ap = ap
        self.idx = idx
        self.free_evt = None


class Builder:
    def __init__(self, nc, stack):
        self.nc = nc
        self.stack = stack
        self.plan = {e: [] for e in ENGS}
        self.wm = {e: {} for e in ENGS}
        self.sem = {k: Sem(stack, nc, k) for k in
                    ("tE", "vE", "sE", "gE", "cc")}
        self.chans = {}
        self.cur_scope = "p0"
        self.ln_slot = [0, 0]
        self.tmp_slot = [(None, 0), (None, 0)]
        self.rb_consumed = 0

    def scope(self, name):
        self.cur_scope = name

    def em(self, eng, fn):
        self.plan[eng].append((self.cur_scope, fn))

    def wait(self, eng, sem, val):
        if val is None or val <= 0:
            return
        if self.wm[eng].get(sem.name, 0) >= val:
            return
        self.wm[eng][sem.name] = val
        self.plan[eng].append((self.cur_scope, lambda e, s=sem.h, v=val: e.wait_ge(s, v)))

    def waite(self, eng, evt):
        if evt is not None and evt[0] is not None:
            self.wait(eng, evt[0], evt[1])

    def chan(self, name):
        if name not in self.chans:
            self.chans[name] = Sem(self.stack, self.nc, "ch_" + name)
        return self.chans[name]

    def dma(self, eng, chname, fn):
        ch = self.chan(chname)
        ch.n += 16
        v = ch.n
        self.plan[eng].append((self.cur_scope, lambda e, f=fn, s=ch.h: f(e).then_inc(s, 16)))
        return (ch, v)

    ENGSEM = {"vector": "vE", "scalar": "sE", "gpsimd": "gE"}

    def op(self, eng, fn):
        sem = self.sem[self.ENGSEM[eng]]
        if sem.n > 0:
            self.wait(eng, sem, sem.n)
        sem.n += 1
        v = sem.n
        self.plan[eng].append((self.cur_scope, lambda e, f=fn, s=sem.h: f(e).then_inc(s, 1)))
        return v

    def inc(self, eng, sem, fn, k=1):
        sem.n += k
        v = sem.n
        self.plan[eng].append((self.cur_scope, lambda e, f=fn, s=sem.h, kk=k: f(e).then_inc(s, kk)))
        return v

    def run(self, eng, fn):
        self.plan[eng].append((self.cur_scope, fn))

    def flush(self, block):
        nc = self.nc
        for eng in ENGS:
            lst = self.plan[eng]
            if not lst:
                continue
            def body(e, lst=lst):
                cur = None
                cur_id = None
                for sc, fn in lst:
                    if sc != cur:
                        if cur is not None:
                            nc.leave_named_scope(cur, cur_id, False)
                        if sc is not None:
                            cur_id, _ = nc.enter_named_scope(sc, False)
                        cur = sc
                    fn(e)
                if cur is not None:
                    nc.leave_named_scope(cur, cur_id, False)
            getattr(block, eng)(body)

    def init_psum(self):
        self.banks = []
        for i in range(8):
            t = self.stack.enter_context(
                self.nc.psum_tensor("ps%d" % i, [128, TT], F32))
            self.banks.append(Bank(t, i))
        self._rr = 0

    def bank(self):
        bk = self.banks[self._rr]
        self._rr = (self._rr + 1) % 8
        return bk


def build_program(stop=None):
    nc = bass.Bass("TRN2", target_bir_lowering=False)
    stack = ExitStack()
    b = Builder(nc, stack)
    S = b.sem

    dt = {}
    def din(name, shape, dtype=F32):
        dt[name] = nc.dram_tensor(name, shape, dtype, kind="ExternalInput")
        return dt[name]
    def dint(name, shape, dtype=BF16):
        dt[name] = nc.dram_tensor(name, shape, dtype)
        return dt[name]

    x0h = din("x0", [D, T])
    din("in_w", [D, D]); din("out_w", [D, D])
    din("w1a", [D, HID]); din("w2a", [HID, D])
    din("ssm_mats", [128, NH * 5 * HD])
    din("wq", [D, D]); din("wkv", [D, D]); din("wo", [D, D])
    din("fw1", [D, HID]); din("fw2", [HID, D]); din("fw3", [D, HID])
    din("vcolsin", [128, 7 * 16])
    din("b1colsin", [128, 64])
    din("maskc", [128, 8])
    din("consts_b", [128, 10 * 128], BF16)
    din("onescol_f", [128, 1])

    y_out = nc.dram_tensor("y", [D, T], F32, kind="ExternalOutput")
    dbg = nc.dram_tensor("dbg", [D, T], F32, kind="ExternalOutput")

    ssmb = dint("ssm_bf", [128, NH * 5 * HD])
    w1T = dint("w1T", [D, HID]); w3T = dint("w3T", [D, HID])
    pw_dram = dint("pw_dram", [NH * HD, 9 * 128])
    u_dram = dint("u_dram", [D, T])
    s_dram = dint("s_dram", [D, T])
    w2bf = dint("w2bf", [HID, D])
    w2t3 = dint("w2t3", [HID, D])
    w1abf = dint("w1abf", [D, HID]); w2abf = dint("w2abf", [HID, D])
    cc1_in = nc.dram_tensor("cc1_in", [128, 16], F32)
    cc1_out = nc.dram_tensor("cc1_out", [8 * 128, 16], F32)
    cc2_in = nc.dram_tensor("cc2_in", [128, 16], F32)
    cc2_out = nc.dram_tensor("cc2_out", [8 * 128, 16], F32)

    sbuf = lambda name, shape, dtype: stack.enter_context(
        nc.sbuf_tensor(name, shape, dtype))

    St = sbuf("S", [128, NCH * T], F32)
    BB1 = sbuf("BB1", [128, NCH * T], BF16)
    TBu = sbuf("TBu", [128, NCH * T], BF16)
    NB = sbuf("NB", [128, NCH * TT], BF16)
    WA = sbuf("WA", [128, 4096], BF16)
    UH = sbuf("UH", [128, 2 * T], BF16)
    SST = sbuf("SST", [128, 2 * T], BF16)
    XBs = sbuf("XBs", [128, 2 * TT], BF16)
    XQs = sbuf("XQs", [128, 2 * TT], BF16)
    TMP = sbuf("TMP", [128, 2 * TT], F32)
    XT = sbuf("XT", [128, 2 * TT], F32)
    ROWS = sbuf("ROWS", [1, 4 * TT], F32)
    RB = sbuf("RB", [1, 2 * TT], BF16)
    SSMW = sbuf("SSMW", [128, 46 * 128], BF16)
    XS0 = sbuf("XS0", [128, PAD + T], BF16)
    XS1 = sbuf("XS1", [128, PAD + T], BF16)
    VBt = sbuf("VBt", [128, T], BF16)
    CAR = sbuf("CAR", [128, NH], F32)
    CARB = sbuf("CARB", [128, NH], BF16)
    GG = sbuf("GG", [128, 8 * NH], F32)
    HIN = sbuf("HIN", [128, NH], F32)
    VCOLS = sbuf("VCOLS", [128, 7 * 16], F32)
    B1C = sbuf("B1C", [128, 64], F32)
    MASK = sbuf("MASK", [128, 8], F32)
    CONS = sbuf("CONS", [128, 10 * 128], BF16)
    ONEC = sbuf("ONEC", [128, 1], F32)
    CONSTF = sbuf("CONSTF", [128, 4], F32)

    b.init_psum()

    def cons(i):
        return CONS[:, i * 128:(i + 1) * 128]
    ONESROW_B = cons(0)[0:1, :]
    ONESCOL_B = cons(0)[:, 0:1]
    I128 = cons(1)
    H128 = cons(2)
    H64BD = cons(3)
    DFTC = cons(4)
    DFTS = cons(5)
    IDFTC = cons(6)
    IDFTS = cons(7)

    VKEYS = ["ln1w", "ln1b", "ln2w", "ln2b", "in_b", "out_b", "b2"]
    def vcol(key, c):
        i = VKEYS.index(key)
        return VCOLS[:, i * 16 + c: i * 16 + c + 1]
    def b1col(c):
        return B1C[:, c:c + 1]

    x0_lds = []
    for xp in range(4):
        x0_lds.append(b.dma("sync", "x0_%d" % xp, lambda e, xp=xp: e.dma_start(
            out=St[:, xp * 4 * T:(xp + 1) * 4 * T].rearrange("p (c t) -> p c t", c=4),
            in_=x0h[xp * 512:(xp + 1) * 512, :].rearrange("(c p) t -> p c t", p=128))))
    x0_ld = x0_lds[-1]
    b.dma("sync", "cons", lambda e: e.dma_start(
        out=CONS[:, :], in_=dt["consts_b"].ap()))
    b.dma("sync", "cons", lambda e: e.dma_start(
        out=MASK[:, :], in_=dt["maskc"].ap()))
    b.dma("sync", "cons", lambda e: e.dma_start(
        out=ONEC[:, :], in_=dt["onescol_f"].ap()))
    b.dma("sync", "cons", lambda e: e.dma_start(
        out=VCOLS[:, :], in_=dt["vcolsin"].ap()))
    cons_grp = b.dma("sync", "cons", lambda e: e.dma_start(
        out=B1C[:, :], in_=dt["b1colsin"].ap()))
    b.op("vector", lambda e: e.memset(CONSTF[:, 0:1], float(EPS)))
    b.op("vector", lambda e: e.memset(CONSTF[:, 1:2], 0.0))
    constf_ev = b.op("vector", lambda e: e.memset(CONSTF[:, 2:3], 1.0))
    EPSC = CONSTF[0:1, 0:1]

    cast_evts = {}
    cast_piece_evts = {}
    cast_queue = []
    cast_ch_state = {"i": 0, "last": [None] * 6}
    def cast_chunked(name, src, dst, rows, piece):
        for r0 in range(0, rows, piece):
            r1 = min(rows, r0 + piece)
            cast_queue.append((name, src, dst, r0, r1))

    def cast_pump(n=1, pace=True):
        for _ in range(n):
            if not cast_queue:
                return
            name, srct, dstt, r0, r1 = cast_queue.pop(0)
            if pace:
                b.wait("gpsimd", S["tE"], S["tE"].n)
            ci = cast_ch_state["i"]; cast_ch_state["i"] = (ci + 1) % 6
            b.waite("gpsimd", cast_ch_state["last"][ci])
            ev = b.dma("gpsimd", "cast%d" % ci,
                       lambda e, r0=r0, r1=r1, srct=srct, dstt=dstt:
                       e.dma_start(out=dstt[r0:r1, :], in_=srct[r0:r1, :]))
            cast_ch_state["last"][ci] = ev
            cast_evts[name] = ev
            cast_piece_evts.setdefault(name, []).append(ev)

    def cast_flush(names):
        while any(q[0] in names for q in cast_queue):
            cast_pump(1)

    cast_chunked("ssm", dt["ssm_mats"], ssmb, 128, 128)
    cast_pump(1, pace=False)
    for _name, _s, _d, _r, _p in (
            ("w1a", dt["w1a"], w1abf, D, 256),
            ("w2a", dt["w2a"], w2abf, HID, 256),
            ("w1", dt["fw1"], w1T, D, 256),
            ("w3", dt["fw3"], w3T, D, 256),
            ("w2", dt["fw2"], w2bf, HID, 256)):
        cast_chunked(_name, _s, _d, _r, _p)

    def emit_ln(src_ch, dst_ch, w_key=None, src_f32=True, src_gate=None,
                dst_gate=None):
        stats_g = [None] * NCH
        bm = b.bank(); bs = b.bank()
        b.waite("tensor", bm.free_evt); b.waite("tensor", bs.free_evt)
        for c in range(NCH):
            sl = c % 2
            xb_ap = XBs[:, sl * TT:(sl + 1) * TT]
            xq_ap = XQs[:, sl * TT:(sl + 1) * TT]
            if c >= 2:
                b.wait("vector", S["tE"], stats_g[c - 2])
            else:
                b.wait("vector", S["tE"], b.ln_slot[sl])
            if src_gate is not None:
                b.waite("vector", src_gate(c))
            if src_f32:
                b.op("vector", lambda e, c=c, xb=xb_ap: e.tensor_copy(xb, src_ch(c)))
                ev = b.op("vector", lambda e, xb=xb_ap, xq=xq_ap:
                           e.tensor_tensor(xq, xb, xb, ALU.mult))
                stats_src = xb_ap
            else:
                ev = b.op("vector", lambda e, c=c, xq=xq_ap:
                           e.tensor_tensor(xq, src_ch(c), src_ch(c), ALU.mult))
                stats_src = src_ch(c)
            b.wait("tensor", S["vE"], ev)
            first = (c == 0); last = (c == NCH - 1)
            b.run("tensor", lambda e, s=stats_src, bm=bm, f=first, l=last:
                  e.matmul(bm.ap[0:1, :], ONESCOL_B, s, start=f, stop=l))
            stats_g[c] = b.inc("tensor", S["tE"],
                               lambda e, xq=xq_ap, bs=bs, f=first, l=last:
                               e.matmul(bs.ap[0:1, :], ONESCOL_B, xq, start=f, stop=l))
        b.ln_slot[0] = stats_g[NCH - 2]
        b.ln_slot[1] = stats_g[NCH - 1]
        g_stats = stats_g[NCH - 1]

        m_row = ROWS[0:1, 0:TT]; r_row = ROWS[0:1, TT:2 * TT]
        q_row = ROWS[0:1, 2 * TT:3 * TT]; s_row = ROWS[0:1, 3 * TT:4 * TT]
        b.wait("vector", S["tE"], g_stats)
        b.op("vector", lambda e, bm=bm: e.tensor_scalar_mul(m_row, bm.ap[0:1, 0:TT], 1.0 / D))
        b.op("vector", lambda e, bs=bs: e.tensor_scalar_mul(s_row, bs.ap[0:1, 0:TT], 1.0 / D))
        b.op("vector", lambda e: e.tensor_tensor(q_row, m_row, m_row, ALU.mult))
        ev = b.op("vector", lambda e: e.tensor_tensor(s_row, s_row, q_row, ALU.subtract))
        bm.free_evt = (S["vE"], ev); bs.free_evt = (S["vE"], ev)
        b.wait("scalar", S["vE"], max(ev, constf_ev))
        ev = b.op("scalar", lambda e: e.activation(
            s_row, s_row, AF.Sqrt, bias=EPSC))
        b.wait("vector", S["sE"], ev)
        b.op("vector", lambda e: e.reciprocal(r_row, s_row))
        b.op("vector", lambda e: e.tensor_tensor(q_row, m_row, r_row, ALU.mult))
        rb_row = RB[0:1, 0:TT]; qb_row = RB[0:1, TT:2 * TT]
        b.wait("vector", S["tE"], b.rb_consumed)
        b.op("vector", lambda e: e.tensor_copy(rb_row, r_row))
        ev = b.op("vector", lambda e: e.tensor_copy(qb_row, q_row))

        br = b.bank(); bq = b.bank()
        b.waite("tensor", br.free_evt); b.waite("tensor", bq.free_evt)
        b.wait("tensor", S["vE"], ev)
        b.run("tensor", lambda e, br=br: e.matmul(br.ap[:, :], ONESROW_B, rb_row,
                                                  start=True, stop=True))
        g_bc = b.inc("tensor", S["tE"], lambda e, bq=bq: e.matmul(
            bq.ap[:, :], ONESROW_B, qb_row, start=True, stop=True))
        b.rb_consumed = g_bc
        b.wait("vector", S["tE"], g_bc)
        if dst_gate is not None:
            b.waite("vector", dst_gate)
            if w_key is not None:
                b.waite("scalar", dst_gate)
        last_ev = None
        for c in range(NCH):
            sl = c % 2
            t1 = TMP[:, sl * TT:(sl + 1) * TT]
            b.waite("vector", b.tmp_slot[sl])
            b.op("vector", lambda e, c=c, t1=t1, br=br:
                  e.tensor_tensor(t1, src_ch(c), br.ap[:, :], ALU.mult))
            if w_key is None:
                ev2 = b.op("vector", lambda e, c=c, t1=t1, bq=bq:
                            e.tensor_tensor(dst_ch(c), t1, bq.ap[:, :], ALU.subtract))
                b.tmp_slot[sl] = (S["vE"], ev2)
                last = (S["vE"], ev2)
            else:
                ev2 = b.op("vector", lambda e, c=c, t1=t1, bq=bq:
                            e.tensor_tensor(t1, t1, bq.ap[:, :], ALU.subtract))
                b.wait("scalar", S["vE"], ev2)
                ev3 = b.op("scalar", lambda e, c=c, t1=t1:
                            e.activation(dst_ch(c), t1, AF.Identity,
                                         bias=vcol(w_key[1], c), scale=vcol(w_key[0], c)))
                b.tmp_slot[sl] = (S["sE"], ev3)
                last = (S["sE"], ev3)
            last_ev = last
        br.free_evt = last_ev if w_key is None else (S["vE"], ev2)
        bq.free_evt = br.free_evt
        return last_ev

    stopped = [False]

    def dump_bf16_full(buf, done_evt):
        b.waite("vector", done_evt)
        st_ev = [None, None]
        for c in range(NCH):
            for tau in range(2):
                sl = (2 * c + tau) % 2
                xt = XT[:, sl * TT:(sl + 1) * TT]
                b.waite("vector", st_ev[sl])
                ev = b.op("vector", lambda e, c=c, tau=tau, xt=xt:
                           e.tensor_copy(xt, buf[:, c * T + tau * TT: c * T + (tau + 1) * TT]))
                b.wait("scalar", S["vE"], ev)
                st_ev[sl] = b.dma("scalar", "dmp%d" % sl, lambda e, c=c, tau=tau, xt=xt:
                              e.dma_start(out=dbg[c * 128:(c + 1) * 128,
                                                  tau * TT:(tau + 1) * TT], in_=xt))
        b.waite("scalar", st_ev[0])
        b.waite("scalar", st_ev[1])
        stopped[0] = True

    def dump_f32_full(buf, done_evt):
        b.waite("sync", done_evt)
        ev = b.dma("sync", "out", lambda e: e.dma_start(
            out=dbg.ap().rearrange("(c p) t -> p c t", p=128),
            in_=buf[:, :].rearrange("p (c t) -> p c t", c=NCH)))
        b.waite("sync", ev)
        stopped[0] = True

    def finish():
        with nc.Block() as block:
            b.flush(block)
        stack.close()
        return nc

    b.scope('p1_ln0')
    b.waite("tensor", cons_grp)
    for tau in range(2):
        h1_done = emit_ln(
            src_ch=lambda c, tau=tau: St[:, c * T + tau * TT: c * T + (tau + 1) * TT],
            dst_ch=lambda c, tau=tau: BB1[:, c * T + tau * TT: c * T + (tau + 1) * TT],
            src_f32=True, src_gate=lambda c: x0_lds[c // 4])
    if stop == "h1":
        dump_bf16_full(BB1, h1_done)
        return finish()

    b.scope('p2_ln1')
    for tau in range(2):
        t_done = emit_ln(
            src_ch=lambda c, tau=tau: BB1[:, c * T + tau * TT: c * T + (tau + 1) * TT],
            dst_ch=lambda c, tau=tau: TBu[:, c * T + tau * TT: c * T + (tau + 1) * TT],
            w_key=("ln1w", "ln1b"), src_f32=False)
    if stop == "t":
        dump_bf16_full(TBu, t_done)
        return finish()

    proj_state = {"wa_evt": [None, None], "ld_i": 0}

    def emit_proj(w_hbm, src_ch, evac, n_k=NCH, n_j=NCH, src_evt=None,
                  taus=(0, 1), shared_tau=True, mode="f32", w_ready=None):
        ps = proj_state
        if isinstance(w_ready, str):
            for pe_ in cast_piece_evts[w_ready]:
                b.waite("sync", pe_)
        elif w_ready is not None:
            b.waite("sync", w_ready)
        jg_size = 2 if shared_tau else 4
        tau_groups = [list(taus)] if shared_tau else [[t] for t in taus]
        gend = None
        for tg in tau_groups:
            for jg in range(n_j // jg_size):
                cast_pump(1)
                banks = {}
                for jj in range(jg_size):
                    for tau in tg:
                        bk = b.bank()
                        b.waite("tensor", bk.free_evt)
                        banks[(jj, tau)] = bk
                if src_evt is not None:
                    b.waite("tensor", src_evt)
                jw = jg_size * 128
                if mode in ("bf16", "f32cast"):
                    assert n_k % 4 == 0 and jw in (256, 512)
                    eng_q = "sync" if mode == "bf16" else "gpsimd"
                    chp = "wa" if mode == "bf16" else "wacast"
                    for k4 in range(n_k // 4):
                        sl = ps["ld_i"] % 2
                        ps["ld_i"] += 1
                        wa = WA[:, sl * 2048: sl * 2048 + 4 * jw]
                        b.waite(eng_q, ps["wa_evt"][sl])
                        ld = b.dma(eng_q, chp + "%d" % sl,
                                   lambda e, k4=k4, jg=jg, wa=wa, jw=jw:
                                   e.dma_start(
                                       out=wa.rearrange("p (k c) -> p k c", k=4),
                                       in_=w_hbm[k4 * 512:(k4 + 1) * 512,
                                                 jg * jw:(jg + 1) * jw]
                                       .rearrange("(k p) c -> p k c", p=128)))
                        b.waite("tensor", ld)
                        gk = None
                        for dk in range(4):
                            k = k4 * 4 + dk
                            first = (k == 0); last = (k == n_k - 1)
                            for jj in range(jg_size):
                                for tau in tg:
                                    bk = banks[(jj, tau)]
                                    fn = (lambda e, bk=bk, wa=wa, jj=jj, k=k, dk=dk,
                                          tau=tau, f=first, l=last:
                                          e.matmul(bk.ap[:, :],
                                                   wa[:, dk * jw + jj * 128: dk * jw + (jj + 1) * 128],
                                                   src_ch(k, tau), start=f, stop=l))
                                    if dk == 3 and jj == jg_size - 1 and tau == tg[-1]:
                                        gk = b.inc("tensor", S["tE"], fn)
                                    else:
                                        b.run("tensor", fn)
                        ps["wa_evt"][sl] = (S["tE"], gk)
                else:
                    for k in range(n_k):
                        sl = ps["ld_i"] % 2
                        ps["ld_i"] += 1
                        wa = WA[:, sl * 2048: sl * 2048 + jw]
                        b.waite("sync", ps["wa_evt"][sl])
                        ld = b.dma("sync", "wa%d" % sl, lambda e, k=k, jg=jg, wa=wa, jw=jw:
                                   e.dma_start(out=wa,
                                               in_=w_hbm[jg * jw:(jg + 1) * jw,
                                                         k * 128:(k + 1) * 128],
                                               transpose=True))
                        b.waite("tensor", ld)
                        first = (k == 0); last = (k == n_k - 1)
                        gk = None
                        for jj in range(jg_size):
                            for tau in tg:
                                bk = banks[(jj, tau)]
                                fn = (lambda e, bk=bk, wa=wa, jj=jj, k=k, tau=tau,
                                      f=first, l=last:
                                      e.matmul(bk.ap[:, :], wa[:, jj * 128:(jj + 1) * 128],
                                               src_ch(k, tau), start=f, stop=l))
                                if jj == jg_size - 1 and tau == tg[-1]:
                                    gk = b.inc("tensor", S["tE"], fn)
                                else:
                                    b.run("tensor", fn)
                        ps["wa_evt"][sl] = (S["tE"], gk)
                gend = gk
                for jj in range(jg_size):
                    for tau in tg:
                        evac(jg * jg_size + jj, tau, banks[(jj, tau)], gend)
        return gend

    b.scope('p3_inproj')
    u_st = [None]
    def evac_inproj(j, tau, bk, gend):
        sl = j % 2
        dst = UH[:, sl * T + tau * TT: sl * T + (tau + 1) * TT]
        b.wait("scalar", S["tE"], gend)
        b.waite("scalar", u_st[0])
        ev = b.op("scalar", lambda e, bk=bk, dst=dst, j=j:
                   e.activation(dst, bk.ap[:, :], AF.Identity, bias=vcol("in_b", j)))
        bk.free_evt = (S["sE"], ev)
        if tau == 1:
            b.wait("scalar", S["sE"], ev)
            u_st[0] = b.dma("scalar", "ust", lambda e, j=j, sl=sl:
                            e.dma_start(out=u_dram[j * 128:(j + 1) * 128, :],
                                        in_=UH[:, sl * T:(sl + 1) * T]))

    emit_proj(dt["in_w"],
              lambda k, tau: TBu[:, k * T + tau * TT: k * T + (tau + 1) * TT],
              evac_inproj, src_evt=t_done, mode="f32cast")
    if stop == "u":
        b.waite("sync", u_st[0])
        ld = b.dma("sync", "out", lambda e: e.dma_start(
            out=BB1[:, :].rearrange("p (c t) -> p c t", c=NCH),
            in_=u_dram.ap().rearrange("(c p) t -> p c t", p=128)))
        b.waite("vector", ld)
        dump_bf16_full(BB1, None)
        return finish()

    b.scope('p4_scan')
    def pwp(par, i):
        off = 23 * par
        return SSMW[:, (off + i) * 128:(off + i + 1) * 128]

    ssmw_used = [0, 0]
    pw_store_ref = [None]
    xs_store = [None]

    b.op("vector", lambda e: e.memset(XS0[:, 0:PAD], 0.0))
    pad_ev = b.op("vector", lambda e: e.memset(XS1[:, 0:PAD], 0.0))
    b.wait("tensor", S["vE"], pad_ev)

    def ssm_head_loads(h, which):
        par = h % 2
        b.wait("scalar", S["tE"], ssmw_used[par])
        b.waite("scalar", cast_evts["ssm"])
        chn = "ssmw%d" % par
        off = 23 * par
        evs = []
        evs.append(b.dma("scalar", chn, lambda e, h=h, off=off: e.dma_start(
            out=SSMW[:, off * 128:(off + 5) * 128],
            in_=ssmb[:, h * 640:(h + 1) * 640])))
        if which == "corr":
            evs.append(b.dma("scalar", chn, lambda e, h=h, off=off: e.dma_start(
                out=SSMW[:, (off + 5) * 128:(off + 14) * 128],
                in_=pw_dram[h * 128:(h + 1) * 128, :])))
        return evs[-1]

    def ssm_powers_stepper(h, ld_ev):
        par = h % 2
        pw = lambda i: pwp(par, i)
        st = {"k": 1, "prev_ev": None}

        def step(n=1):
            for _ in range(n):
                k = st["k"]
                if k > 9:
                    return st["prev_ev"]
                st["k"] += 1
                if k == 1:
                    b.waite("tensor", ld_ev)
                    if pw_store_ref[0] is not None:
                        b.waite("vector", pw_store_ref[0][par])
                bka = b.bank(); bkb = b.bank()
                b.waite("tensor", bka.free_evt); b.waite("tensor", bkb.free_evt)
                if st["prev_ev"] is not None:
                    b.wait("tensor", S["vE"], st["prev_ev"])
                qp = pw(0) if k == 1 else pw(13 + k - 1)
                pp = pw(1) if k == 1 else pw(4 + k - 1)
                b.run("tensor", lambda e, bka=bka, qp=qp, pp=pp: e.matmul(
                    bka.ap[:, 0:128], qp, pp, start=True, stop=True))
                g = b.inc("tensor", S["tE"], lambda e, bkb=bkb, qp=qp, pp=pp: e.matmul(
                    bkb.ap[:, 0:128], pp, qp, start=True, stop=True))
                b.wait("vector", S["tE"], g)
                b.op("vector", lambda e, bka=bka, k=k, pw=pw: e.tensor_copy(pw(4 + k), bka.ap[:, 0:128]))
                st["prev_ev"] = b.op("vector", lambda e, bkb=bkb, k=k, pw=pw:
                                e.tensor_copy(pw(13 + k), bkb.ap[:, 0:128]))
                bka.free_evt = (S["vE"], st["prev_ev"]); bkb.free_evt = (S["vE"], st["prev_ev"])
            return st["prev_ev"]
        return step

    def ssm_powers(h, ld_ev):
        return ssm_powers_stepper(h, ld_ev)(9)

    def load_uh(h):
        sl = h % 2
        b.wait("scalar", S["tE"], ssmw_used[sl])
        b.waite("scalar", u_st[0])
        ev = b.dma("scalar", "uh%d" % sl, lambda e, h=h, sl=sl: e.dma_start(
            out=UH[:, sl * T:(sl + 1) * T], in_=u_dram[h * 128:(h + 1) * 128, :]))
        return sl, ev

    pw_store = [None, None]
    pw_store_ref[0] = pw_store
    ldh = {0: ssm_head_loads(0, "scan")}
    uhd = {0: load_uh(0)}
    pwev = {0: ssm_powers(0, ldh[0])}
    b.wait("scalar", S["vE"], pwev[0])
    pw_store[0] = b.dma("scalar", "pwst0", lambda e: e.dma_start(
        out=pw_dram[0:128, :], in_=SSMW[:, 5 * 128:14 * 128]))
    for h in range(NH):
        par = h % 2
        pw = lambda i, par=par: pwp(par, i)
        usl, u_ev = uhd[h]
        ev_v = ev_s = None
        for tau in range(2):
            bk = b.bank(); b.waite("tensor", bk.free_evt)
            b.waite("tensor", u_ev)
            b.waite("tensor", ldh[h])
            g = b.inc("tensor", S["tE"], lambda e, bk=bk, usl=usl, tau=tau, pw=pw: e.matmul(
                bk.ap[:, :], pw(2), UH[:, usl * T + tau * TT: usl * T + (tau + 1) * TT],
                start=True, stop=True))
            eng = "vector" if tau == 0 else "scalar"
            sem = S["vE"] if tau == 0 else S["sE"]
            b.wait(eng, S["tE"], g)
            b.waite(eng, xs_store[0])
            if tau == 0:
                ev_v = b.op(eng, lambda e, bk=bk, tau=tau: e.tensor_copy(
                    XS0[:, PAD + tau * TT: PAD + (tau + 1) * TT], bk.ap[:, :]))
                bk.free_evt = (S["vE"], ev_v)
            else:
                ev_s = b.op(eng, lambda e, bk=bk, tau=tau: e.activation(
                    XS0[:, PAD + tau * TT: PAD + (tau + 1) * TT], bk.ap[:, :], AF.Identity))
                bk.free_evt = (S["sE"], ev_s)
        pstep = None
        if h + 1 < NH:
            ldh[h + 1] = ssm_head_loads(h + 1, "scan")
            uhd[h + 1] = load_uh(h + 1)
            pstep = ssm_powers_stepper(h + 1, ldh[h + 1])
        cast_pump(2)
        PSCHED = [0, 0, 1, 1, 1, 1, 2, 2, 1, 0]
        pw_ev_last = None
        xsrc, xdst = XS0, XS1
        for k in range(10):
            sh = 1 << k
            nev_v = nev_s = None
            for tau in range(2):
                bk = b.bank(); b.waite("tensor", bk.free_evt)
                b.wait("tensor", S["vE"], ev_v)
                b.wait("tensor", S["sE"], ev_s)
                b.run("tensor", lambda e, bk=bk, xsrc=xsrc, tau=tau: e.matmul(
                    bk.ap[:, :], I128,
                    xsrc[:, PAD + tau * TT: PAD + (tau + 1) * TT], start=True, stop=False))
                g = b.inc("tensor", S["tE"], lambda e, bk=bk, xsrc=xsrc, tau=tau, k=k, sh=sh, pw=pw:
                          e.matmul(bk.ap[:, :], pw(1) if k == 0 else pw(4 + k),
                                   xsrc[:, PAD + tau * TT - sh: PAD + (tau + 1) * TT - sh],
                                   start=False, stop=True))
                if tau == 0:
                    b.wait("vector", S["tE"], g)
                    nev_v = b.op("vector", lambda e, bk=bk, xdst=xdst, tau=tau:
                               e.tensor_copy(xdst[:, PAD + tau * TT: PAD + (tau + 1) * TT],
                                             bk.ap[:, :]))
                    bk.free_evt = (S["vE"], nev_v)
                else:
                    b.wait("scalar", S["tE"], g)
                    nev_s = b.op("scalar", lambda e, bk=bk, xdst=xdst, tau=tau:
                               e.activation(xdst[:, PAD + tau * TT: PAD + (tau + 1) * TT],
                                            bk.ap[:, :], AF.Identity))
                    bk.free_evt = (S["sE"], nev_s)
            ev_v, ev_s = nev_v, nev_s
            xsrc, xdst = xdst, xsrc
            if pstep is not None:
                pw_ev_last = pstep(PSCHED[k])
        if h + 1 < NH:
            pwev[h + 1] = pw_ev_last
            b.wait("scalar", S["vE"], pwev[h + 1])
            pw_store[(h + 1) % 2] = b.dma("scalar", "pwst%d" % ((h + 1) % 2),
                lambda e, h1=h + 1: e.dma_start(
                    out=pw_dram[h1 * 128:(h1 + 1) * 128, :],
                    in_=SSMW[:, (23 * ((h1) % 2) + 5) * 128:(23 * ((h1) % 2) + 14) * 128]))
        ssmw_used[par] = S["tE"].n
        b.wait("vector", S["sE"], ev_s)
        car_ev = b.op("vector", lambda e, h=h: e.tensor_copy(
            CAR[:, h:h + 1], XS0[:, PAD + T - 1: PAD + T]))
        b.wait("scalar", S["vE"], car_ev)
        xs_store[0] = b.dma("scalar", "xst", lambda e, h=h: e.dma_start(
            out=s_dram[h * 128:(h + 1) * 128, :], in_=XS0[:, PAD:PAD + T]))
    if stop == "states":
        b.waite("sync", xs_store[0])
        ld = b.dma("sync", "out", lambda e: e.dma_start(
            out=BB1[:, :].rearrange("p (c t) -> p c t", c=NCH),
            in_=s_dram.ap().rearrange("(c p) t -> p c t", p=128)))
        b.waite("vector", ld)
        dump_bf16_full(BB1, None)
        return finish()

    b.scope('p4b_cc')
    b.wait("gpsimd", S["vE"], car_ev)
    b.wait("sync", S["vE"], car_ev)
    cin_ev = b.dma("sync", "ccio", lambda e: e.dma_start(
        out=cc1_in.ap(), in_=CAR[:, :]))
    b.waite("gpsimd", cin_ev)
    cc_ev = b.inc("gpsimd", S["cc"], lambda e: e.collective_compute(
        "AllGather", ALU.bypass, replica_groups=[list(range(8))],
        ins=[cc1_in.ap().opt()], outs=[cc1_out.ap().opt()]))
    b.wait("gpsimd", S["cc"], cc_ev)
    b.wait("sync", S["cc"], cc_ev)
    gg_ev = b.dma("sync", "ccio", lambda e: e.dma_start(
        out=GG[:, :].rearrange("p (r h) -> p r h", r=8),
        in_=cc1_out.ap().rearrange("(r i) h -> i r h", i=128)))
    b.waite("vector", gg_ev)
    b.waite("vector", cons_grp)
    b.op("vector", lambda e: e.tensor_scalar(
        HIN[:, :], GG[:, 0:NH], MASK[:, 0:1], None, ALU.mult))
    for r in range(1, 8):
        b.op("vector", lambda e, r=r: e.scalar_tensor_tensor(
            HIN[:, :], GG[:, r * NH:(r + 1) * NH], MASK[:, r:r + 1], HIN[:, :],
            ALU.mult, ALU.add))
    hinb_ev = b.op("vector", lambda e: e.tensor_copy(CARB[:, :], HIN[:, :]))

    b.scope('p4c_corr')
    sst_used = [0]
    VB2 = [VBt[:, 0:T], XQs[:, 0:2 * TT]]
    _bar4c = (S["tE"], S["tE"].n)
    vb_used = [_bar4c, _bar4c]

    def emit_vchain(h):
        par = h % 2
        pw = lambda i: pwp(par, i)
        vb = VB2[par]
        bk = b.bank(); b.waite("tensor", bk.free_evt)
        b.waite("tensor", ldh[h])
        b.wait("tensor", S["vE"], hinb_ev)
        g = b.inc("tensor", S["tE"], lambda e, bk=bk, h=h, pw=pw: e.matmul(
            bk.ap[:, 0:1], pw(1), CARB[:, h:h + 1], start=True, stop=True))
        b.wait("vector", S["tE"], g)
        b.waite("vector", vb_used[par])
        vev = b.op("vector", lambda e, bk=bk, vb=vb: e.tensor_copy(
            vb[:, 0:1], bk.ap[:, 0:1]))
        bk.free_evt = (S["vE"], vev)
        for k in range(10):
            w = 1 << k
            bk = b.bank(); b.waite("tensor", bk.free_evt)
            b.wait("tensor", S["vE"], vev)
            g = b.inc("tensor", S["tE"], lambda e, bk=bk, k=k, w=w, pw=pw, vb=vb: e.matmul(
                bk.ap[:, 0:w], pw(1) if k == 0 else pw(4 + k), vb[:, 0:w],
                start=True, stop=True))
            b.wait("vector", S["tE"], g)
            vev = b.op("vector", lambda e, bk=bk, w=w, vb=vb: e.tensor_copy(
                vb[:, w:2 * w], bk.ap[:, 0:w]))
            bk.free_evt = (S["vE"], vev)
        return vev

    b.waite("scalar", pw_store[0]); b.waite("scalar", pw_store[1])
    ldh = {0: ssm_head_loads(0, "corr")}
    uhd = {0: load_uh(0)}
    vch = {0: emit_vchain(0)}
    std = {}
    for h in range(NH):
        par = h % 2
        pw = lambda i, par=par: pwp(par, i)
        if h == 0:
            sl = 0
            b.wait("scalar", S["tE"], sst_used[0])
            b.waite("scalar", xs_store[0])
            std[0] = b.dma("scalar", "sst0", lambda e: e.dma_start(
                out=SST[:, 0:T], in_=s_dram[0:128, :]))
        st_ld = std[h]
        usl, u_ev = uhd[h]
        sl = h % 2
        if h + 1 < NH:
            ldh[h + 1] = ssm_head_loads(h + 1, "corr")
            uhd[h + 1] = load_uh(h + 1)
            sl2 = (h + 1) % 2
            b.wait("scalar", S["tE"], sst_used[0])
            std[h + 1] = b.dma("scalar", "sst%d" % sl2, lambda e, h=h, sl2=sl2: e.dma_start(
                out=SST[:, sl2 * T:(sl2 + 1) * T],
                in_=s_dram[(h + 1) * 128:(h + 2) * 128, :]))
            vch[h + 1] = emit_vchain(h + 1)
        cast_pump(2)
        b.waite("vector", st_ld)
        b.wait("vector", S["vE"], vch[h])
        add_ev = b.op("vector", lambda e, sl=sl: e.tensor_tensor(
            SST[:, sl * T:(sl + 1) * T], SST[:, sl * T:(sl + 1) * T],
            VB2[sl][:, 0:T], ALU.add))
        vb_used[sl] = (S["vE"], add_ev)
        for tau in range(2):
            bk = b.bank(); b.waite("tensor", bk.free_evt)
            b.wait("tensor", S["vE"], add_ev)
            b.waite("tensor", u_ev)
            b.run("tensor", lambda e, bk=bk, sl=sl, tau=tau, pw=pw: e.matmul(
                bk.ap[:, :], pw(3), SST[:, sl * T + tau * TT: sl * T + (tau + 1) * TT],
                start=True, stop=False))
            g = b.inc("tensor", S["tE"], lambda e, bk=bk, usl=usl, tau=tau, pw=pw: e.matmul(
                bk.ap[:, :], pw(4), UH[:, usl * T + tau * TT: usl * T + (tau + 1) * TT],
                start=False, stop=True))
            if tau == 0:
                b.wait("vector", S["tE"], g)
                yev = b.op("vector", lambda e, bk=bk, h=h, tau=tau: e.tensor_copy(
                    St[:, h * T + tau * TT: h * T + (tau + 1) * TT], bk.ap[:, :]))
                bk.free_evt = (S["vE"], yev)
            else:
                b.wait("scalar", S["tE"], g)
                yev = b.op("scalar", lambda e, bk=bk, h=h, tau=tau: e.activation(
                    St[:, h * T + tau * TT: h * T + (tau + 1) * TT], bk.ap[:, :], AF.Identity))
                bk.free_evt = (S["sE"], yev)
        ssmw_used[par] = S["tE"].n
        sst_used[0] = S["tE"].n
    b.ln_slot = [S["tE"].n, S["tE"].n]
    y_done = (S["vE"], S["vE"].n)
    y_done_s = (S["sE"], S["sE"].n)
    if stop == "yss":
        dump_f32_full(St, y_done)
        return finish()

    b.scope('p5_outproj')
    z_done = None
    for tau in range(2):
        cast_ev = None
        for c in range(NCH):
            b.waite("vector", y_done)
            b.wait("vector", S["sE"], y_done_s[1])
            cast_ev = b.op("vector", lambda e, c=c, tau=tau: e.tensor_copy(
                NB[:, c * TT:(c + 1) * TT], St[:, c * T + tau * TT: c * T + (tau + 1) * TT]))
        def evac_z(j, t2, bk, gend, tau=tau):
            b.wait("vector", S["tE"], gend)
            ev = b.op("vector", lambda e, bk=bk, j=j, tau=tau:
                       e.scalar_tensor_tensor(
                           St[:, j * T + tau * TT: j * T + (tau + 1) * TT],
                           bk.ap[:, :], vcol("out_b", j),
                           BB1[:, j * T + tau * TT: j * T + (tau + 1) * TT],
                           ALU.add, ALU.add))
            bk.free_evt = (S["vE"], ev)
            return ev
        evz = []
        def evac_z_wrap(j, t2, bk, gend):
            evz.append(evac_z(j, t2, bk, gend))
        emit_proj(dt["out_w"], lambda k, t2, tau=tau: NB[:, k * TT:(k + 1) * TT],
                  evac_z_wrap, src_evt=(S["vE"], cast_ev), taus=(tau,),
                  shared_tau=False, mode="f32cast")
        z_done = (S["vE"], evz[-1])
    if stop == "z":
        dump_f32_full(St, z_done)
        return finish()

    b.scope('p68_ffnA')
    def hidtile(c):
        buf = BB1 if c < 32 else TBu
        return buf[:, (c % 32) * TT:(c % 32 + 1) * TT]

    xt_used = [None, None]

    def emit_x0_add(tau):
        ev = None
        for c in range(NCH):
            sl = c % 2
            xt = XT[:, sl * TT:(sl + 1) * TT]
            b.waite("sync", xt_used[sl])
            ld = b.dma("sync", "xt%d" % sl, lambda e, c=c, tau=tau, xt=xt:
                       e.dma_start(out=xt, in_=x0h[c * 128:(c + 1) * 128,
                                                   tau * TT:(tau + 1) * TT]))
            b.waite("vector", ld)
            ev = b.op("vector", lambda e, c=c, tau=tau, xt=xt: e.tensor_tensor(
                St[:, c * T + tau * TT: c * T + (tau + 1) * TT],
                St[:, c * T + tau * TT: c * T + (tau + 1) * TT], xt, ALU.add))
            xt_used[sl] = (S["vE"], ev)
        return ev

    ln2_evs = {}
    ln2_evs[0] = emit_ln(
        src_ch=lambda c: St[:, c * T: c * T + TT],
        dst_ch=lambda c: NB[:, c * TT:(c + 1) * TT],
        w_key=("ln2w", "ln2b"), src_f32=True)
    emit_x0_add(0)
    x1_ev = [None]
    for tau in range(2):
        gelu_ev = {}
        def evac_gelu(j, t2, bk, gend):
            b.wait("scalar", S["tE"], gend)
            ev = b.op("scalar", lambda e, bk=bk, j=j: e.activation(
                hidtile(j), bk.ap[:, :], AF.Gelu, bias=b1col(j)))
            bk.free_evt = (S["sE"], ev)
            gelu_ev[j] = ev
        w1a_gend = emit_proj(w1abf, lambda k, t2: NB[:, k * TT:(k + 1) * TT],
                  evac_gelu, n_k=NCH, n_j=HCH, src_evt=ln2_evs[tau],
                  taus=(tau,), shared_tau=False, mode="bf16", w_ready="w1a")
        if tau == 0:
            ln2_evs[1] = emit_ln(
                src_ch=lambda c: St[:, c * T + TT: c * T + 2 * TT],
                dst_ch=lambda c: NB[:, c * TT:(c + 1) * TT],
                w_key=("ln2w", "ln2b"), src_f32=True,
                dst_gate=(S["tE"], w1a_gend))
            emit_x0_add(1)
        def evac_w2a(j, t2, bk, gend, tau=tau):
            b.wait("vector", S["tE"], gend)
            ev = b.op("vector", lambda e, bk=bk, j=j, tau=tau:
                      e.scalar_tensor_tensor(
                          St[:, j * T + tau * TT: j * T + (tau + 1) * TT],
                          bk.ap[:, :], vcol("b2", j),
                          St[:, j * T + tau * TT: j * T + (tau + 1) * TT],
                          ALU.add, ALU.add))
            bk.free_evt = (S["vE"], ev)
            x1_ev[0] = ev
        emit_proj(w2abf, lambda k, t2: hidtile(k),
                  evac_w2a, n_k=HCH, n_j=NCH, src_evt=(S["sE"], gelu_ev[HCH - 1]),
                  taus=(tau,), shared_tau=False, mode="bf16", w_ready="w2a")
    x1_done = (S["vE"], x1_ev[0])
    if stop == "x1":
        dump_f32_full(St, x1_done)
        return finish()

    b.scope('p_w2prep')
    prep_in_used = [None, None, None, None]
    prep_out_used = [None, None, None, None]
    PREP_OUT = [WA[:, 0:2048], WA[:, 2048:4096], UH[:, 0:2048], SST[:, 0:2048]]

    def prep_pass():
        st_ev = None
        for i in range(HCH):
            sl = i % 4
            inslot = NB[:, sl * 2048:(sl + 1) * 2048]
            chn = "prep_in%d" % sl
            b.waite("sync", prep_in_used[sl])
            ld = b.dma("sync", chn, lambda e, i=i, inslot=inslot: e.dma_start(
                out=inslot, in_=w2bf[i * 128:(i + 1) * 128, :]))
            b.waite("tensor", ld)
            out_wa = PREP_OUT[sl]
            ev_ev = None
            for dtile in range(4):
                bk = b.bank(); b.waite("tensor", bk.free_evt)
                g = b.inc("tensor", S["tE"], lambda e, bk=bk, inslot=inslot,
                          dtile=dtile: e.matmul(
                    bk.ap[:, :], H128, inslot[:, dtile * TT:(dtile + 1) * TT],
                    start=True, stop=True))
                eng = "vector" if dtile % 2 == 0 else "scalar"
                b.wait(eng, S["tE"], g)
                if dtile == 0:
                    b.waite("vector", prep_out_used[sl])
                if dtile == 1:
                    b.waite("scalar", prep_out_used[sl])
                if eng == "vector":
                    ev_ev = b.op(eng, lambda e, bk=bk, dtile=dtile, out_wa=out_wa:
                                 e.tensor_copy(out_wa[:, dtile * TT:(dtile + 1) * TT],
                                               bk.ap[:, :]))
                    bk.free_evt = (S["vE"], ev_ev)
                else:
                    ev_s = b.op(eng, lambda e, bk=bk, dtile=dtile, out_wa=out_wa:
                                e.activation(out_wa[:, dtile * TT:(dtile + 1) * TT],
                                             bk.ap[:, :], AF.Identity))
                    bk.free_evt = (S["sE"], ev_s)
            prep_in_used[sl] = (S["tE"], S["tE"].n)
            b.wait("scalar", S["vE"], ev_ev)
            b.wait("scalar", S["sE"], ev_s)
            st = b.dma("scalar", "prep_st%d" % sl, lambda e, i=i, out_wa=out_wa:
                       e.dma_start(
                out=bass.AP(w2t3, i * D, [[128 * D, 64], [64 * D, 2], [1, D]]),
                in_=out_wa))
            prep_out_used[sl] = st
            st_ev = st
        return st_ev

    cast_flush(["w2"])
    for pe_ in cast_piece_evts["w2"]:
        b.waite("sync", pe_)
    prep_bar = (S["tE"], S["tE"].n)
    for _q in range(4):
        prep_in_used[_q] = prep_bar
    b.wait("sync", S["tE"], S["tE"].n)
    b.wait("vector", S["tE"], S["tE"].n)
    b.wait("scalar", S["tE"], S["tE"].n)
    prep_pass()
    for _q in range(4):
        b.waite("sync", prep_out_used[_q])
    p2_end = prep_out_used[3]
    b.wait("vector", S["tE"], S["tE"].n)
    b.wait("scalar", S["tE"], S["tE"].n)
    proj_state["wa_evt"] = [prep_out_used[0], prep_out_used[1]]

    def emit_ffn(x_stop=None):
        out_ev = [None]
        ln_evs = {}
        ln_evs[0] = emit_ln(
            src_ch=lambda c: St[:, c * T: c * T + TT],
            dst_ch=lambda c: NB[:, c * TT:(c + 1) * TT],
            w_key=None, src_f32=True)
        for tau in range(2):
            silu_ev = {}
            h64_ev = {}
            def evac_silu(j, t2, bk, gend):
                b.wait("scalar", S["tE"], gend)
                ev = b.op("scalar", lambda e, bk=bk, j=j: e.activation(
                    hidtile(j), bk.ap[:, :], AF.Silu))
                bk.free_evt = (S["sE"], ev)
                silu_ev[j] = ev
            emit_proj(w1T, lambda k, t2: NB[:, k * TT:(k + 1) * TT],
                      evac_silu, n_k=NCH, n_j=HCH, src_evt=ln_evs[tau],
                      taus=(tau,), shared_tau=False, mode="bf16",
                      w_ready="w1")
            def evac_lin(j, t2, bk, gend):
                b.wait("vector", S["tE"], gend)
                b.wait("vector", S["sE"], silu_ev[j])
                ev = b.op("vector", lambda e, bk=bk, j=j: e.tensor_tensor(
                    hidtile(j), hidtile(j), bk.ap[:, :], ALU.mult))
                bk.free_evt = (S["vE"], ev)
                silu_ev[j] = ev
                bh = b.bank(); b.waite("tensor", bh.free_evt)
                b.wait("tensor", S["vE"], ev)
                g2 = b.inc("tensor", S["tE"], lambda e, bh=bh, j=j: e.matmul(
                    bh.ap[:, :], H64BD, hidtile(j), start=True, stop=True))
                b.wait("scalar", S["tE"], g2)
                ev2 = b.op("scalar", lambda e, bh=bh, j=j: e.activation(
                    hidtile(j), bh.ap[:, :], AF.Identity))
                bh.free_evt = (S["sE"], ev2)
                h64_ev[j] = ev2
            w3_gend = emit_proj(w3T, lambda k, t2: NB[:, k * TT:(k + 1) * TT],
                      evac_lin, n_k=NCH, n_j=HCH, src_evt=ln_evs[tau],
                      taus=(tau,), shared_tau=False, mode="bf16",
                      w_ready="w3")
            if tau == 0:
                ln_evs[1] = emit_ln(
                    src_ch=lambda c: St[:, c * T + TT: c * T + 2 * TT],
                    dst_ch=lambda c: NB[:, c * TT:(c + 1) * TT],
                    w_key=None, src_f32=True,
                    dst_gate=(S["tE"], w3_gend))
            def evac_out(j, t2, bk, gend, tau=tau):
                b.wait("vector", S["tE"], gend)
                ev = b.op("vector", lambda e, bk=bk, j=j, tau=tau: e.tensor_tensor(
                    St[:, j * T + tau * TT: j * T + (tau + 1) * TT],
                    St[:, j * T + tau * TT: j * T + (tau + 1) * TT],
                    bk.ap[:, :], ALU.add))
                bk.free_evt = (S["vE"], ev)
                out_ev[0] = ev
            emit_proj(w2t3, lambda k, t2: hidtile(k),
                      evac_out, n_k=HCH, n_j=NCH,
                      src_evt=(S["sE"], h64_ev[HCH - 1]),
                      taus=(tau,), shared_tau=False, mode="bf16",
                      w_ready=p2_end)
        return (S["vE"], out_ev[0])

    b.scope('p_ffn1')
    x2_done = emit_ffn()
    if stop == "x2":
        dump_f32_full(St, x2_done)
        return finish()

    b.scope('p_attnb_proj')
    ones_ev = b.op("vector", lambda e: e.memset(VBt[:, :], 1.0))

    qr_ev = [None]
    for tau in range(2):
        ln_ev = emit_ln(
            src_ch=lambda c, tau=tau: St[:, c * T + tau * TT: c * T + (tau + 1) * TT],
            dst_ch=lambda c: NB[:, c * TT:(c + 1) * TT],
            w_key=None, src_f32=True)
        for (whbm, dstbuf) in ((dt["wq"], TBu), (dt["wkv"], BB1)):
            def evac_raw(j, t2, bk, gend, dstbuf=dstbuf, tau=tau):
                b.wait("vector", S["tE"], gend)
                ev = b.op("vector", lambda e, bk=bk, j=j, tau=tau, dstbuf=dstbuf:
                          e.tensor_copy(dstbuf[:, j * T + tau * TT: j * T + (tau + 1) * TT],
                                        bk.ap[:, :]))
                bk.free_evt = (S["vE"], ev)
                qr_ev[0] = ev
            emit_proj(whbm, lambda k, t2: NB[:, k * TT:(k + 1) * TT],
                      evac_raw, n_k=NCH, n_j=NCH, src_evt=ln_ev,
                      taus=(tau,), shared_tau=False, mode="f32cast")

    b.scope('p_attnb_norm')
    nrm_state = {"i": 0, "sq": [0, 0], "srow": [0, 0, 0, 0],
                 "rb": [0, 0], "xb": [0, 0]}

    def emit_normalize(buf, h, tau):
        ns = nrm_state
        i = ns["i"]; ns["i"] += 1
        tile = buf[:, h * T + tau * TT: h * T + (tau + 1) * TT]
        sq = XQs[:, (i % 2) * TT:(i % 2 + 1) * TT]
        b.wait("gpsimd", S["tE"], ns["sq"][i % 2])
        if ns.get("src_ev") is not None:
            b.wait("gpsimd", S["vE"], ns["src_ev"])
        sq_ev = b.op("gpsimd", lambda e, sq=sq, tile=tile:
                     e.tensor_tensor(sq, tile, tile, ALU.mult))
        bk = b.bank(); b.waite("tensor", bk.free_evt)
        b.wait("tensor", S["gE"], sq_ev)
        g = b.inc("tensor", S["tE"], lambda e, bk=bk, sq=sq: e.matmul(
            bk.ap[0:1, :], ONESCOL_B, sq, start=True, stop=True))
        ns["sq"][i % 2] = g
        s_row = ROWS[0:1, (i % 4) * TT:(i % 4 + 1) * TT]
        b.wait("scalar", S["tE"], g)
        b.wait("scalar", S["vE"], ns["srow"][i % 4])
        sc_ev = b.op("scalar", lambda e, bk=bk, s_row=s_row: e.activation(
            s_row, bk.ap[0:1, :], AF.Sqrt, scale=128.0))
        bk.free_evt = (S["sE"], sc_ev)
        b.wait("vector", S["sE"], sc_ev)
        b.op("vector", lambda e, s_row=s_row: e.reciprocal(s_row, s_row))
        rb_row = RB[0:1, (i % 2) * TT:(i % 2 + 1) * TT]
        b.wait("vector", S["tE"], ns["rb"][i % 2])
        rb_ev = b.op("vector", lambda e, rb_row=rb_row, s_row=s_row:
                     e.tensor_copy(rb_row, s_row))
        ns["srow"][i % 4] = rb_ev
        bc = b.bank(); b.waite("tensor", bc.free_evt)
        b.wait("tensor", S["vE"], rb_ev)
        g2 = b.inc("tensor", S["tE"], lambda e, bc=bc, rb_row=rb_row: e.matmul(
            bc.ap[:, :], ONESROW_B, rb_row, start=True, stop=True))
        ns["rb"][i % 2] = g2
        b.wait("scalar", S["tE"], g2)
        xb = XBs[:, (i % 2) * TT:(i % 2 + 1) * TT]
        b.waite("scalar", ns["xb_used"][i % 2])
        cp_ev = b.op("scalar", lambda e, bc=bc, xb=xb: e.activation(
            xb, bc.ap[:, :], AF.Identity))
        bc.free_evt = (S["sE"], cp_ev)
        b.wait("vector", S["sE"], cp_ev)
        nrm_ev = b.op("vector", lambda e, tile=tile, xb=xb:
                      e.tensor_tensor(tile, tile, xb, ALU.mult))
        ns["xb_used"][i % 2] = (S["vE"], nrm_ev)
        return nrm_ev

    nrm_state["sq"] = [b.ln_slot[0], b.ln_slot[1]]
    nrm_state["rb"] = [b.rb_consumed, b.rb_consumed]
    nrm_state["srow"] = [S["vE"].n] * 4
    nrm_state["xb_used"] = [(S["vE"], S["vE"].n), (S["vE"], S["vE"].n)]
    nrm_state["src_ev"] = qr_ev[0]
    kv_norm_ev = None
    for h in range(NH):
        for tau in range(2):
            emit_normalize(TBu, h, tau)
            kv_norm_ev = emit_normalize(BB1, h, tau)
    b.ln_slot = [max(nrm_state["sq"]), max(nrm_state["sq"])]
    b.rb_consumed = max(nrm_state["rb"])

    b.waite("vector", cin_ev)
    scan_evs = {}
    for h in range(NH):
        sc_ev = b.op("vector", lambda e, h=h: e.tensor_tensor_scan(
            BB1[:, h * T:(h + 1) * T], VBt[:, :], BB1[:, h * T:(h + 1) * T],
            0.0, ALU.mult, ALU.add))
        car_ev = b.op("vector", lambda e, h=h: e.tensor_copy(
            CAR[:, h:h + 1], BB1[:, h * T + T - 1: h * T + T]))
        scan_evs[h] = car_ev

    b.wait("gpsimd", S["vE"], scan_evs[NH - 1])
    b.wait("gpsimd", S["vE"], hinb_ev)
    b.wait("sync", S["vE"], scan_evs[NH - 1])
    cin2 = b.dma("sync", "ccio", lambda e: e.dma_start(
        out=cc2_in.ap(), in_=CAR[:, :]))
    b.waite("gpsimd", cin2)
    cc2_ev = b.inc("gpsimd", S["cc"], lambda e: e.collective_compute(
        "AllGather", ALU.bypass, replica_groups=[list(range(8))],
        ins=[cc2_in.ap().opt()], outs=[cc2_out.ap().opt()]))
    b.wait("gpsimd", S["cc"], cc2_ev)
    b.wait("sync", S["cc"], cc2_ev)
    b.wait("sync", S["vE"], hinb_ev)
    gg2 = b.dma("sync", "ccio", lambda e: e.dma_start(
        out=GG[:, :].rearrange("p (r h) -> p r h", r=8),
        in_=cc2_out.ap().rearrange("(r i) h -> i r h", i=128)))
    b.waite("vector", gg2)
    b.op("vector", lambda e: e.tensor_scalar(
        HIN[:, :], GG[:, 0:NH], MASK[:, 0:1], None, ALU.mult))
    for r in range(1, 8):
        b.op("vector", lambda e, r=r: e.scalar_tensor_tensor(
            HIN[:, :], GG[:, r * NH:(r + 1) * NH], MASK[:, r:r + 1], HIN[:, :],
            ALU.mult, ALU.add))
    addc_ev = {}
    for h in range(NH):
        addc_ev[h] = b.op("vector", lambda e, h=h: e.tensor_scalar(
            BB1[:, h * T:(h + 1) * T], BB1[:, h * T:(h + 1) * T],
            HIN[:, h:h + 1], None, ALU.add))

    b.scope('p_attnb_fft')
    val_ev = None
    for h in range(NH):
        for tau in range(2):
            qt = TBu[:, h * T + tau * TT: h * T + (tau + 1) * TT]
            kt = BB1[:, h * T + tau * TT: h * T + (tau + 1) * TT]
            fs = [XS0[:, 0:TT], XS0[:, TT:2 * TT], XS0[:, 2 * TT:3 * TT],
                  XS1[:, 0:TT], XS1[:, TT:2 * TT]]
            banks = []
            b.wait("tensor", S["vE"], addc_ev[h])
            for (lhs, src) in ((DFTC, qt), (DFTS, qt), (DFTC, kt), (DFTS, kt)):
                bk = b.bank(); b.waite("tensor", bk.free_evt)
                g = b.inc("tensor", S["tE"], lambda e, bk=bk, lhs=lhs, src=src:
                          e.matmul(bk.ap[:, :], lhs, src, start=True, stop=True))
                banks.append((bk, g))
            for i, (bk, g) in enumerate(banks):
                b.wait("vector", S["tE"], g)
                ev = b.op("vector", lambda e, bk=bk, i=i: e.tensor_copy(
                    fs[i], bk.ap[:, :]))
                bk.free_evt = (S["vE"], ev)
            b.op("vector", lambda e: e.tensor_tensor(fs[4], fs[1], fs[2], ALU.mult))
            b.op("vector", lambda e: e.tensor_tensor(fs[2], fs[0], fs[2], ALU.mult))
            b.op("vector", lambda e: e.tensor_tensor(fs[0], fs[0], fs[3], ALU.mult))
            b.op("vector", lambda e: e.tensor_tensor(fs[3], fs[1], fs[3], ALU.mult))
            b.op("vector", lambda e: e.tensor_tensor(fs[2], fs[2], fs[3], ALU.subtract))
            pr_ev = b.op("vector", lambda e: e.tensor_tensor(fs[0], fs[0], fs[4], ALU.add))
            bk = b.bank(); b.waite("tensor", bk.free_evt)
            b.wait("tensor", S["vE"], pr_ev)
            b.run("tensor", lambda e, bk=bk: e.matmul(
                bk.ap[:, :], IDFTC, fs[2], start=True, stop=False))
            g = b.inc("tensor", S["tE"], lambda e, bk=bk: e.matmul(
                bk.ap[:, :], IDFTS, fs[0], start=False, stop=True))
            b.wait("vector", S["tE"], g)
            val_ev = b.op("vector", lambda e, bk=bk, qt=qt: e.tensor_copy(
                qt, bk.ap[:, :]))
            bk.free_evt = (S["vE"], val_ev)

    b.scope('p_attnb_wo')
    x3_ev = [None]
    for tau in range(2):
        def evac_wo(j, t2, bk, gend, tau=tau):
            b.wait("vector", S["tE"], gend)
            ev = b.op("vector", lambda e, bk=bk, j=j, tau=tau: e.tensor_tensor(
                St[:, j * T + tau * TT: j * T + (tau + 1) * TT],
                St[:, j * T + tau * TT: j * T + (tau + 1) * TT],
                bk.ap[:, :], ALU.add))
            bk.free_evt = (S["vE"], ev)
            x3_ev[0] = ev
        emit_proj(dt["wo"], lambda k, t2, tau=tau: TBu[:, k * T + tau * TT: k * T + (tau + 1) * TT],
                  evac_wo, n_k=NCH, n_j=NCH, src_evt=(S["vE"], val_ev),
                  taus=(tau,), shared_tau=False, mode="f32cast")
    x3_done = (S["vE"], x3_ev[0])
    if stop == "x3":
        dump_f32_full(St, x3_done)
        return finish()

    b.scope('p_ffn2')
    x4_done = emit_ffn()
    if stop == "x4":
        dump_f32_full(St, x4_done)
        return finish()

    b.scope('p_out')
    b.waite("sync", x4_done)
    out_ev = b.dma("sync", "out", lambda e: e.dma_start(
        out=y_out.ap().rearrange("(c p) t -> p c t", p=128),
        in_=St[:, :].rearrange("p (c t) -> p c t", c=NCH)))
    b.waite("sync", out_ev)
    return finish()


def _consts_b():
    blocks = []
    blocks.append(np.ones((128, 128), np.float32))
    blocks.append(np.eye(128, dtype=np.float32))
    def hadamard(n):
        h = np.array([[1.0]], np.float32)
        while h.shape[0] < n:
            h = np.block([[h, h], [h, -h]])
        return h
    blocks.append(hadamard(128))
    h64 = hadamard(64) / np.sqrt(8192.0)
    bd = np.zeros((128, 128), np.float32)
    bd[:64, :64] = h64; bd[64:, 64:] = h64
    blocks.append(bd)
    j = np.arange(128)
    ang = 2.0 * np.pi * np.outer(j, j) / 128.0
    C = np.cos(ang).astype(np.float32); Sm = np.sin(ang).astype(np.float32)
    blocks.append(C); blocks.append(Sm)
    blocks.append(C / 128.0); blocks.append(Sm / 128.0)
    blocks.append(np.zeros((128, 128), np.float32))
    blocks.append(np.zeros((128, 128), np.float32))
    return np.concatenate(blocks, 1).astype(ml_dtypes.bfloat16)


def _hid_perm():
    j = np.arange(HID)
    return (j % 64) * 128 + 2 * (j // 128) + (j % 128) // 64


def make_inputs(full):
    g = lambda k: np.asarray(full[k])
    x = g("L0_attn_a_x")
    vkeys = [g("L0_attn_a_ln1_w"), g("L0_attn_a_ln1_b"),
             g("L0_attn_a_ln2_w"), g("L0_attn_a_ln2_b"),
             g("L0_attn_a_in_b"), g("L0_attn_a_out_b"), g("L0_attn_a_b2")]
    vcols = np.concatenate(
        [np.asarray(v, np.float32).reshape(16, 128).T for v in vkeys], 1)
    b1cols = np.ascontiguousarray(
        g("L0_attn_a_b1").astype(np.float32).reshape(64, 128).T)
    A_ = g("L0_attn_a_A"); Bm_ = g("L0_attn_a_Bm")
    Cm_ = g("L0_attn_a_Cm"); Dm_ = g("L0_attn_a_Dm")
    blocks = []
    for h in range(NH):
        blocks += [A_[h], A_[h].T, Bm_[h].T, Cm_[h].T, Dm_[h].T]
    ssm = np.ascontiguousarray(np.concatenate(blocks, 1), np.float32)
    shared = dict(
        in_w=np.ascontiguousarray(g("L0_attn_a_in_w"), np.float32),
        out_w=np.ascontiguousarray(g("L0_attn_a_out_w"), np.float32),
        w1a=np.ascontiguousarray(g("L0_attn_a_w1"), np.float32),
        w2a=np.ascontiguousarray(g("L0_attn_a_w2"), np.float32),
        ssm_mats=ssm,
        wq=np.ascontiguousarray(g("L0_attn_b_W_q").T, np.float32),
        wkv=np.ascontiguousarray(g("L0_attn_b_W_kv").T, np.float32),
        wo=np.ascontiguousarray(g("L0_attn_b_W_o").T, np.float32),
        fw1=np.ascontiguousarray(g("L0_ffn_W1").T[:, _hid_perm()], np.float32),
        fw2=np.ascontiguousarray(g("L0_ffn_W2").T, np.float32),
        fw3=np.ascontiguousarray(g("L0_ffn_W3").T[:, _hid_perm()], np.float32),
        vcolsin=np.ascontiguousarray(vcols, np.float32),
        b1colsin=b1cols, consts_b=_consts_b(),
        onescol_f=np.ones((128, 1), np.float32),
    )
    in_maps = []
    for core in range(8):
        bb, half = core // 2, core % 2
        xs = np.ascontiguousarray(x[bb, half * T:(half + 1) * T, :].T.astype(np.float32))
        maskc = np.zeros((128, 8), np.float32)
        if half == 1:
            maskc[:, core - 1] = 1.0
        m = dict(shared)
        m["x0"] = xs
        m["maskc"] = maskc
        in_maps.append(m)
    return in_maps


def kernel(**inputs):
    in_maps = make_inputs(inputs)
    nc = build_program(stop=None)
    res = run_bass_kernel_spmd(nc, in_maps, core_ids=list(range(8)))
    out = np.empty((4, 2048, 2048), np.float32)
    for core in range(8):
        bb, half = core // 2, core % 2
        out[bb, half * T:(half + 1) * T, :] = res.results[core]["y"].T
    return out


# version 19
# speedup vs baseline: 1.0393x; 1.0104x over previous
import numpy as np
import ml_dtypes
from contextlib import ExitStack

import concourse.bass as bass
import concourse.mybir as mybir
from concourse.bass_utils import run_bass_kernel_spmd

F32 = mybir.dt.float32
BF16 = mybir.dt.bfloat16
AF = mybir.ActivationFunctionType
ALU = mybir.AluOpType

D = 2048
T = 1024
TT = 512
NCH = 16
NH = 16
HD = 128
HID = 8192
HCH = 64
EPS = 1e-5
PAD = 512

ENGS = ("tensor", "vector", "scalar", "sync", "gpsimd")


class Sem:
    def __init__(self, stack, nc, name):
        self.h = stack.enter_context(nc.semaphore(name))
        self.n = 0
        self.name = name


class Bank:
    def __init__(self, ap, idx):
        self.ap = ap
        self.idx = idx
        self.free_evt = None


class Builder:
    def __init__(self, nc, stack):
        self.nc = nc
        self.stack = stack
        self.plan = {e: [] for e in ENGS}
        self.wm = {e: {} for e in ENGS}
        self.sem = {k: Sem(stack, nc, k) for k in
                    ("tE", "vE", "sE", "gE", "cc")}
        self.chans = {}
        self.cur_scope = "p0"
        self.ln_slot = [0, 0]
        self.tmp_slot = [(None, 0), (None, 0)]
        self.rb_consumed = 0

    def scope(self, name):
        self.cur_scope = name

    def em(self, eng, fn):
        self.plan[eng].append((self.cur_scope, fn))

    def wait(self, eng, sem, val):
        if val is None or val <= 0:
            return
        if self.wm[eng].get(sem.name, 0) >= val:
            return
        self.wm[eng][sem.name] = val
        self.plan[eng].append((self.cur_scope, lambda e, s=sem.h, v=val: e.wait_ge(s, v)))

    def waite(self, eng, evt):
        if evt is not None and evt[0] is not None:
            self.wait(eng, evt[0], evt[1])

    def chan(self, name):
        if name not in self.chans:
            self.chans[name] = Sem(self.stack, self.nc, "ch_" + name)
        return self.chans[name]

    def dma(self, eng, chname, fn):
        ch = self.chan(chname)
        ch.n += 16
        v = ch.n
        self.plan[eng].append((self.cur_scope, lambda e, f=fn, s=ch.h: f(e).then_inc(s, 16)))
        return (ch, v)

    ENGSEM = {"vector": "vE", "scalar": "sE", "gpsimd": "gE"}

    def op(self, eng, fn):
        sem = self.sem[self.ENGSEM[eng]]
        if sem.n > 0:
            self.wait(eng, sem, sem.n)
        sem.n += 1
        v = sem.n
        self.plan[eng].append((self.cur_scope, lambda e, f=fn, s=sem.h: f(e).then_inc(s, 1)))
        return v

    def inc(self, eng, sem, fn, k=1):
        sem.n += k
        v = sem.n
        self.plan[eng].append((self.cur_scope, lambda e, f=fn, s=sem.h, kk=k: f(e).then_inc(s, kk)))
        return v

    def run(self, eng, fn):
        self.plan[eng].append((self.cur_scope, fn))

    def flush(self, block):
        nc = self.nc
        for eng in ENGS:
            lst = self.plan[eng]
            if not lst:
                continue
            def body(e, lst=lst):
                cur = None
                cur_id = None
                for sc, fn in lst:
                    if sc != cur:
                        if cur is not None:
                            nc.leave_named_scope(cur, cur_id, False)
                        if sc is not None:
                            cur_id, _ = nc.enter_named_scope(sc, False)
                        cur = sc
                    fn(e)
                if cur is not None:
                    nc.leave_named_scope(cur, cur_id, False)
            getattr(block, eng)(body)

    def init_psum(self):
        self.banks = []
        for i in range(8):
            t = self.stack.enter_context(
                self.nc.psum_tensor("ps%d" % i, [128, TT], F32))
            self.banks.append(Bank(t, i))
        self._rr = 0

    def bank(self):
        bk = self.banks[self._rr]
        self._rr = (self._rr + 1) % 8
        return bk


def build_program(stop=None):
    nc = bass.Bass("TRN2", target_bir_lowering=False)
    stack = ExitStack()
    b = Builder(nc, stack)
    S = b.sem

    dt = {}
    def din(name, shape, dtype=F32):
        dt[name] = nc.dram_tensor(name, shape, dtype, kind="ExternalInput")
        return dt[name]
    def dint(name, shape, dtype=BF16):
        dt[name] = nc.dram_tensor(name, shape, dtype)
        return dt[name]

    x0h = din("x0", [D, T])
    din("in_w", [D, D]); din("out_w", [D, D])
    din("w1a", [D, HID]); din("w2a", [HID, D])
    din("ssm_mats", [128, NH * 5 * HD])
    din("wq", [D, D]); din("wkv", [D, D]); din("wo", [D, D])
    din("fw1", [D, HID]); din("fw2", [HID, D]); din("fw3", [D, HID])
    din("vcolsin", [128, 7 * 16])
    din("b1colsin", [128, 64])
    din("maskc", [128, 8])
    din("consts_b", [128, 10 * 128], BF16)
    din("onescol_f", [128, 1])

    y_out = nc.dram_tensor("y", [D, T], F32, kind="ExternalOutput")
    dbg = nc.dram_tensor("dbg", [D, T], F32, kind="ExternalOutput")

    ssmb = dint("ssm_bf", [128, NH * 5 * HD])
    w1T = dint("w1T", [D, HID]); w3T = dint("w3T", [D, HID])
    pw_dram = dint("pw_dram", [NH * HD, 9 * 128])
    u_dram = dint("u_dram", [D, T])
    s_dram = dint("s_dram", [D, T])
    w2bf = dint("w2bf", [HID, D])
    w2t3 = dint("w2t3", [HID, D])
    cc1_in = nc.dram_tensor("cc1_in", [128, 16], F32)
    cc1_out = nc.dram_tensor("cc1_out", [8 * 128, 16], F32)
    cc2_in = nc.dram_tensor("cc2_in", [128, 16], F32)
    cc2_out = nc.dram_tensor("cc2_out", [8 * 128, 16], F32)

    sbuf = lambda name, shape, dtype: stack.enter_context(
        nc.sbuf_tensor(name, shape, dtype))

    St = sbuf("S", [128, NCH * T], F32)
    BB1 = sbuf("BB1", [128, NCH * T], BF16)
    TBu = sbuf("TBu", [128, NCH * T], BF16)
    NB = sbuf("NB", [128, NCH * TT], BF16)
    WA = sbuf("WA", [128, 4096], BF16)
    UH = sbuf("UH", [128, 2 * T], BF16)
    SST = sbuf("SST", [128, 2 * T], BF16)
    XBs = sbuf("XBs", [128, 2 * TT], BF16)
    XQs = sbuf("XQs", [128, 2 * TT], BF16)
    TMP = sbuf("TMP", [128, 2 * TT], F32)
    XT = sbuf("XT", [128, 2 * TT], F32)
    ROWS = sbuf("ROWS", [1, 4 * TT], F32)
    RB = sbuf("RB", [1, 2 * TT], BF16)
    SSMW = sbuf("SSMW", [128, 46 * 128], BF16)
    XS0 = sbuf("XS0", [128, PAD + T], BF16)
    XS1 = sbuf("XS1", [128, PAD + T], BF16)
    VBt = sbuf("VBt", [128, T], BF16)
    CAR = sbuf("CAR", [128, NH], F32)
    CARB = sbuf("CARB", [128, NH], BF16)
    GG = sbuf("GG", [128, 8 * NH], F32)
    HIN = sbuf("HIN", [128, NH], F32)
    VCOLS = sbuf("VCOLS", [128, 7 * 16], F32)
    B1C = sbuf("B1C", [128, 64], F32)
    MASK = sbuf("MASK", [128, 8], F32)
    CONS = sbuf("CONS", [128, 10 * 128], BF16)
    ONEC = sbuf("ONEC", [128, 1], F32)
    CONSTF = sbuf("CONSTF", [128, 4], F32)

    b.init_psum()

    def cons(i):
        return CONS[:, i * 128:(i + 1) * 128]
    ONESROW_B = cons(0)[0:1, :]
    ONESCOL_B = cons(0)[:, 0:1]
    I128 = cons(1)
    H128 = cons(2)
    H64BD = cons(3)
    DFTC = cons(4)
    DFTS = cons(5)
    IDFTC = cons(6)
    IDFTS = cons(7)

    VKEYS = ["ln1w", "ln1b", "ln2w", "ln2b", "in_b", "out_b", "b2"]
    def vcol(key, c):
        i = VKEYS.index(key)
        return VCOLS[:, i * 16 + c: i * 16 + c + 1]
    def b1col(c):
        return B1C[:, c:c + 1]

    x0_lds = []
    for xp in range(4):
        x0_lds.append(b.dma("sync", "x0_%d" % xp, lambda e, xp=xp: e.dma_start(
            out=St[:, xp * 4 * T:(xp + 1) * 4 * T].rearrange("p (c t) -> p c t", c=4),
            in_=x0h[xp * 512:(xp + 1) * 512, :].rearrange("(c p) t -> p c t", p=128))))
    x0_ld = x0_lds[-1]
    b.dma("sync", "cons", lambda e: e.dma_start(
        out=CONS[:, :], in_=dt["consts_b"].ap()))
    b.dma("sync", "cons", lambda e: e.dma_start(
        out=MASK[:, :], in_=dt["maskc"].ap()))
    b.dma("sync", "cons", lambda e: e.dma_start(
        out=ONEC[:, :], in_=dt["onescol_f"].ap()))
    b.dma("sync", "cons", lambda e: e.dma_start(
        out=VCOLS[:, :], in_=dt["vcolsin"].ap()))
    cons_grp = b.dma("sync", "cons", lambda e: e.dma_start(
        out=B1C[:, :], in_=dt["b1colsin"].ap()))
    b.op("vector", lambda e: e.memset(CONSTF[:, 0:1], float(EPS)))
    b.op("vector", lambda e: e.memset(CONSTF[:, 1:2], 0.0))
    constf_ev = b.op("vector", lambda e: e.memset(CONSTF[:, 2:3], 1.0))
    EPSC = CONSTF[0:1, 0:1]

    cast_evts = {}
    cast_piece_evts = {}
    cast_queue = []
    cast_ch_state = {"i": 0, "last": [None] * 6}
    def cast_chunked(name, src, dst, rows, piece):
        for r0 in range(0, rows, piece):
            r1 = min(rows, r0 + piece)
            cast_queue.append((name, src, dst, r0, r1))

    def cast_pump(n=1, pace=True):
        for _ in range(n):
            if not cast_queue:
                return
            name, srct, dstt, r0, r1 = cast_queue.pop(0)
            if pace:
                b.wait("gpsimd", S["tE"], S["tE"].n)
            ci = cast_ch_state["i"]; cast_ch_state["i"] = (ci + 1) % 6
            b.waite("gpsimd", cast_ch_state["last"][ci])
            ev = b.dma("gpsimd", "cast%d" % ci,
                       lambda e, r0=r0, r1=r1, srct=srct, dstt=dstt:
                       e.dma_start(out=dstt[r0:r1, :], in_=srct[r0:r1, :]))
            cast_ch_state["last"][ci] = ev
            cast_evts[name] = ev
            cast_piece_evts.setdefault(name, []).append(ev)

    def cast_flush(names):
        while any(q[0] in names for q in cast_queue):
            cast_pump(1)

    cast_chunked("ssm", dt["ssm_mats"], ssmb, 128, 128)
    cast_pump(1, pace=False)
    for _name, _s, _d, _r, _p in (
            ("w1", dt["fw1"], w1T, D, 256),
            ("w3", dt["fw3"], w3T, D, 256),
            ("w2", dt["fw2"], w2bf, HID, 256)):
        cast_chunked(_name, _s, _d, _r, _p)

    def emit_ln(src_ch, dst_ch, w_key=None, src_f32=True, src_gate=None,
                dst_gate=None):
        stats_g = [None] * NCH
        bm = b.bank(); bs = b.bank()
        b.waite("tensor", bm.free_evt); b.waite("tensor", bs.free_evt)
        for c in range(NCH):
            sl = c % 2
            xb_ap = XBs[:, sl * TT:(sl + 1) * TT]
            xq_ap = XQs[:, sl * TT:(sl + 1) * TT]
            if c >= 2:
                b.wait("vector", S["tE"], stats_g[c - 2])
            else:
                b.wait("vector", S["tE"], b.ln_slot[sl])
            if src_gate is not None:
                b.waite("vector", src_gate(c))
            if src_f32:
                b.op("vector", lambda e, c=c, xb=xb_ap: e.tensor_copy(xb, src_ch(c)))
                ev = b.op("vector", lambda e, xb=xb_ap, xq=xq_ap:
                           e.tensor_tensor(xq, xb, xb, ALU.mult))
                stats_src = xb_ap
            else:
                ev = b.op("vector", lambda e, c=c, xq=xq_ap:
                           e.tensor_tensor(xq, src_ch(c), src_ch(c), ALU.mult))
                stats_src = src_ch(c)
            b.wait("tensor", S["vE"], ev)
            first = (c == 0); last = (c == NCH - 1)
            b.run("tensor", lambda e, s=stats_src, bm=bm, f=first, l=last:
                  e.matmul(bm.ap[0:1, :], ONESCOL_B, s, start=f, stop=l))
            stats_g[c] = b.inc("tensor", S["tE"],
                               lambda e, xq=xq_ap, bs=bs, f=first, l=last:
                               e.matmul(bs.ap[0:1, :], ONESCOL_B, xq, start=f, stop=l))
        b.ln_slot[0] = stats_g[NCH - 2]
        b.ln_slot[1] = stats_g[NCH - 1]
        g_stats = stats_g[NCH - 1]

        m_row = ROWS[0:1, 0:TT]; r_row = ROWS[0:1, TT:2 * TT]
        q_row = ROWS[0:1, 2 * TT:3 * TT]; s_row = ROWS[0:1, 3 * TT:4 * TT]
        b.wait("vector", S["tE"], g_stats)
        b.op("vector", lambda e, bm=bm: e.tensor_scalar_mul(m_row, bm.ap[0:1, 0:TT], 1.0 / D))
        b.op("vector", lambda e, bs=bs: e.tensor_scalar_mul(s_row, bs.ap[0:1, 0:TT], 1.0 / D))
        b.op("vector", lambda e: e.tensor_tensor(q_row, m_row, m_row, ALU.mult))
        ev = b.op("vector", lambda e: e.tensor_tensor(s_row, s_row, q_row, ALU.subtract))
        bm.free_evt = (S["vE"], ev); bs.free_evt = (S["vE"], ev)
        b.wait("scalar", S["vE"], max(ev, constf_ev))
        ev = b.op("scalar", lambda e: e.activation(
            s_row, s_row, AF.Sqrt, bias=EPSC))
        b.wait("vector", S["sE"], ev)
        b.op("vector", lambda e: e.reciprocal(r_row, s_row))
        b.op("vector", lambda e: e.tensor_tensor(q_row, m_row, r_row, ALU.mult))
        rb_row = RB[0:1, 0:TT]; qb_row = RB[0:1, TT:2 * TT]
        b.wait("vector", S["tE"], b.rb_consumed)
        b.op("vector", lambda e: e.tensor_copy(rb_row, r_row))
        ev = b.op("vector", lambda e: e.tensor_copy(qb_row, q_row))

        br = b.bank(); bq = b.bank()
        b.waite("tensor", br.free_evt); b.waite("tensor", bq.free_evt)
        b.wait("tensor", S["vE"], ev)
        b.run("tensor", lambda e, br=br: e.matmul(br.ap[:, :], ONESROW_B, rb_row,
                                                  start=True, stop=True))
        g_bc = b.inc("tensor", S["tE"], lambda e, bq=bq: e.matmul(
            bq.ap[:, :], ONESROW_B, qb_row, start=True, stop=True))
        b.rb_consumed = g_bc
        b.wait("vector", S["tE"], g_bc)
        if dst_gate is not None:
            b.waite("vector", dst_gate)
            if w_key is not None:
                b.waite("scalar", dst_gate)
        last_ev = None
        for c in range(NCH):
            sl = c % 2
            t1 = TMP[:, sl * TT:(sl + 1) * TT]
            b.waite("vector", b.tmp_slot[sl])
            b.op("vector", lambda e, c=c, t1=t1, br=br:
                  e.tensor_tensor(t1, src_ch(c), br.ap[:, :], ALU.mult))
            if w_key is None:
                ev2 = b.op("vector", lambda e, c=c, t1=t1, bq=bq:
                            e.tensor_tensor(dst_ch(c), t1, bq.ap[:, :], ALU.subtract))
                b.tmp_slot[sl] = (S["vE"], ev2)
                last = (S["vE"], ev2)
            else:
                ev2 = b.op("vector", lambda e, c=c, t1=t1, bq=bq:
                            e.tensor_tensor(t1, t1, bq.ap[:, :], ALU.subtract))
                b.wait("scalar", S["vE"], ev2)
                ev3 = b.op("scalar", lambda e, c=c, t1=t1:
                            e.activation(dst_ch(c), t1, AF.Identity,
                                         bias=vcol(w_key[1], c), scale=vcol(w_key[0], c)))
                b.tmp_slot[sl] = (S["sE"], ev3)
                last = (S["sE"], ev3)
            last_ev = last
        br.free_evt = last_ev if w_key is None else (S["vE"], ev2)
        bq.free_evt = br.free_evt
        return last_ev

    stopped = [False]

    def dump_bf16_full(buf, done_evt):
        b.waite("vector", done_evt)
        st_ev = [None, None]
        for c in range(NCH):
            for tau in range(2):
                sl = (2 * c + tau) % 2
                xt = XT[:, sl * TT:(sl + 1) * TT]
                b.waite("vector", st_ev[sl])
                ev = b.op("vector", lambda e, c=c, tau=tau, xt=xt:
                           e.tensor_copy(xt, buf[:, c * T + tau * TT: c * T + (tau + 1) * TT]))
                b.wait("scalar", S["vE"], ev)
                st_ev[sl] = b.dma("scalar", "dmp%d" % sl, lambda e, c=c, tau=tau, xt=xt:
                              e.dma_start(out=dbg[c * 128:(c + 1) * 128,
                                                  tau * TT:(tau + 1) * TT], in_=xt))
        b.waite("scalar", st_ev[0])
        b.waite("scalar", st_ev[1])
        stopped[0] = True

    def dump_f32_full(buf, done_evt):
        b.waite("sync", done_evt)
        ev = b.dma("sync", "out", lambda e: e.dma_start(
            out=dbg.ap().rearrange("(c p) t -> p c t", p=128),
            in_=buf[:, :].rearrange("p (c t) -> p c t", c=NCH)))
        b.waite("sync", ev)
        stopped[0] = True

    def finish():
        with nc.Block() as block:
            b.flush(block)
        stack.close()
        return nc

    b.scope('p1_ln0')
    b.waite("tensor", cons_grp)
    for tau in range(2):
        h1_done = emit_ln(
            src_ch=lambda c, tau=tau: St[:, c * T + tau * TT: c * T + (tau + 1) * TT],
            dst_ch=lambda c, tau=tau: BB1[:, c * T + tau * TT: c * T + (tau + 1) * TT],
            src_f32=True, src_gate=lambda c: x0_lds[c // 4])
    if stop == "h1":
        dump_bf16_full(BB1, h1_done)
        return finish()

    b.scope('p2_ln1')
    for tau in range(2):
        t_done = emit_ln(
            src_ch=lambda c, tau=tau: BB1[:, c * T + tau * TT: c * T + (tau + 1) * TT],
            dst_ch=lambda c, tau=tau: TBu[:, c * T + tau * TT: c * T + (tau + 1) * TT],
            w_key=("ln1w", "ln1b"), src_f32=False)
    if stop == "t":
        dump_bf16_full(TBu, t_done)
        return finish()

    proj_state = {"wa_evt": [None, None], "ld_i": 0}

    def emit_proj(w_hbm, src_ch, evac, n_k=NCH, n_j=NCH, src_evt=None,
                  taus=(0, 1), shared_tau=True, mode="f32", w_ready=None):
        ps = proj_state
        if isinstance(w_ready, str):
            for pe_ in cast_piece_evts[w_ready]:
                b.waite("sync", pe_)
        elif w_ready is not None:
            b.waite("sync", w_ready)
        jg_size = 2 if shared_tau else 4
        tau_groups = [list(taus)] if shared_tau else [[t] for t in taus]
        gend = None
        for tg in tau_groups:
            for jg in range(n_j // jg_size):
                cast_pump(1)
                banks = {}
                for jj in range(jg_size):
                    for tau in tg:
                        bk = b.bank()
                        b.waite("tensor", bk.free_evt)
                        banks[(jj, tau)] = bk
                if src_evt is not None:
                    b.waite("tensor", src_evt)
                jw = jg_size * 128
                if mode in ("bf16", "f32cast"):
                    assert n_k % 4 == 0 and jw in (256, 512)
                    eng_q = "sync" if mode == "bf16" else "gpsimd"
                    chp = "wa" if mode == "bf16" else "wacast"
                    for k4 in range(n_k // 4):
                        sl = ps["ld_i"] % 2
                        ps["ld_i"] += 1
                        wa = WA[:, sl * 2048: sl * 2048 + 4 * jw]
                        b.waite(eng_q, ps["wa_evt"][sl])
                        ld = b.dma(eng_q, chp + "%d" % sl,
                                   lambda e, k4=k4, jg=jg, wa=wa, jw=jw:
                                   e.dma_start(
                                       out=wa.rearrange("p (k c) -> p k c", k=4),
                                       in_=w_hbm[k4 * 512:(k4 + 1) * 512,
                                                 jg * jw:(jg + 1) * jw]
                                       .rearrange("(k p) c -> p k c", p=128)))
                        b.waite("tensor", ld)
                        gk = None
                        for dk in range(4):
                            k = k4 * 4 + dk
                            first = (k == 0); last = (k == n_k - 1)
                            for jj in range(jg_size):
                                for tau in tg:
                                    bk = banks[(jj, tau)]
                                    fn = (lambda e, bk=bk, wa=wa, jj=jj, k=k, dk=dk,
                                          tau=tau, f=first, l=last:
                                          e.matmul(bk.ap[:, :],
                                                   wa[:, dk * jw + jj * 128: dk * jw + (jj + 1) * 128],
                                                   src_ch(k, tau), start=f, stop=l))
                                    if dk == 3 and jj == jg_size - 1 and tau == tg[-1]:
                                        gk = b.inc("tensor", S["tE"], fn)
                                    else:
                                        b.run("tensor", fn)
                        ps["wa_evt"][sl] = (S["tE"], gk)
                else:
                    for k in range(n_k):
                        sl = ps["ld_i"] % 2
                        ps["ld_i"] += 1
                        wa = WA[:, sl * 2048: sl * 2048 + jw]
                        b.waite("sync", ps["wa_evt"][sl])
                        ld = b.dma("sync", "wa%d" % sl, lambda e, k=k, jg=jg, wa=wa, jw=jw:
                                   e.dma_start(out=wa,
                                               in_=w_hbm[jg * jw:(jg + 1) * jw,
                                                         k * 128:(k + 1) * 128],
                                               transpose=True))
                        b.waite("tensor", ld)
                        first = (k == 0); last = (k == n_k - 1)
                        gk = None
                        for jj in range(jg_size):
                            for tau in tg:
                                bk = banks[(jj, tau)]
                                fn = (lambda e, bk=bk, wa=wa, jj=jj, k=k, tau=tau,
                                      f=first, l=last:
                                      e.matmul(bk.ap[:, :], wa[:, jj * 128:(jj + 1) * 128],
                                               src_ch(k, tau), start=f, stop=l))
                                if jj == jg_size - 1 and tau == tg[-1]:
                                    gk = b.inc("tensor", S["tE"], fn)
                                else:
                                    b.run("tensor", fn)
                        ps["wa_evt"][sl] = (S["tE"], gk)
                gend = gk
                for jj in range(jg_size):
                    for tau in tg:
                        evac(jg * jg_size + jj, tau, banks[(jj, tau)], gend)
        return gend

    b.scope('p3_inproj')
    u_st = [None]
    def evac_inproj(j, tau, bk, gend):
        sl = j % 2
        dst = UH[:, sl * T + tau * TT: sl * T + (tau + 1) * TT]
        b.wait("scalar", S["tE"], gend)
        b.waite("scalar", u_st[0])
        ev = b.op("scalar", lambda e, bk=bk, dst=dst, j=j:
                   e.activation(dst, bk.ap[:, :], AF.Identity, bias=vcol("in_b", j)))
        bk.free_evt = (S["sE"], ev)
        if tau == 1:
            b.wait("scalar", S["sE"], ev)
            u_st[0] = b.dma("scalar", "ust", lambda e, j=j, sl=sl:
                            e.dma_start(out=u_dram[j * 128:(j + 1) * 128, :],
                                        in_=UH[:, sl * T:(sl + 1) * T]))

    emit_proj(dt["in_w"],
              lambda k, tau: TBu[:, k * T + tau * TT: k * T + (tau + 1) * TT],
              evac_inproj, src_evt=t_done, mode="f32cast")
    if stop == "u":
        b.waite("sync", u_st[0])
        ld = b.dma("sync", "out", lambda e: e.dma_start(
            out=BB1[:, :].rearrange("p (c t) -> p c t", c=NCH),
            in_=u_dram.ap().rearrange("(c p) t -> p c t", p=128)))
        b.waite("vector", ld)
        dump_bf16_full(BB1, None)
        return finish()

    b.scope('p4_scan')
    def pwp(par, i):
        off = 23 * par
        return SSMW[:, (off + i) * 128:(off + i + 1) * 128]

    ssmw_used = [0, 0]
    pw_store_ref = [None]
    xs_store = [None]

    b.op("vector", lambda e: e.memset(XS0[:, 0:PAD], 0.0))
    pad_ev = b.op("vector", lambda e: e.memset(XS1[:, 0:PAD], 0.0))
    b.wait("tensor", S["vE"], pad_ev)

    def ssm_head_loads(h, which):
        par = h % 2
        b.wait("scalar", S["tE"], ssmw_used[par])
        b.waite("scalar", cast_evts["ssm"])
        chn = "ssmw%d" % par
        off = 23 * par
        evs = []
        evs.append(b.dma("scalar", chn, lambda e, h=h, off=off: e.dma_start(
            out=SSMW[:, off * 128:(off + 5) * 128],
            in_=ssmb[:, h * 640:(h + 1) * 640])))
        if which == "corr":
            evs.append(b.dma("scalar", chn, lambda e, h=h, off=off: e.dma_start(
                out=SSMW[:, (off + 5) * 128:(off + 14) * 128],
                in_=pw_dram[h * 128:(h + 1) * 128, :])))
        return evs[-1]

    def ssm_powers_stepper(h, ld_ev):
        par = h % 2
        pw = lambda i: pwp(par, i)
        st = {"k": 1, "prev_ev": None}

        def step(n=1):
            for _ in range(n):
                k = st["k"]
                if k > 9:
                    return st["prev_ev"]
                st["k"] += 1
                if k == 1:
                    b.waite("tensor", ld_ev)
                    if pw_store_ref[0] is not None:
                        b.waite("vector", pw_store_ref[0][par])
                bka = b.bank(); bkb = b.bank()
                b.waite("tensor", bka.free_evt); b.waite("tensor", bkb.free_evt)
                if st["prev_ev"] is not None:
                    b.wait("tensor", S["vE"], st["prev_ev"])
                qp = pw(0) if k == 1 else pw(13 + k - 1)
                pp = pw(1) if k == 1 else pw(4 + k - 1)
                b.run("tensor", lambda e, bka=bka, qp=qp, pp=pp: e.matmul(
                    bka.ap[:, 0:128], qp, pp, start=True, stop=True))
                g = b.inc("tensor", S["tE"], lambda e, bkb=bkb, qp=qp, pp=pp: e.matmul(
                    bkb.ap[:, 0:128], pp, qp, start=True, stop=True))
                b.wait("vector", S["tE"], g)
                b.op("vector", lambda e, bka=bka, k=k, pw=pw: e.tensor_copy(pw(4 + k), bka.ap[:, 0:128]))
                st["prev_ev"] = b.op("vector", lambda e, bkb=bkb, k=k, pw=pw:
                                e.tensor_copy(pw(13 + k), bkb.ap[:, 0:128]))
                bka.free_evt = (S["vE"], st["prev_ev"]); bkb.free_evt = (S["vE"], st["prev_ev"])
            return st["prev_ev"]
        return step

    def ssm_powers(h, ld_ev):
        return ssm_powers_stepper(h, ld_ev)(9)

    def load_uh(h):
        sl = h % 2
        b.wait("scalar", S["tE"], ssmw_used[sl])
        b.waite("scalar", u_st[0])
        ev = b.dma("scalar", "uh%d" % sl, lambda e, h=h, sl=sl: e.dma_start(
            out=UH[:, sl * T:(sl + 1) * T], in_=u_dram[h * 128:(h + 1) * 128, :]))
        return sl, ev

    pw_store = [None, None]
    pw_store_ref[0] = pw_store
    ldh = {0: ssm_head_loads(0, "scan")}
    uhd = {0: load_uh(0)}
    pwev = {0: ssm_powers(0, ldh[0])}
    b.wait("scalar", S["vE"], pwev[0])
    pw_store[0] = b.dma("scalar", "pwst0", lambda e: e.dma_start(
        out=pw_dram[0:128, :], in_=SSMW[:, 5 * 128:14 * 128]))
    for h in range(NH):
        par = h % 2
        pw = lambda i, par=par: pwp(par, i)
        usl, u_ev = uhd[h]
        ev_v = ev_s = None
        for tau in range(2):
            bk = b.bank(); b.waite("tensor", bk.free_evt)
            b.waite("tensor", u_ev)
            b.waite("tensor", ldh[h])
            g = b.inc("tensor", S["tE"], lambda e, bk=bk, usl=usl, tau=tau, pw=pw: e.matmul(
                bk.ap[:, :], pw(2), UH[:, usl * T + tau * TT: usl * T + (tau + 1) * TT],
                start=True, stop=True))
            eng = "vector" if tau == 0 else "scalar"
            sem = S["vE"] if tau == 0 else S["sE"]
            b.wait(eng, S["tE"], g)
            b.waite(eng, xs_store[0])
            if tau == 0:
                ev_v = b.op(eng, lambda e, bk=bk, tau=tau: e.tensor_copy(
                    XS0[:, PAD + tau * TT: PAD + (tau + 1) * TT], bk.ap[:, :]))
                bk.free_evt = (S["vE"], ev_v)
            else:
                ev_s = b.op(eng, lambda e, bk=bk, tau=tau: e.activation(
                    XS0[:, PAD + tau * TT: PAD + (tau + 1) * TT], bk.ap[:, :], AF.Identity))
                bk.free_evt = (S["sE"], ev_s)
        pstep = None
        if h + 1 < NH:
            ldh[h + 1] = ssm_head_loads(h + 1, "scan")
            uhd[h + 1] = load_uh(h + 1)
            pstep = ssm_powers_stepper(h + 1, ldh[h + 1])
        cast_pump(2)
        PSCHED = [0, 0, 1, 1, 1, 1, 2, 2, 1, 0]
        pw_ev_last = None
        xsrc, xdst = XS0, XS1
        for k in range(10):
            sh = 1 << k
            nev_v = nev_s = None
            for tau in range(2):
                bk = b.bank(); b.waite("tensor", bk.free_evt)
                b.wait("tensor", S["vE"], ev_v)
                b.wait("tensor", S["sE"], ev_s)
                b.run("tensor", lambda e, bk=bk, xsrc=xsrc, tau=tau: e.matmul(
                    bk.ap[:, :], I128,
                    xsrc[:, PAD + tau * TT: PAD + (tau + 1) * TT], start=True, stop=False))
                g = b.inc("tensor", S["tE"], lambda e, bk=bk, xsrc=xsrc, tau=tau, k=k, sh=sh, pw=pw:
                          e.matmul(bk.ap[:, :], pw(1) if k == 0 else pw(4 + k),
                                   xsrc[:, PAD + tau * TT - sh: PAD + (tau + 1) * TT - sh],
                                   start=False, stop=True))
                if tau == 0:
                    b.wait("vector", S["tE"], g)
                    nev_v = b.op("vector", lambda e, bk=bk, xdst=xdst, tau=tau:
                               e.tensor_copy(xdst[:, PAD + tau * TT: PAD + (tau + 1) * TT],
                                             bk.ap[:, :]))
                    bk.free_evt = (S["vE"], nev_v)
                else:
                    b.wait("scalar", S["tE"], g)
                    nev_s = b.op("scalar", lambda e, bk=bk, xdst=xdst, tau=tau:
                               e.activation(xdst[:, PAD + tau * TT: PAD + (tau + 1) * TT],
                                            bk.ap[:, :], AF.Identity))
                    bk.free_evt = (S["sE"], nev_s)
            ev_v, ev_s = nev_v, nev_s
            xsrc, xdst = xdst, xsrc
            if pstep is not None:
                pw_ev_last = pstep(PSCHED[k])
        if h + 1 < NH:
            pwev[h + 1] = pw_ev_last
            b.wait("scalar", S["vE"], pwev[h + 1])
            pw_store[(h + 1) % 2] = b.dma("scalar", "pwst%d" % ((h + 1) % 2),
                lambda e, h1=h + 1: e.dma_start(
                    out=pw_dram[h1 * 128:(h1 + 1) * 128, :],
                    in_=SSMW[:, (23 * ((h1) % 2) + 5) * 128:(23 * ((h1) % 2) + 14) * 128]))
        ssmw_used[par] = S["tE"].n
        b.wait("vector", S["sE"], ev_s)
        car_ev = b.op("vector", lambda e, h=h: e.tensor_copy(
            CAR[:, h:h + 1], XS0[:, PAD + T - 1: PAD + T]))
        b.wait("scalar", S["vE"], car_ev)
        xs_store[0] = b.dma("scalar", "xst", lambda e, h=h: e.dma_start(
            out=s_dram[h * 128:(h + 1) * 128, :], in_=XS0[:, PAD:PAD + T]))
    if stop == "states":
        b.waite("sync", xs_store[0])
        ld = b.dma("sync", "out", lambda e: e.dma_start(
            out=BB1[:, :].rearrange("p (c t) -> p c t", c=NCH),
            in_=s_dram.ap().rearrange("(c p) t -> p c t", p=128)))
        b.waite("vector", ld)
        dump_bf16_full(BB1, None)
        return finish()

    b.scope('p4b_cc')
    b.wait("gpsimd", S["vE"], car_ev)
    b.wait("sync", S["vE"], car_ev)
    cin_ev = b.dma("sync", "ccio", lambda e: e.dma_start(
        out=cc1_in.ap(), in_=CAR[:, :]))
    b.waite("gpsimd", cin_ev)
    cc_ev = b.inc("gpsimd", S["cc"], lambda e: e.collective_compute(
        "AllGather", ALU.bypass, replica_groups=[list(range(8))],
        ins=[cc1_in.ap().opt()], outs=[cc1_out.ap().opt()]))
    b.wait("gpsimd", S["cc"], cc_ev)
    b.wait("sync", S["cc"], cc_ev)
    gg_ev = b.dma("sync", "ccio", lambda e: e.dma_start(
        out=GG[:, :].rearrange("p (r h) -> p r h", r=8),
        in_=cc1_out.ap().rearrange("(r i) h -> i r h", i=128)))
    b.waite("vector", gg_ev)
    b.waite("vector", cons_grp)
    b.op("vector", lambda e: e.tensor_scalar(
        HIN[:, :], GG[:, 0:NH], MASK[:, 0:1], None, ALU.mult))
    for r in range(1, 8):
        b.op("vector", lambda e, r=r: e.scalar_tensor_tensor(
            HIN[:, :], GG[:, r * NH:(r + 1) * NH], MASK[:, r:r + 1], HIN[:, :],
            ALU.mult, ALU.add))
    hinb_ev = b.op("vector", lambda e: e.tensor_copy(CARB[:, :], HIN[:, :]))

    b.scope('p4c_corr')
    sst_used = [0]
    VB2 = [VBt[:, 0:T], XQs[:, 0:2 * TT]]
    _bar4c = (S["tE"], S["tE"].n)
    vb_used = [_bar4c, _bar4c]

    def emit_vchain(h):
        par = h % 2
        pw = lambda i: pwp(par, i)
        vb = VB2[par]
        bk = b.bank(); b.waite("tensor", bk.free_evt)
        b.waite("tensor", ldh[h])
        b.wait("tensor", S["vE"], hinb_ev)
        g = b.inc("tensor", S["tE"], lambda e, bk=bk, h=h, pw=pw: e.matmul(
            bk.ap[:, 0:1], pw(1), CARB[:, h:h + 1], start=True, stop=True))
        b.wait("vector", S["tE"], g)
        b.waite("vector", vb_used[par])
        vev = b.op("vector", lambda e, bk=bk, vb=vb: e.tensor_copy(
            vb[:, 0:1], bk.ap[:, 0:1]))
        bk.free_evt = (S["vE"], vev)
        for k in range(10):
            w = 1 << k
            bk = b.bank(); b.waite("tensor", bk.free_evt)
            b.wait("tensor", S["vE"], vev)
            g = b.inc("tensor", S["tE"], lambda e, bk=bk, k=k, w=w, pw=pw, vb=vb: e.matmul(
                bk.ap[:, 0:w], pw(1) if k == 0 else pw(4 + k), vb[:, 0:w],
                start=True, stop=True))
            b.wait("vector", S["tE"], g)
            vev = b.op("vector", lambda e, bk=bk, w=w, vb=vb: e.tensor_copy(
                vb[:, w:2 * w], bk.ap[:, 0:w]))
            bk.free_evt = (S["vE"], vev)
        return vev

    b.waite("scalar", pw_store[0]); b.waite("scalar", pw_store[1])
    ldh = {0: ssm_head_loads(0, "corr")}
    uhd = {0: load_uh(0)}
    vch = {0: emit_vchain(0)}
    std = {}
    for h in range(NH):
        par = h % 2
        pw = lambda i, par=par: pwp(par, i)
        if h == 0:
            sl = 0
            b.wait("scalar", S["tE"], sst_used[0])
            b.waite("scalar", xs_store[0])
            std[0] = b.dma("scalar", "sst0", lambda e: e.dma_start(
                out=SST[:, 0:T], in_=s_dram[0:128, :]))
        st_ld = std[h]
        usl, u_ev = uhd[h]
        sl = h % 2
        if h + 1 < NH:
            ldh[h + 1] = ssm_head_loads(h + 1, "corr")
            uhd[h + 1] = load_uh(h + 1)
            sl2 = (h + 1) % 2
            b.wait("scalar", S["tE"], sst_used[0])
            std[h + 1] = b.dma("scalar", "sst%d" % sl2, lambda e, h=h, sl2=sl2: e.dma_start(
                out=SST[:, sl2 * T:(sl2 + 1) * T],
                in_=s_dram[(h + 1) * 128:(h + 2) * 128, :]))
            vch[h + 1] = emit_vchain(h + 1)
        cast_pump(2)
        b.waite("vector", st_ld)
        b.wait("vector", S["vE"], vch[h])
        add_ev = b.op("vector", lambda e, sl=sl: e.tensor_tensor(
            SST[:, sl * T:(sl + 1) * T], SST[:, sl * T:(sl + 1) * T],
            VB2[sl][:, 0:T], ALU.add))
        vb_used[sl] = (S["vE"], add_ev)
        for tau in range(2):
            bk = b.bank(); b.waite("tensor", bk.free_evt)
            b.wait("tensor", S["vE"], add_ev)
            b.waite("tensor", u_ev)
            b.run("tensor", lambda e, bk=bk, sl=sl, tau=tau, pw=pw: e.matmul(
                bk.ap[:, :], pw(3), SST[:, sl * T + tau * TT: sl * T + (tau + 1) * TT],
                start=True, stop=False))
            g = b.inc("tensor", S["tE"], lambda e, bk=bk, usl=usl, tau=tau, pw=pw: e.matmul(
                bk.ap[:, :], pw(4), UH[:, usl * T + tau * TT: usl * T + (tau + 1) * TT],
                start=False, stop=True))
            if tau == 0:
                b.wait("vector", S["tE"], g)
                yev = b.op("vector", lambda e, bk=bk, h=h, tau=tau: e.tensor_copy(
                    St[:, h * T + tau * TT: h * T + (tau + 1) * TT], bk.ap[:, :]))
                bk.free_evt = (S["vE"], yev)
            else:
                b.wait("scalar", S["tE"], g)
                yev = b.op("scalar", lambda e, bk=bk, h=h, tau=tau: e.activation(
                    St[:, h * T + tau * TT: h * T + (tau + 1) * TT], bk.ap[:, :], AF.Identity))
                bk.free_evt = (S["sE"], yev)
        ssmw_used[par] = S["tE"].n
        sst_used[0] = S["tE"].n
    b.ln_slot = [S["tE"].n, S["tE"].n]
    y_done = (S["vE"], S["vE"].n)
    y_done_s = (S["sE"], S["sE"].n)
    if stop == "yss":
        dump_f32_full(St, y_done)
        return finish()

    b.scope('p5_outproj')
    z_done = None
    for tau in range(2):
        cast_ev = None
        for c in range(NCH):
            b.waite("vector", y_done)
            b.wait("vector", S["sE"], y_done_s[1])
            cast_ev = b.op("vector", lambda e, c=c, tau=tau: e.tensor_copy(
                NB[:, c * TT:(c + 1) * TT], St[:, c * T + tau * TT: c * T + (tau + 1) * TT]))
        def evac_z(j, t2, bk, gend, tau=tau):
            b.wait("vector", S["tE"], gend)
            ev = b.op("vector", lambda e, bk=bk, j=j, tau=tau:
                       e.scalar_tensor_tensor(
                           St[:, j * T + tau * TT: j * T + (tau + 1) * TT],
                           bk.ap[:, :], vcol("out_b", j),
                           BB1[:, j * T + tau * TT: j * T + (tau + 1) * TT],
                           ALU.add, ALU.add))
            bk.free_evt = (S["vE"], ev)
            return ev
        evz = []
        def evac_z_wrap(j, t2, bk, gend):
            evz.append(evac_z(j, t2, bk, gend))
        emit_proj(dt["out_w"], lambda k, t2, tau=tau: NB[:, k * TT:(k + 1) * TT],
                  evac_z_wrap, src_evt=(S["vE"], cast_ev), taus=(tau,),
                  shared_tau=False, mode="f32cast")
        z_done = (S["vE"], evz[-1])
    if stop == "z":
        dump_f32_full(St, z_done)
        return finish()

    b.scope('p68_ffnA')
    def hidtile(c):
        buf = BB1 if c < 32 else TBu
        return buf[:, (c % 32) * TT:(c % 32 + 1) * TT]

    xt_used = [None, None]

    def emit_x0_add(tau):
        ev = None
        for c in range(NCH):
            sl = c % 2
            xt = XT[:, sl * TT:(sl + 1) * TT]
            b.waite("sync", xt_used[sl])
            ld = b.dma("sync", "xt%d" % sl, lambda e, c=c, tau=tau, xt=xt:
                       e.dma_start(out=xt, in_=x0h[c * 128:(c + 1) * 128,
                                                   tau * TT:(tau + 1) * TT]))
            b.waite("vector", ld)
            ev = b.op("vector", lambda e, c=c, tau=tau, xt=xt: e.tensor_tensor(
                St[:, c * T + tau * TT: c * T + (tau + 1) * TT],
                St[:, c * T + tau * TT: c * T + (tau + 1) * TT], xt, ALU.add))
            xt_used[sl] = (S["vE"], ev)
        return ev

    ln2_evs = {}
    ln2_evs[0] = emit_ln(
        src_ch=lambda c: St[:, c * T: c * T + TT],
        dst_ch=lambda c: NB[:, c * TT:(c + 1) * TT],
        w_key=("ln2w", "ln2b"), src_f32=True)
    emit_x0_add(0)
    x1_ev = [None]
    for tau in range(2):
        gelu_ev = {}
        def evac_gelu(j, t2, bk, gend):
            b.wait("scalar", S["tE"], gend)
            ev = b.op("scalar", lambda e, bk=bk, j=j: e.activation(
                hidtile(j), bk.ap[:, :], AF.Gelu, bias=b1col(j)))
            bk.free_evt = (S["sE"], ev)
            gelu_ev[j] = ev
        w1a_gend = emit_proj(dt["w1a"], lambda k, t2: NB[:, k * TT:(k + 1) * TT],
                  evac_gelu, n_k=NCH, n_j=HCH, src_evt=ln2_evs[tau],
                  taus=(tau,), shared_tau=False, mode="f32cast")
        if tau == 0:
            ln2_evs[1] = emit_ln(
                src_ch=lambda c: St[:, c * T + TT: c * T + 2 * TT],
                dst_ch=lambda c: NB[:, c * TT:(c + 1) * TT],
                w_key=("ln2w", "ln2b"), src_f32=True,
                dst_gate=(S["tE"], w1a_gend))
            emit_x0_add(1)
        def evac_w2a(j, t2, bk, gend, tau=tau):
            b.wait("vector", S["tE"], gend)
            ev = b.op("vector", lambda e, bk=bk, j=j, tau=tau:
                      e.scalar_tensor_tensor(
                          St[:, j * T + tau * TT: j * T + (tau + 1) * TT],
                          bk.ap[:, :], vcol("b2", j),
                          St[:, j * T + tau * TT: j * T + (tau + 1) * TT],
                          ALU.add, ALU.add))
            bk.free_evt = (S["vE"], ev)
            x1_ev[0] = ev
        emit_proj(dt["w2a"], lambda k, t2: hidtile(k),
                  evac_w2a, n_k=HCH, n_j=NCH, src_evt=(S["sE"], gelu_ev[HCH - 1]),
                  taus=(tau,), shared_tau=False, mode="f32cast")
    x1_done = (S["vE"], x1_ev[0])
    if stop == "x1":
        dump_f32_full(St, x1_done)
        return finish()

    b.scope('p_w2prep')
    prep_in_used = [None, None, None, None]
    prep_out_used = [None, None, None, None]
    PREP_OUT = [WA[:, 0:2048], WA[:, 2048:4096], UH[:, 0:2048], SST[:, 0:2048]]

    def prep_pass():
        st_ev = None
        for i in range(HCH):
            sl = i % 4
            inslot = NB[:, sl * 2048:(sl + 1) * 2048]
            chn = "prep_in%d" % sl
            b.waite("sync", prep_in_used[sl])
            ld = b.dma("sync", chn, lambda e, i=i, inslot=inslot: e.dma_start(
                out=inslot, in_=w2bf[i * 128:(i + 1) * 128, :]))
            b.waite("tensor", ld)
            out_wa = PREP_OUT[sl]
            ev_ev = None
            for dtile in range(4):
                bk = b.bank(); b.waite("tensor", bk.free_evt)
                g = b.inc("tensor", S["tE"], lambda e, bk=bk, inslot=inslot,
                          dtile=dtile: e.matmul(
                    bk.ap[:, :], H128, inslot[:, dtile * TT:(dtile + 1) * TT],
                    start=True, stop=True))
                eng = "vector" if dtile % 2 == 0 else "scalar"
                b.wait(eng, S["tE"], g)
                if dtile == 0:
                    b.waite("vector", prep_out_used[sl])
                if dtile == 1:
                    b.waite("scalar", prep_out_used[sl])
                if eng == "vector":
                    ev_ev = b.op(eng, lambda e, bk=bk, dtile=dtile, out_wa=out_wa:
                                 e.tensor_copy(out_wa[:, dtile * TT:(dtile + 1) * TT],
                                               bk.ap[:, :]))
                    bk.free_evt = (S["vE"], ev_ev)
                else:
                    ev_s = b.op(eng, lambda e, bk=bk, dtile=dtile, out_wa=out_wa:
                                e.activation(out_wa[:, dtile * TT:(dtile + 1) * TT],
                                             bk.ap[:, :], AF.Identity))
                    bk.free_evt = (S["sE"], ev_s)
            prep_in_used[sl] = (S["tE"], S["tE"].n)
            b.wait("scalar", S["vE"], ev_ev)
            b.wait("scalar", S["sE"], ev_s)
            st = b.dma("scalar", "prep_st%d" % sl, lambda e, i=i, out_wa=out_wa:
                       e.dma_start(
                out=bass.AP(w2t3, i * D, [[128 * D, 64], [64 * D, 2], [1, D]]),
                in_=out_wa))
            prep_out_used[sl] = st
            st_ev = st
        return st_ev

    cast_flush(["w2"])
    for pe_ in cast_piece_evts["w2"]:
        b.waite("sync", pe_)
    prep_bar = (S["tE"], S["tE"].n)
    for _q in range(4):
        prep_in_used[_q] = prep_bar
    b.wait("sync", S["tE"], S["tE"].n)
    b.wait("vector", S["tE"], S["tE"].n)
    b.wait("scalar", S["tE"], S["tE"].n)
    prep_pass()
    for _q in range(4):
        b.waite("sync", prep_out_used[_q])
    p2_end = prep_out_used[3]
    b.wait("vector", S["tE"], S["tE"].n)
    b.wait("scalar", S["tE"], S["tE"].n)
    proj_state["wa_evt"] = [prep_out_used[0], prep_out_used[1]]

    def emit_ffn(x_stop=None):
        out_ev = [None]
        ln_evs = {}
        ln_evs[0] = emit_ln(
            src_ch=lambda c: St[:, c * T: c * T + TT],
            dst_ch=lambda c: NB[:, c * TT:(c + 1) * TT],
            w_key=None, src_f32=True)
        for tau in range(2):
            silu_ev = {}
            h64_ev = {}
            def evac_silu(j, t2, bk, gend):
                b.wait("scalar", S["tE"], gend)
                ev = b.op("scalar", lambda e, bk=bk, j=j: e.activation(
                    hidtile(j), bk.ap[:, :], AF.Silu))
                bk.free_evt = (S["sE"], ev)
                silu_ev[j] = ev
            emit_proj(w1T, lambda k, t2: NB[:, k * TT:(k + 1) * TT],
                      evac_silu, n_k=NCH, n_j=HCH, src_evt=ln_evs[tau],
                      taus=(tau,), shared_tau=False, mode="bf16",
                      w_ready="w1")
            def evac_lin(j, t2, bk, gend):
                b.wait("vector", S["tE"], gend)
                b.wait("vector", S["sE"], silu_ev[j])
                ev = b.op("vector", lambda e, bk=bk, j=j: e.tensor_tensor(
                    hidtile(j), hidtile(j), bk.ap[:, :], ALU.mult))
                bk.free_evt = (S["vE"], ev)
                silu_ev[j] = ev
                bh = b.bank(); b.waite("tensor", bh.free_evt)
                b.wait("tensor", S["vE"], ev)
                g2 = b.inc("tensor", S["tE"], lambda e, bh=bh, j=j: e.matmul(
                    bh.ap[:, :], H64BD, hidtile(j), start=True, stop=True))
                b.wait("scalar", S["tE"], g2)
                ev2 = b.op("scalar", lambda e, bh=bh, j=j: e.activation(
                    hidtile(j), bh.ap[:, :], AF.Identity))
                bh.free_evt = (S["sE"], ev2)
                h64_ev[j] = ev2
            w3_gend = emit_proj(w3T, lambda k, t2: NB[:, k * TT:(k + 1) * TT],
                      evac_lin, n_k=NCH, n_j=HCH, src_evt=ln_evs[tau],
                      taus=(tau,), shared_tau=False, mode="bf16",
                      w_ready="w3")
            if tau == 0:
                ln_evs[1] = emit_ln(
                    src_ch=lambda c: St[:, c * T + TT: c * T + 2 * TT],
                    dst_ch=lambda c: NB[:, c * TT:(c + 1) * TT],
                    w_key=None, src_f32=True,
                    dst_gate=(S["tE"], w3_gend))
            def evac_out(j, t2, bk, gend, tau=tau):
                b.wait("vector", S["tE"], gend)
                ev = b.op("vector", lambda e, bk=bk, j=j, tau=tau: e.tensor_tensor(
                    St[:, j * T + tau * TT: j * T + (tau + 1) * TT],
                    St[:, j * T + tau * TT: j * T + (tau + 1) * TT],
                    bk.ap[:, :], ALU.add))
                bk.free_evt = (S["vE"], ev)
                out_ev[0] = ev
            emit_proj(w2t3, lambda k, t2: hidtile(k),
                      evac_out, n_k=HCH, n_j=NCH,
                      src_evt=(S["sE"], h64_ev[HCH - 1]),
                      taus=(tau,), shared_tau=False, mode="bf16",
                      w_ready=p2_end)
        return (S["vE"], out_ev[0])

    b.scope('p_ffn1')
    x2_done = emit_ffn()
    if stop == "x2":
        dump_f32_full(St, x2_done)
        return finish()

    b.scope('p_attnb_proj')
    ones_ev = b.op("vector", lambda e: e.memset(VBt[:, :], 1.0))

    qr_ev = [None]
    for tau in range(2):
        ln_ev = emit_ln(
            src_ch=lambda c, tau=tau: St[:, c * T + tau * TT: c * T + (tau + 1) * TT],
            dst_ch=lambda c: NB[:, c * TT:(c + 1) * TT],
            w_key=None, src_f32=True)
        for (whbm, dstbuf) in ((dt["wq"], TBu), (dt["wkv"], BB1)):
            def evac_raw(j, t2, bk, gend, dstbuf=dstbuf, tau=tau):
                b.wait("vector", S["tE"], gend)
                ev = b.op("vector", lambda e, bk=bk, j=j, tau=tau, dstbuf=dstbuf:
                          e.tensor_copy(dstbuf[:, j * T + tau * TT: j * T + (tau + 1) * TT],
                                        bk.ap[:, :]))
                bk.free_evt = (S["vE"], ev)
                qr_ev[0] = ev
            emit_proj(whbm, lambda k, t2: NB[:, k * TT:(k + 1) * TT],
                      evac_raw, n_k=NCH, n_j=NCH, src_evt=ln_ev,
                      taus=(tau,), shared_tau=False, mode="f32cast")

    b.scope('p_attnb_norm')
    nrm_state = {"i": 0, "sq": [0, 0], "srow": [0, 0, 0, 0],
                 "rb": [0, 0], "xb": [0, 0]}

    def emit_normalize(buf, h, tau):
        ns = nrm_state
        i = ns["i"]; ns["i"] += 1
        tile = buf[:, h * T + tau * TT: h * T + (tau + 1) * TT]
        sq = XQs[:, (i % 2) * TT:(i % 2 + 1) * TT]
        b.wait("gpsimd", S["tE"], ns["sq"][i % 2])
        if ns.get("src_ev") is not None:
            b.wait("gpsimd", S["vE"], ns["src_ev"])
        sq_ev = b.op("gpsimd", lambda e, sq=sq, tile=tile:
                     e.tensor_tensor(sq, tile, tile, ALU.mult))
        bk = b.bank(); b.waite("tensor", bk.free_evt)
        b.wait("tensor", S["gE"], sq_ev)
        g = b.inc("tensor", S["tE"], lambda e, bk=bk, sq=sq: e.matmul(
            bk.ap[0:1, :], ONESCOL_B, sq, start=True, stop=True))
        ns["sq"][i % 2] = g
        s_row = ROWS[0:1, (i % 4) * TT:(i % 4 + 1) * TT]
        b.wait("scalar", S["tE"], g)
        b.wait("scalar", S["vE"], ns["srow"][i % 4])
        sc_ev = b.op("scalar", lambda e, bk=bk, s_row=s_row: e.activation(
            s_row, bk.ap[0:1, :], AF.Sqrt, scale=128.0))
        bk.free_evt = (S["sE"], sc_ev)
        b.wait("vector", S["sE"], sc_ev)
        b.op("vector", lambda e, s_row=s_row: e.reciprocal(s_row, s_row))
        rb_row = RB[0:1, (i % 2) * TT:(i % 2 + 1) * TT]
        b.wait("vector", S["tE"], ns["rb"][i % 2])
        rb_ev = b.op("vector", lambda e, rb_row=rb_row, s_row=s_row:
                     e.tensor_copy(rb_row, s_row))
        ns["srow"][i % 4] = rb_ev
        bc = b.bank(); b.waite("tensor", bc.free_evt)
        b.wait("tensor", S["vE"], rb_ev)
        g2 = b.inc("tensor", S["tE"], lambda e, bc=bc, rb_row=rb_row: e.matmul(
            bc.ap[:, :], ONESROW_B, rb_row, start=True, stop=True))
        ns["rb"][i % 2] = g2
        b.wait("scalar", S["tE"], g2)
        xb = XBs[:, (i % 2) * TT:(i % 2 + 1) * TT]
        b.waite("scalar", ns["xb_used"][i % 2])
        cp_ev = b.op("scalar", lambda e, bc=bc, xb=xb: e.activation(
            xb, bc.ap[:, :], AF.Identity))
        bc.free_evt = (S["sE"], cp_ev)
        b.wait("vector", S["sE"], cp_ev)
        nrm_ev = b.op("vector", lambda e, tile=tile, xb=xb:
                      e.tensor_tensor(tile, tile, xb, ALU.mult))
        ns["xb_used"][i % 2] = (S["vE"], nrm_ev)
        return nrm_ev

    nrm_state["sq"] = [b.ln_slot[0], b.ln_slot[1]]
    nrm_state["rb"] = [b.rb_consumed, b.rb_consumed]
    nrm_state["srow"] = [S["vE"].n] * 4
    nrm_state["xb_used"] = [(S["vE"], S["vE"].n), (S["vE"], S["vE"].n)]
    nrm_state["src_ev"] = qr_ev[0]
    kv_norm_ev = None
    for h in range(NH):
        for tau in range(2):
            emit_normalize(TBu, h, tau)
            kv_norm_ev = emit_normalize(BB1, h, tau)
    b.ln_slot = [max(nrm_state["sq"]), max(nrm_state["sq"])]
    b.rb_consumed = max(nrm_state["rb"])

    b.waite("vector", cin_ev)
    scan_evs = {}
    for h in range(NH):
        sc_ev = b.op("vector", lambda e, h=h: e.tensor_tensor_scan(
            BB1[:, h * T:(h + 1) * T], VBt[:, :], BB1[:, h * T:(h + 1) * T],
            0.0, ALU.mult, ALU.add))
        car_ev = b.op("vector", lambda e, h=h: e.tensor_copy(
            CAR[:, h:h + 1], BB1[:, h * T + T - 1: h * T + T]))
        scan_evs[h] = car_ev

    b.wait("gpsimd", S["vE"], scan_evs[NH - 1])
    b.wait("gpsimd", S["vE"], hinb_ev)
    b.wait("sync", S["vE"], scan_evs[NH - 1])
    cin2 = b.dma("sync", "ccio", lambda e: e.dma_start(
        out=cc2_in.ap(), in_=CAR[:, :]))
    b.waite("gpsimd", cin2)
    cc2_ev = b.inc("gpsimd", S["cc"], lambda e: e.collective_compute(
        "AllGather", ALU.bypass, replica_groups=[list(range(8))],
        ins=[cc2_in.ap().opt()], outs=[cc2_out.ap().opt()]))
    b.wait("gpsimd", S["cc"], cc2_ev)
    b.wait("sync", S["cc"], cc2_ev)
    b.wait("sync", S["vE"], hinb_ev)
    gg2 = b.dma("sync", "ccio", lambda e: e.dma_start(
        out=GG[:, :].rearrange("p (r h) -> p r h", r=8),
        in_=cc2_out.ap().rearrange("(r i) h -> i r h", i=128)))
    b.waite("vector", gg2)
    b.op("vector", lambda e: e.tensor_scalar(
        HIN[:, :], GG[:, 0:NH], MASK[:, 0:1], None, ALU.mult))
    for r in range(1, 8):
        b.op("vector", lambda e, r=r: e.scalar_tensor_tensor(
            HIN[:, :], GG[:, r * NH:(r + 1) * NH], MASK[:, r:r + 1], HIN[:, :],
            ALU.mult, ALU.add))
    addc_ev = {}
    for h in range(NH):
        addc_ev[h] = b.op("vector", lambda e, h=h: e.tensor_scalar(
            BB1[:, h * T:(h + 1) * T], BB1[:, h * T:(h + 1) * T],
            HIN[:, h:h + 1], None, ALU.add))

    b.scope('p_attnb_fft')
    val_ev = None
    for h in range(NH):
        for tau in range(2):
            qt = TBu[:, h * T + tau * TT: h * T + (tau + 1) * TT]
            kt = BB1[:, h * T + tau * TT: h * T + (tau + 1) * TT]
            fs = [XS0[:, 0:TT], XS0[:, TT:2 * TT], XS0[:, 2 * TT:3 * TT],
                  XS1[:, 0:TT], XS1[:, TT:2 * TT]]
            banks = []
            b.wait("tensor", S["vE"], addc_ev[h])
            for (lhs, src) in ((DFTC, qt), (DFTS, qt), (DFTC, kt), (DFTS, kt)):
                bk = b.bank(); b.waite("tensor", bk.free_evt)
                g = b.inc("tensor", S["tE"], lambda e, bk=bk, lhs=lhs, src=src:
                          e.matmul(bk.ap[:, :], lhs, src, start=True, stop=True))
                banks.append((bk, g))
            for i, (bk, g) in enumerate(banks):
                b.wait("vector", S["tE"], g)
                ev = b.op("vector", lambda e, bk=bk, i=i: e.tensor_copy(
                    fs[i], bk.ap[:, :]))
                bk.free_evt = (S["vE"], ev)
            b.op("vector", lambda e: e.tensor_tensor(fs[4], fs[1], fs[2], ALU.mult))
            b.op("vector", lambda e: e.tensor_tensor(fs[2], fs[0], fs[2], ALU.mult))
            b.op("vector", lambda e: e.tensor_tensor(fs[0], fs[0], fs[3], ALU.mult))
            b.op("vector", lambda e: e.tensor_tensor(fs[3], fs[1], fs[3], ALU.mult))
            b.op("vector", lambda e: e.tensor_tensor(fs[2], fs[2], fs[3], ALU.subtract))
            pr_ev = b.op("vector", lambda e: e.tensor_tensor(fs[0], fs[0], fs[4], ALU.add))
            bk = b.bank(); b.waite("tensor", bk.free_evt)
            b.wait("tensor", S["vE"], pr_ev)
            b.run("tensor", lambda e, bk=bk: e.matmul(
                bk.ap[:, :], IDFTC, fs[2], start=True, stop=False))
            g = b.inc("tensor", S["tE"], lambda e, bk=bk: e.matmul(
                bk.ap[:, :], IDFTS, fs[0], start=False, stop=True))
            b.wait("vector", S["tE"], g)
            val_ev = b.op("vector", lambda e, bk=bk, qt=qt: e.tensor_copy(
                qt, bk.ap[:, :]))
            bk.free_evt = (S["vE"], val_ev)

    b.scope('p_attnb_wo')
    x3_ev = [None]
    for tau in range(2):
        def evac_wo(j, t2, bk, gend, tau=tau):
            b.wait("vector", S["tE"], gend)
            ev = b.op("vector", lambda e, bk=bk, j=j, tau=tau: e.tensor_tensor(
                St[:, j * T + tau * TT: j * T + (tau + 1) * TT],
                St[:, j * T + tau * TT: j * T + (tau + 1) * TT],
                bk.ap[:, :], ALU.add))
            bk.free_evt = (S["vE"], ev)
            x3_ev[0] = ev
        emit_proj(dt["wo"], lambda k, t2, tau=tau: TBu[:, k * T + tau * TT: k * T + (tau + 1) * TT],
                  evac_wo, n_k=NCH, n_j=NCH, src_evt=(S["vE"], val_ev),
                  taus=(tau,), shared_tau=False, mode="f32cast")
    x3_done = (S["vE"], x3_ev[0])
    if stop == "x3":
        dump_f32_full(St, x3_done)
        return finish()

    b.scope('p_ffn2')
    x4_done = emit_ffn()
    if stop == "x4":
        dump_f32_full(St, x4_done)
        return finish()

    b.scope('p_out')
    b.waite("sync", x4_done)
    out_ev = b.dma("sync", "out", lambda e: e.dma_start(
        out=y_out.ap().rearrange("(c p) t -> p c t", p=128),
        in_=St[:, :].rearrange("p (c t) -> p c t", c=NCH)))
    b.waite("sync", out_ev)
    return finish()


def _consts_b():
    blocks = []
    blocks.append(np.ones((128, 128), np.float32))
    blocks.append(np.eye(128, dtype=np.float32))
    def hadamard(n):
        h = np.array([[1.0]], np.float32)
        while h.shape[0] < n:
            h = np.block([[h, h], [h, -h]])
        return h
    blocks.append(hadamard(128))
    h64 = hadamard(64) / np.sqrt(8192.0)
    bd = np.zeros((128, 128), np.float32)
    bd[:64, :64] = h64; bd[64:, 64:] = h64
    blocks.append(bd)
    j = np.arange(128)
    ang = 2.0 * np.pi * np.outer(j, j) / 128.0
    C = np.cos(ang).astype(np.float32); Sm = np.sin(ang).astype(np.float32)
    blocks.append(C); blocks.append(Sm)
    blocks.append(C / 128.0); blocks.append(Sm / 128.0)
    blocks.append(np.zeros((128, 128), np.float32))
    blocks.append(np.zeros((128, 128), np.float32))
    return np.concatenate(blocks, 1).astype(ml_dtypes.bfloat16)


def _hid_perm():
    j = np.arange(HID)
    return (j % 64) * 128 + 2 * (j // 128) + (j % 128) // 64


def make_inputs(full):
    g = lambda k: np.asarray(full[k])
    x = g("L0_attn_a_x")
    vkeys = [g("L0_attn_a_ln1_w"), g("L0_attn_a_ln1_b"),
             g("L0_attn_a_ln2_w"), g("L0_attn_a_ln2_b"),
             g("L0_attn_a_in_b"), g("L0_attn_a_out_b"), g("L0_attn_a_b2")]
    vcols = np.concatenate(
        [np.asarray(v, np.float32).reshape(16, 128).T for v in vkeys], 1)
    b1cols = np.ascontiguousarray(
        g("L0_attn_a_b1").astype(np.float32).reshape(64, 128).T)
    A_ = g("L0_attn_a_A"); Bm_ = g("L0_attn_a_Bm")
    Cm_ = g("L0_attn_a_Cm"); Dm_ = g("L0_attn_a_Dm")
    blocks = []
    for h in range(NH):
        blocks += [A_[h], A_[h].T, Bm_[h].T, Cm_[h].T, Dm_[h].T]
    ssm = np.ascontiguousarray(np.concatenate(blocks, 1), np.float32)
    shared = dict(
        in_w=np.ascontiguousarray(g("L0_attn_a_in_w"), np.float32),
        out_w=np.ascontiguousarray(g("L0_attn_a_out_w"), np.float32),
        w1a=np.ascontiguousarray(g("L0_attn_a_w1"), np.float32),
        w2a=np.ascontiguousarray(g("L0_attn_a_w2"), np.float32),
        ssm_mats=ssm,
        wq=np.ascontiguousarray(g("L0_attn_b_W_q").T, np.float32),
        wkv=np.ascontiguousarray(g("L0_attn_b_W_kv").T, np.float32),
        wo=np.ascontiguousarray(g("L0_attn_b_W_o").T, np.float32),
        fw1=np.ascontiguousarray(g("L0_ffn_W1").T[:, _hid_perm()], np.float32),
        fw2=np.ascontiguousarray(g("L0_ffn_W2").T, np.float32),
        fw3=np.ascontiguousarray(g("L0_ffn_W3").T[:, _hid_perm()], np.float32),
        vcolsin=np.ascontiguousarray(vcols, np.float32),
        b1colsin=b1cols, consts_b=_consts_b(),
        onescol_f=np.ones((128, 1), np.float32),
    )
    in_maps = []
    for core in range(8):
        bb, half = core // 2, core % 2
        xs = np.ascontiguousarray(x[bb, half * T:(half + 1) * T, :].T.astype(np.float32))
        maskc = np.zeros((128, 8), np.float32)
        if half == 1:
            maskc[:, core - 1] = 1.0
        m = dict(shared)
        m["x0"] = xs
        m["maskc"] = maskc
        in_maps.append(m)
    return in_maps


def kernel(**inputs):
    in_maps = make_inputs(inputs)
    nc = build_program(stop=None)
    res = run_bass_kernel_spmd(nc, in_maps, core_ids=list(range(8)))
    out = np.empty((4, 2048, 2048), np.float32)
    for core in range(8):
        bb, half = core // 2, core % 2
        out[bb, half * T:(half + 1) * T, :] = res.results[core]["y"].T
    return out


# version 20
# speedup vs baseline: 1.0401x; 1.0008x over previous
import numpy as np
import ml_dtypes
from contextlib import ExitStack

import concourse.bass as bass
import concourse.mybir as mybir
from concourse.bass_utils import run_bass_kernel_spmd

F32 = mybir.dt.float32
BF16 = mybir.dt.bfloat16
AF = mybir.ActivationFunctionType
ALU = mybir.AluOpType

D = 2048
T = 1024
TT = 512
NCH = 16
NH = 16
HD = 128
HID = 8192
HCH = 64
EPS = 1e-5
PAD = 512

ENGS = ("tensor", "vector", "scalar", "sync", "gpsimd")


class Sem:
    def __init__(self, stack, nc, name):
        self.h = stack.enter_context(nc.semaphore(name))
        self.n = 0
        self.name = name


class Bank:
    def __init__(self, ap, idx):
        self.ap = ap
        self.idx = idx
        self.free_evt = None


class Builder:
    def __init__(self, nc, stack):
        self.nc = nc
        self.stack = stack
        self.plan = {e: [] for e in ENGS}
        self.wm = {e: {} for e in ENGS}
        self.sem = {k: Sem(stack, nc, k) for k in
                    ("tE", "vE", "sE", "gE", "cc")}
        self.chans = {}
        self.cur_scope = "p0"
        self.ln_slot = [0, 0]
        self.tmp_slot = [(None, 0), (None, 0)]
        self.rb_consumed = 0

    def scope(self, name):
        self.cur_scope = name

    def em(self, eng, fn):
        self.plan[eng].append((self.cur_scope, fn))

    def wait(self, eng, sem, val):
        if val is None or val <= 0:
            return
        if self.wm[eng].get(sem.name, 0) >= val:
            return
        self.wm[eng][sem.name] = val
        self.plan[eng].append((self.cur_scope, lambda e, s=sem.h, v=val: e.wait_ge(s, v)))

    def waite(self, eng, evt):
        if evt is not None and evt[0] is not None:
            self.wait(eng, evt[0], evt[1])

    def chan(self, name):
        if name not in self.chans:
            self.chans[name] = Sem(self.stack, self.nc, "ch_" + name)
        return self.chans[name]

    def dma(self, eng, chname, fn):
        ch = self.chan(chname)
        ch.n += 16
        v = ch.n
        self.plan[eng].append((self.cur_scope, lambda e, f=fn, s=ch.h: f(e).then_inc(s, 16)))
        return (ch, v)

    ENGSEM = {"vector": "vE", "scalar": "sE", "gpsimd": "gE"}

    def op(self, eng, fn):
        sem = self.sem[self.ENGSEM[eng]]
        if sem.n > 0:
            self.wait(eng, sem, sem.n)
        sem.n += 1
        v = sem.n
        self.plan[eng].append((self.cur_scope, lambda e, f=fn, s=sem.h: f(e).then_inc(s, 1)))
        return v

    def inc(self, eng, sem, fn, k=1):
        sem.n += k
        v = sem.n
        self.plan[eng].append((self.cur_scope, lambda e, f=fn, s=sem.h, kk=k: f(e).then_inc(s, kk)))
        return v

    def run(self, eng, fn):
        self.plan[eng].append((self.cur_scope, fn))

    def flush(self, block):
        nc = self.nc
        for eng in ENGS:
            lst = self.plan[eng]
            if not lst:
                continue
            def body(e, lst=lst):
                cur = None
                cur_id = None
                for sc, fn in lst:
                    if sc != cur:
                        if cur is not None:
                            nc.leave_named_scope(cur, cur_id, False)
                        if sc is not None:
                            cur_id, _ = nc.enter_named_scope(sc, False)
                        cur = sc
                    fn(e)
                if cur is not None:
                    nc.leave_named_scope(cur, cur_id, False)
            getattr(block, eng)(body)

    def init_psum(self):
        self.banks = []
        for i in range(8):
            t = self.stack.enter_context(
                self.nc.psum_tensor("ps%d" % i, [128, TT], F32))
            self.banks.append(Bank(t, i))
        self._rr = 0

    def bank(self):
        bk = self.banks[self._rr]
        self._rr = (self._rr + 1) % 8
        return bk


def build_program(stop=None):
    nc = bass.Bass("TRN2", target_bir_lowering=False)
    stack = ExitStack()
    b = Builder(nc, stack)
    S = b.sem

    dt = {}
    def din(name, shape, dtype=F32):
        dt[name] = nc.dram_tensor(name, shape, dtype, kind="ExternalInput")
        return dt[name]
    def dint(name, shape, dtype=BF16):
        dt[name] = nc.dram_tensor(name, shape, dtype)
        return dt[name]

    x0h = din("x0", [D, T])
    din("in_w", [D, D]); din("out_w", [D, D])
    din("w1a", [D, HID]); din("w2a", [HID, D])
    din("ssm_mats", [128, NH * 5 * HD])
    din("wq", [D, D]); din("wkv", [D, D]); din("wo", [D, D])
    din("fw1", [D, HID]); din("fw2", [HID, D]); din("fw3", [D, HID])
    din("vcolsin", [128, 7 * 16])
    din("b1colsin", [128, 64])
    din("maskc", [128, 8])
    din("consts_b", [128, 10 * 128], BF16)
    din("onescol_f", [128, 1])

    y_out = nc.dram_tensor("y", [D, T], F32, kind="ExternalOutput")
    dbg = nc.dram_tensor("dbg", [D, T], F32, kind="ExternalOutput")

    ssmb = dint("ssm_bf", [128, NH * 5 * HD])
    w1T = dint("w1T", [D, HID]); w3T = dint("w3T", [D, HID])
    pw_dram = dint("pw_dram", [NH * HD, 9 * 128])
    u_dram = dint("u_dram", [D, T])
    s_dram = dint("s_dram", [D, T])
    w2t3 = dint("w2t3", [HID, D])
    cc1_in = nc.dram_tensor("cc1_in", [128, 16], F32)
    cc1_out = nc.dram_tensor("cc1_out", [8 * 128, 16], F32)
    cc2_in = nc.dram_tensor("cc2_in", [128, 16], F32)
    cc2_out = nc.dram_tensor("cc2_out", [8 * 128, 16], F32)

    sbuf = lambda name, shape, dtype: stack.enter_context(
        nc.sbuf_tensor(name, shape, dtype))

    St = sbuf("S", [128, NCH * T], F32)
    BB1 = sbuf("BB1", [128, NCH * T], BF16)
    TBu = sbuf("TBu", [128, NCH * T], BF16)
    NB = sbuf("NB", [128, NCH * TT], BF16)
    WA = sbuf("WA", [128, 4096], BF16)
    UH = sbuf("UH", [128, 2 * T], BF16)
    SST = sbuf("SST", [128, 2 * T], BF16)
    XBs = sbuf("XBs", [128, 2 * TT], BF16)
    XQs = sbuf("XQs", [128, 2 * TT], BF16)
    TMP = sbuf("TMP", [128, 2 * TT], F32)
    XT = sbuf("XT", [128, 2 * TT], F32)
    ROWS = sbuf("ROWS", [1, 4 * TT], F32)
    RB = sbuf("RB", [1, 2 * TT], BF16)
    SSMW = sbuf("SSMW", [128, 46 * 128], BF16)
    XS0 = sbuf("XS0", [128, PAD + T], BF16)
    XS1 = sbuf("XS1", [128, PAD + T], BF16)
    VBt = sbuf("VBt", [128, T], BF16)
    CAR = sbuf("CAR", [128, NH], F32)
    CARB = sbuf("CARB", [128, NH], BF16)
    GG = sbuf("GG", [128, 8 * NH], F32)
    HIN = sbuf("HIN", [128, NH], F32)
    VCOLS = sbuf("VCOLS", [128, 7 * 16], F32)
    B1C = sbuf("B1C", [128, 64], F32)
    MASK = sbuf("MASK", [128, 8], F32)
    CONS = sbuf("CONS", [128, 10 * 128], BF16)
    ONEC = sbuf("ONEC", [128, 1], F32)
    CONSTF = sbuf("CONSTF", [128, 4], F32)

    b.init_psum()

    def cons(i):
        return CONS[:, i * 128:(i + 1) * 128]
    ONESROW_B = cons(0)[0:1, :]
    ONESCOL_B = cons(0)[:, 0:1]
    I128 = cons(1)
    H128 = cons(2)
    H64BD = cons(3)
    DFTC = cons(4)
    DFTS = cons(5)
    IDFTC = cons(6)
    IDFTS = cons(7)

    VKEYS = ["ln1w", "ln1b", "ln2w", "ln2b", "in_b", "out_b", "b2"]
    def vcol(key, c):
        i = VKEYS.index(key)
        return VCOLS[:, i * 16 + c: i * 16 + c + 1]
    def b1col(c):
        return B1C[:, c:c + 1]

    x0_lds = []
    for xp in range(4):
        x0_lds.append(b.dma("sync", "x0_%d" % xp, lambda e, xp=xp: e.dma_start(
            out=St[:, xp * 4 * T:(xp + 1) * 4 * T].rearrange("p (c t) -> p c t", c=4),
            in_=x0h[xp * 512:(xp + 1) * 512, :].rearrange("(c p) t -> p c t", p=128))))
    x0_ld = x0_lds[-1]
    b.dma("sync", "cons", lambda e: e.dma_start(
        out=CONS[:, :], in_=dt["consts_b"].ap()))
    b.dma("sync", "cons", lambda e: e.dma_start(
        out=MASK[:, :], in_=dt["maskc"].ap()))
    b.dma("sync", "cons", lambda e: e.dma_start(
        out=ONEC[:, :], in_=dt["onescol_f"].ap()))
    b.dma("sync", "cons", lambda e: e.dma_start(
        out=VCOLS[:, :], in_=dt["vcolsin"].ap()))
    cons_grp = b.dma("sync", "cons", lambda e: e.dma_start(
        out=B1C[:, :], in_=dt["b1colsin"].ap()))
    b.op("vector", lambda e: e.memset(CONSTF[:, 0:1], float(EPS)))
    b.op("vector", lambda e: e.memset(CONSTF[:, 1:2], 0.0))
    constf_ev = b.op("vector", lambda e: e.memset(CONSTF[:, 2:3], 1.0))
    EPSC = CONSTF[0:1, 0:1]

    cast_evts = {}
    cast_piece_evts = {}
    cast_queue = []
    cast_ch_state = {"i": 0, "last": [None] * 6}
    def cast_chunked(name, src, dst, rows, piece):
        for r0 in range(0, rows, piece):
            r1 = min(rows, r0 + piece)
            cast_queue.append((name, src, dst, r0, r1))

    def cast_pump(n=1, pace=True):
        for _ in range(n):
            if not cast_queue:
                return
            name, srct, dstt, r0, r1 = cast_queue.pop(0)
            if pace:
                b.wait("gpsimd", S["tE"], S["tE"].n)
            ci = cast_ch_state["i"]; cast_ch_state["i"] = (ci + 1) % 6
            b.waite("gpsimd", cast_ch_state["last"][ci])
            ev = b.dma("gpsimd", "cast%d" % ci,
                       lambda e, r0=r0, r1=r1, srct=srct, dstt=dstt:
                       e.dma_start(out=dstt[r0:r1, :], in_=srct[r0:r1, :]))
            cast_ch_state["last"][ci] = ev
            cast_evts[name] = ev
            cast_piece_evts.setdefault(name, []).append(ev)

    def cast_flush(names):
        while any(q[0] in names for q in cast_queue):
            cast_pump(1)

    cast_chunked("ssm", dt["ssm_mats"], ssmb, 128, 128)
    cast_pump(1, pace=False)
    for _name, _s, _d, _r, _p in (
            ("w1", dt["fw1"], w1T, D, 256),
            ("w3", dt["fw3"], w3T, D, 256)):
        cast_chunked(_name, _s, _d, _r, _p)

    def emit_ln(src_ch, dst_ch, w_key=None, src_f32=True, src_gate=None,
                dst_gate=None):
        stats_g = [None] * NCH
        bm = b.bank(); bs = b.bank()
        b.waite("tensor", bm.free_evt); b.waite("tensor", bs.free_evt)
        for c in range(NCH):
            sl = c % 2
            xb_ap = XBs[:, sl * TT:(sl + 1) * TT]
            xq_ap = XQs[:, sl * TT:(sl + 1) * TT]
            if c >= 2:
                b.wait("vector", S["tE"], stats_g[c - 2])
            else:
                b.wait("vector", S["tE"], b.ln_slot[sl])
            if src_gate is not None:
                b.waite("vector", src_gate(c))
            if src_f32:
                b.op("vector", lambda e, c=c, xb=xb_ap: e.tensor_copy(xb, src_ch(c)))
                ev = b.op("vector", lambda e, xb=xb_ap, xq=xq_ap:
                           e.tensor_tensor(xq, xb, xb, ALU.mult))
                stats_src = xb_ap
            else:
                ev = b.op("vector", lambda e, c=c, xq=xq_ap:
                           e.tensor_tensor(xq, src_ch(c), src_ch(c), ALU.mult))
                stats_src = src_ch(c)
            b.wait("tensor", S["vE"], ev)
            first = (c == 0); last = (c == NCH - 1)
            b.run("tensor", lambda e, s=stats_src, bm=bm, f=first, l=last:
                  e.matmul(bm.ap[0:1, :], ONESCOL_B, s, start=f, stop=l))
            stats_g[c] = b.inc("tensor", S["tE"],
                               lambda e, xq=xq_ap, bs=bs, f=first, l=last:
                               e.matmul(bs.ap[0:1, :], ONESCOL_B, xq, start=f, stop=l))
        b.ln_slot[0] = stats_g[NCH - 2]
        b.ln_slot[1] = stats_g[NCH - 1]
        g_stats = stats_g[NCH - 1]

        m_row = ROWS[0:1, 0:TT]; r_row = ROWS[0:1, TT:2 * TT]
        q_row = ROWS[0:1, 2 * TT:3 * TT]; s_row = ROWS[0:1, 3 * TT:4 * TT]
        b.wait("vector", S["tE"], g_stats)
        b.op("vector", lambda e, bm=bm: e.tensor_scalar_mul(m_row, bm.ap[0:1, 0:TT], 1.0 / D))
        b.op("vector", lambda e, bs=bs: e.tensor_scalar_mul(s_row, bs.ap[0:1, 0:TT], 1.0 / D))
        b.op("vector", lambda e: e.tensor_tensor(q_row, m_row, m_row, ALU.mult))
        ev = b.op("vector", lambda e: e.tensor_tensor(s_row, s_row, q_row, ALU.subtract))
        bm.free_evt = (S["vE"], ev); bs.free_evt = (S["vE"], ev)
        b.wait("scalar", S["vE"], max(ev, constf_ev))
        ev = b.op("scalar", lambda e: e.activation(
            s_row, s_row, AF.Sqrt, bias=EPSC))
        b.wait("vector", S["sE"], ev)
        b.op("vector", lambda e: e.reciprocal(r_row, s_row))
        b.op("vector", lambda e: e.tensor_tensor(q_row, m_row, r_row, ALU.mult))
        rb_row = RB[0:1, 0:TT]; qb_row = RB[0:1, TT:2 * TT]
        b.wait("vector", S["tE"], b.rb_consumed)
        b.op("vector", lambda e: e.tensor_copy(rb_row, r_row))
        ev = b.op("vector", lambda e: e.tensor_copy(qb_row, q_row))

        br = b.bank(); bq = b.bank()
        b.waite("tensor", br.free_evt); b.waite("tensor", bq.free_evt)
        b.wait("tensor", S["vE"], ev)
        b.run("tensor", lambda e, br=br: e.matmul(br.ap[:, :], ONESROW_B, rb_row,
                                                  start=True, stop=True))
        g_bc = b.inc("tensor", S["tE"], lambda e, bq=bq: e.matmul(
            bq.ap[:, :], ONESROW_B, qb_row, start=True, stop=True))
        b.rb_consumed = g_bc
        b.wait("vector", S["tE"], g_bc)
        if dst_gate is not None:
            b.waite("vector", dst_gate)
            if w_key is not None:
                b.waite("scalar", dst_gate)
        last_ev = None
        for c in range(NCH):
            sl = c % 2
            t1 = TMP[:, sl * TT:(sl + 1) * TT]
            b.waite("vector", b.tmp_slot[sl])
            b.op("vector", lambda e, c=c, t1=t1, br=br:
                  e.tensor_tensor(t1, src_ch(c), br.ap[:, :], ALU.mult))
            if w_key is None:
                ev2 = b.op("vector", lambda e, c=c, t1=t1, bq=bq:
                            e.tensor_tensor(dst_ch(c), t1, bq.ap[:, :], ALU.subtract))
                b.tmp_slot[sl] = (S["vE"], ev2)
                last = (S["vE"], ev2)
            else:
                ev2 = b.op("vector", lambda e, c=c, t1=t1, bq=bq:
                            e.tensor_tensor(t1, t1, bq.ap[:, :], ALU.subtract))
                b.wait("scalar", S["vE"], ev2)
                ev3 = b.op("scalar", lambda e, c=c, t1=t1:
                            e.activation(dst_ch(c), t1, AF.Identity,
                                         bias=vcol(w_key[1], c), scale=vcol(w_key[0], c)))
                b.tmp_slot[sl] = (S["sE"], ev3)
                last = (S["sE"], ev3)
            last_ev = last
        br.free_evt = last_ev if w_key is None else (S["vE"], ev2)
        bq.free_evt = br.free_evt
        return last_ev

    stopped = [False]

    def dump_bf16_full(buf, done_evt):
        b.waite("vector", done_evt)
        st_ev = [None, None]
        for c in range(NCH):
            for tau in range(2):
                sl = (2 * c + tau) % 2
                xt = XT[:, sl * TT:(sl + 1) * TT]
                b.waite("vector", st_ev[sl])
                ev = b.op("vector", lambda e, c=c, tau=tau, xt=xt:
                           e.tensor_copy(xt, buf[:, c * T + tau * TT: c * T + (tau + 1) * TT]))
                b.wait("scalar", S["vE"], ev)
                st_ev[sl] = b.dma("scalar", "dmp%d" % sl, lambda e, c=c, tau=tau, xt=xt:
                              e.dma_start(out=dbg[c * 128:(c + 1) * 128,
                                                  tau * TT:(tau + 1) * TT], in_=xt))
        b.waite("scalar", st_ev[0])
        b.waite("scalar", st_ev[1])
        stopped[0] = True

    def dump_f32_full(buf, done_evt):
        b.waite("sync", done_evt)
        ev = b.dma("sync", "out", lambda e: e.dma_start(
            out=dbg.ap().rearrange("(c p) t -> p c t", p=128),
            in_=buf[:, :].rearrange("p (c t) -> p c t", c=NCH)))
        b.waite("sync", ev)
        stopped[0] = True

    def finish():
        with nc.Block() as block:
            b.flush(block)
        stack.close()
        return nc

    b.scope('p1_ln0')
    b.waite("tensor", cons_grp)
    for tau in range(2):
        h1_done = emit_ln(
            src_ch=lambda c, tau=tau: St[:, c * T + tau * TT: c * T + (tau + 1) * TT],
            dst_ch=lambda c, tau=tau: BB1[:, c * T + tau * TT: c * T + (tau + 1) * TT],
            src_f32=True, src_gate=lambda c: x0_lds[c // 4])
    if stop == "h1":
        dump_bf16_full(BB1, h1_done)
        return finish()

    b.scope('p2_ln1')
    for tau in range(2):
        t_done = emit_ln(
            src_ch=lambda c, tau=tau: BB1[:, c * T + tau * TT: c * T + (tau + 1) * TT],
            dst_ch=lambda c, tau=tau: TBu[:, c * T + tau * TT: c * T + (tau + 1) * TT],
            w_key=("ln1w", "ln1b"), src_f32=False)
    if stop == "t":
        dump_bf16_full(TBu, t_done)
        return finish()

    proj_state = {"wa_evt": [None, None], "ld_i": 0}

    def emit_proj(w_hbm, src_ch, evac, n_k=NCH, n_j=NCH, src_evt=None,
                  taus=(0, 1), shared_tau=True, mode="f32", w_ready=None):
        ps = proj_state
        if isinstance(w_ready, str):
            for pe_ in cast_piece_evts[w_ready]:
                b.waite("sync", pe_)
        elif w_ready is not None:
            b.waite("sync", w_ready)
        jg_size = 2 if shared_tau else 4
        tau_groups = [list(taus)] if shared_tau else [[t] for t in taus]
        gend = None
        for tg in tau_groups:
            for jg in range(n_j // jg_size):
                cast_pump(1)
                banks = {}
                for jj in range(jg_size):
                    for tau in tg:
                        bk = b.bank()
                        b.waite("tensor", bk.free_evt)
                        banks[(jj, tau)] = bk
                if src_evt is not None:
                    b.waite("tensor", src_evt)
                jw = jg_size * 128
                if mode in ("bf16", "f32cast"):
                    assert n_k % 4 == 0 and jw in (256, 512)
                    eng_q = "sync" if mode == "bf16" else "gpsimd"
                    chp = "wa" if mode == "bf16" else "wacast"
                    for k4 in range(n_k // 4):
                        sl = ps["ld_i"] % 2
                        ps["ld_i"] += 1
                        wa = WA[:, sl * 2048: sl * 2048 + 4 * jw]
                        b.waite(eng_q, ps["wa_evt"][sl])
                        ld = b.dma(eng_q, chp + "%d" % sl,
                                   lambda e, k4=k4, jg=jg, wa=wa, jw=jw:
                                   e.dma_start(
                                       out=wa.rearrange("p (k c) -> p k c", k=4),
                                       in_=w_hbm[k4 * 512:(k4 + 1) * 512,
                                                 jg * jw:(jg + 1) * jw]
                                       .rearrange("(k p) c -> p k c", p=128)))
                        b.waite("tensor", ld)
                        gk = None
                        for dk in range(4):
                            k = k4 * 4 + dk
                            first = (k == 0); last = (k == n_k - 1)
                            for jj in range(jg_size):
                                for tau in tg:
                                    bk = banks[(jj, tau)]
                                    fn = (lambda e, bk=bk, wa=wa, jj=jj, k=k, dk=dk,
                                          tau=tau, f=first, l=last:
                                          e.matmul(bk.ap[:, :],
                                                   wa[:, dk * jw + jj * 128: dk * jw + (jj + 1) * 128],
                                                   src_ch(k, tau), start=f, stop=l))
                                    if dk == 3 and jj == jg_size - 1 and tau == tg[-1]:
                                        gk = b.inc("tensor", S["tE"], fn)
                                    else:
                                        b.run("tensor", fn)
                        ps["wa_evt"][sl] = (S["tE"], gk)
                else:
                    for k in range(n_k):
                        sl = ps["ld_i"] % 2
                        ps["ld_i"] += 1
                        wa = WA[:, sl * 2048: sl * 2048 + jw]
                        b.waite("sync", ps["wa_evt"][sl])
                        ld = b.dma("sync", "wa%d" % sl, lambda e, k=k, jg=jg, wa=wa, jw=jw:
                                   e.dma_start(out=wa,
                                               in_=w_hbm[jg * jw:(jg + 1) * jw,
                                                         k * 128:(k + 1) * 128],
                                               transpose=True))
                        b.waite("tensor", ld)
                        first = (k == 0); last = (k == n_k - 1)
                        gk = None
                        for jj in range(jg_size):
                            for tau in tg:
                                bk = banks[(jj, tau)]
                                fn = (lambda e, bk=bk, wa=wa, jj=jj, k=k, tau=tau,
                                      f=first, l=last:
                                      e.matmul(bk.ap[:, :], wa[:, jj * 128:(jj + 1) * 128],
                                               src_ch(k, tau), start=f, stop=l))
                                if jj == jg_size - 1 and tau == tg[-1]:
                                    gk = b.inc("tensor", S["tE"], fn)
                                else:
                                    b.run("tensor", fn)
                        ps["wa_evt"][sl] = (S["tE"], gk)
                gend = gk
                for jj in range(jg_size):
                    for tau in tg:
                        evac(jg * jg_size + jj, tau, banks[(jj, tau)], gend)
        return gend

    b.scope('p3_inproj')
    u_st = [None]
    def evac_inproj(j, tau, bk, gend):
        sl = j % 2
        dst = UH[:, sl * T + tau * TT: sl * T + (tau + 1) * TT]
        b.wait("scalar", S["tE"], gend)
        b.waite("scalar", u_st[0])
        ev = b.op("scalar", lambda e, bk=bk, dst=dst, j=j:
                   e.activation(dst, bk.ap[:, :], AF.Identity, bias=vcol("in_b", j)))
        bk.free_evt = (S["sE"], ev)
        if tau == 1:
            b.wait("scalar", S["sE"], ev)
            u_st[0] = b.dma("scalar", "ust", lambda e, j=j, sl=sl:
                            e.dma_start(out=u_dram[j * 128:(j + 1) * 128, :],
                                        in_=UH[:, sl * T:(sl + 1) * T]))

    emit_proj(dt["in_w"],
              lambda k, tau: TBu[:, k * T + tau * TT: k * T + (tau + 1) * TT],
              evac_inproj, src_evt=t_done, mode="f32cast")
    if stop == "u":
        b.waite("sync", u_st[0])
        ld = b.dma("sync", "out", lambda e: e.dma_start(
            out=BB1[:, :].rearrange("p (c t) -> p c t", c=NCH),
            in_=u_dram.ap().rearrange("(c p) t -> p c t", p=128)))
        b.waite("vector", ld)
        dump_bf16_full(BB1, None)
        return finish()

    b.scope('p4_scan')
    def pwp(par, i):
        off = 23 * par
        return SSMW[:, (off + i) * 128:(off + i + 1) * 128]

    ssmw_used = [0, 0]
    pw_store_ref = [None]
    xs_store = [None]

    b.op("vector", lambda e: e.memset(XS0[:, 0:PAD], 0.0))
    pad_ev = b.op("vector", lambda e: e.memset(XS1[:, 0:PAD], 0.0))
    b.wait("tensor", S["vE"], pad_ev)

    def ssm_head_loads(h, which):
        par = h % 2
        b.wait("scalar", S["tE"], ssmw_used[par])
        b.waite("scalar", cast_evts["ssm"])
        chn = "ssmw%d" % par
        off = 23 * par
        evs = []
        evs.append(b.dma("scalar", chn, lambda e, h=h, off=off: e.dma_start(
            out=SSMW[:, off * 128:(off + 5) * 128],
            in_=ssmb[:, h * 640:(h + 1) * 640])))
        if which == "corr":
            evs.append(b.dma("scalar", chn, lambda e, h=h, off=off: e.dma_start(
                out=SSMW[:, (off + 5) * 128:(off + 14) * 128],
                in_=pw_dram[h * 128:(h + 1) * 128, :])))
        return evs[-1]

    def ssm_powers_stepper(h, ld_ev):
        par = h % 2
        pw = lambda i: pwp(par, i)
        st = {"k": 1, "prev_ev": None}

        def step(n=1):
            for _ in range(n):
                k = st["k"]
                if k > 9:
                    return st["prev_ev"]
                st["k"] += 1
                if k == 1:
                    b.waite("tensor", ld_ev)
                    if pw_store_ref[0] is not None:
                        b.waite("vector", pw_store_ref[0][par])
                bka = b.bank(); bkb = b.bank()
                b.waite("tensor", bka.free_evt); b.waite("tensor", bkb.free_evt)
                if st["prev_ev"] is not None:
                    b.wait("tensor", S["vE"], st["prev_ev"])
                qp = pw(0) if k == 1 else pw(13 + k - 1)
                pp = pw(1) if k == 1 else pw(4 + k - 1)
                b.run("tensor", lambda e, bka=bka, qp=qp, pp=pp: e.matmul(
                    bka.ap[:, 0:128], qp, pp, start=True, stop=True))
                g = b.inc("tensor", S["tE"], lambda e, bkb=bkb, qp=qp, pp=pp: e.matmul(
                    bkb.ap[:, 0:128], pp, qp, start=True, stop=True))
                b.wait("vector", S["tE"], g)
                b.op("vector", lambda e, bka=bka, k=k, pw=pw: e.tensor_copy(pw(4 + k), bka.ap[:, 0:128]))
                st["prev_ev"] = b.op("vector", lambda e, bkb=bkb, k=k, pw=pw:
                                e.tensor_copy(pw(13 + k), bkb.ap[:, 0:128]))
                bka.free_evt = (S["vE"], st["prev_ev"]); bkb.free_evt = (S["vE"], st["prev_ev"])
            return st["prev_ev"]
        return step

    def ssm_powers(h, ld_ev):
        return ssm_powers_stepper(h, ld_ev)(9)

    def load_uh(h):
        sl = h % 2
        b.wait("scalar", S["tE"], ssmw_used[sl])
        b.waite("scalar", u_st[0])
        ev = b.dma("scalar", "uh%d" % sl, lambda e, h=h, sl=sl: e.dma_start(
            out=UH[:, sl * T:(sl + 1) * T], in_=u_dram[h * 128:(h + 1) * 128, :]))
        return sl, ev

    pw_store = [None, None]
    pw_store_ref[0] = pw_store
    ldh = {0: ssm_head_loads(0, "scan")}
    uhd = {0: load_uh(0)}
    pwev = {0: ssm_powers(0, ldh[0])}
    b.wait("scalar", S["vE"], pwev[0])
    pw_store[0] = b.dma("scalar", "pwst0", lambda e: e.dma_start(
        out=pw_dram[0:128, :], in_=SSMW[:, 5 * 128:14 * 128]))
    for h in range(NH):
        par = h % 2
        pw = lambda i, par=par: pwp(par, i)
        usl, u_ev = uhd[h]
        ev_v = ev_s = None
        for tau in range(2):
            bk = b.bank(); b.waite("tensor", bk.free_evt)
            b.waite("tensor", u_ev)
            b.waite("tensor", ldh[h])
            g = b.inc("tensor", S["tE"], lambda e, bk=bk, usl=usl, tau=tau, pw=pw: e.matmul(
                bk.ap[:, :], pw(2), UH[:, usl * T + tau * TT: usl * T + (tau + 1) * TT],
                start=True, stop=True))
            eng = "vector" if tau == 0 else "scalar"
            sem = S["vE"] if tau == 0 else S["sE"]
            b.wait(eng, S["tE"], g)
            b.waite(eng, xs_store[0])
            if tau == 0:
                ev_v = b.op(eng, lambda e, bk=bk, tau=tau: e.tensor_copy(
                    XS0[:, PAD + tau * TT: PAD + (tau + 1) * TT], bk.ap[:, :]))
                bk.free_evt = (S["vE"], ev_v)
            else:
                ev_s = b.op(eng, lambda e, bk=bk, tau=tau: e.activation(
                    XS0[:, PAD + tau * TT: PAD + (tau + 1) * TT], bk.ap[:, :], AF.Identity))
                bk.free_evt = (S["sE"], ev_s)
        pstep = None
        if h + 1 < NH:
            ldh[h + 1] = ssm_head_loads(h + 1, "scan")
            uhd[h + 1] = load_uh(h + 1)
            pstep = ssm_powers_stepper(h + 1, ldh[h + 1])
        cast_pump(2)
        PSCHED = [0, 0, 1, 1, 1, 1, 2, 2, 1, 0]
        pw_ev_last = None
        xsrc, xdst = XS0, XS1
        for k in range(10):
            sh = 1 << k
            nev_v = nev_s = None
            for tau in range(2):
                bk = b.bank(); b.waite("tensor", bk.free_evt)
                b.wait("tensor", S["vE"], ev_v)
                b.wait("tensor", S["sE"], ev_s)
                b.run("tensor", lambda e, bk=bk, xsrc=xsrc, tau=tau: e.matmul(
                    bk.ap[:, :], I128,
                    xsrc[:, PAD + tau * TT: PAD + (tau + 1) * TT], start=True, stop=False))
                g = b.inc("tensor", S["tE"], lambda e, bk=bk, xsrc=xsrc, tau=tau, k=k, sh=sh, pw=pw:
                          e.matmul(bk.ap[:, :], pw(1) if k == 0 else pw(4 + k),
                                   xsrc[:, PAD + tau * TT - sh: PAD + (tau + 1) * TT - sh],
                                   start=False, stop=True))
                if tau == 0:
                    b.wait("vector", S["tE"], g)
                    nev_v = b.op("vector", lambda e, bk=bk, xdst=xdst, tau=tau:
                               e.tensor_copy(xdst[:, PAD + tau * TT: PAD + (tau + 1) * TT],
                                             bk.ap[:, :]))
                    bk.free_evt = (S["vE"], nev_v)
                else:
                    b.wait("scalar", S["tE"], g)
                    nev_s = b.op("scalar", lambda e, bk=bk, xdst=xdst, tau=tau:
                               e.activation(xdst[:, PAD + tau * TT: PAD + (tau + 1) * TT],
                                            bk.ap[:, :], AF.Identity))
                    bk.free_evt = (S["sE"], nev_s)
            ev_v, ev_s = nev_v, nev_s
            xsrc, xdst = xdst, xsrc
            if pstep is not None:
                pw_ev_last = pstep(PSCHED[k])
        if h + 1 < NH:
            pwev[h + 1] = pw_ev_last
            b.wait("scalar", S["vE"], pwev[h + 1])
            pw_store[(h + 1) % 2] = b.dma("scalar", "pwst%d" % ((h + 1) % 2),
                lambda e, h1=h + 1: e.dma_start(
                    out=pw_dram[h1 * 128:(h1 + 1) * 128, :],
                    in_=SSMW[:, (23 * ((h1) % 2) + 5) * 128:(23 * ((h1) % 2) + 14) * 128]))
        ssmw_used[par] = S["tE"].n
        b.wait("vector", S["sE"], ev_s)
        car_ev = b.op("vector", lambda e, h=h: e.tensor_copy(
            CAR[:, h:h + 1], XS0[:, PAD + T - 1: PAD + T]))
        b.wait("scalar", S["vE"], car_ev)
        xs_store[0] = b.dma("scalar", "xst", lambda e, h=h: e.dma_start(
            out=s_dram[h * 128:(h + 1) * 128, :], in_=XS0[:, PAD:PAD + T]))
    if stop == "states":
        b.waite("sync", xs_store[0])
        ld = b.dma("sync", "out", lambda e: e.dma_start(
            out=BB1[:, :].rearrange("p (c t) -> p c t", c=NCH),
            in_=s_dram.ap().rearrange("(c p) t -> p c t", p=128)))
        b.waite("vector", ld)
        dump_bf16_full(BB1, None)
        return finish()

    b.scope('p4b_cc')
    b.wait("gpsimd", S["vE"], car_ev)
    b.wait("sync", S["vE"], car_ev)
    cin_ev = b.dma("sync", "ccio", lambda e: e.dma_start(
        out=cc1_in.ap(), in_=CAR[:, :]))
    b.waite("gpsimd", cin_ev)
    cc_ev = b.inc("gpsimd", S["cc"], lambda e: e.collective_compute(
        "AllGather", ALU.bypass, replica_groups=[list(range(8))],
        ins=[cc1_in.ap().opt()], outs=[cc1_out.ap().opt()]))
    b.wait("gpsimd", S["cc"], cc_ev)
    b.wait("sync", S["cc"], cc_ev)
    gg_ev = b.dma("sync", "ccio", lambda e: e.dma_start(
        out=GG[:, :].rearrange("p (r h) -> p r h", r=8),
        in_=cc1_out.ap().rearrange("(r i) h -> i r h", i=128)))
    b.waite("vector", gg_ev)
    b.waite("vector", cons_grp)
    b.op("vector", lambda e: e.tensor_scalar(
        HIN[:, :], GG[:, 0:NH], MASK[:, 0:1], None, ALU.mult))
    for r in range(1, 8):
        b.op("vector", lambda e, r=r: e.scalar_tensor_tensor(
            HIN[:, :], GG[:, r * NH:(r + 1) * NH], MASK[:, r:r + 1], HIN[:, :],
            ALU.mult, ALU.add))
    hinb_ev = b.op("vector", lambda e: e.tensor_copy(CARB[:, :], HIN[:, :]))

    b.scope('p4c_corr')
    sst_used = [0]
    VB2 = [VBt[:, 0:T], XQs[:, 0:2 * TT]]
    _bar4c = (S["tE"], S["tE"].n)
    vb_used = [_bar4c, _bar4c]

    def emit_vchain(h):
        par = h % 2
        pw = lambda i: pwp(par, i)
        vb = VB2[par]
        bk = b.bank(); b.waite("tensor", bk.free_evt)
        b.waite("tensor", ldh[h])
        b.wait("tensor", S["vE"], hinb_ev)
        g = b.inc("tensor", S["tE"], lambda e, bk=bk, h=h, pw=pw: e.matmul(
            bk.ap[:, 0:1], pw(1), CARB[:, h:h + 1], start=True, stop=True))
        b.wait("vector", S["tE"], g)
        b.waite("vector", vb_used[par])
        vev = b.op("vector", lambda e, bk=bk, vb=vb: e.tensor_copy(
            vb[:, 0:1], bk.ap[:, 0:1]))
        bk.free_evt = (S["vE"], vev)
        for k in range(10):
            w = 1 << k
            bk = b.bank(); b.waite("tensor", bk.free_evt)
            b.wait("tensor", S["vE"], vev)
            g = b.inc("tensor", S["tE"], lambda e, bk=bk, k=k, w=w, pw=pw, vb=vb: e.matmul(
                bk.ap[:, 0:w], pw(1) if k == 0 else pw(4 + k), vb[:, 0:w],
                start=True, stop=True))
            b.wait("vector", S["tE"], g)
            vev = b.op("vector", lambda e, bk=bk, w=w, vb=vb: e.tensor_copy(
                vb[:, w:2 * w], bk.ap[:, 0:w]))
            bk.free_evt = (S["vE"], vev)
        return vev

    b.waite("scalar", pw_store[0]); b.waite("scalar", pw_store[1])
    ldh = {0: ssm_head_loads(0, "corr")}
    uhd = {0: load_uh(0)}
    vch = {0: emit_vchain(0)}
    std = {}
    for h in range(NH):
        par = h % 2
        pw = lambda i, par=par: pwp(par, i)
        if h == 0:
            sl = 0
            b.wait("scalar", S["tE"], sst_used[0])
            b.waite("scalar", xs_store[0])
            std[0] = b.dma("scalar", "sst0", lambda e: e.dma_start(
                out=SST[:, 0:T], in_=s_dram[0:128, :]))
        st_ld = std[h]
        usl, u_ev = uhd[h]
        sl = h % 2
        if h + 1 < NH:
            ldh[h + 1] = ssm_head_loads(h + 1, "corr")
            uhd[h + 1] = load_uh(h + 1)
            sl2 = (h + 1) % 2
            b.wait("scalar", S["tE"], sst_used[0])
            std[h + 1] = b.dma("scalar", "sst%d" % sl2, lambda e, h=h, sl2=sl2: e.dma_start(
                out=SST[:, sl2 * T:(sl2 + 1) * T],
                in_=s_dram[(h + 1) * 128:(h + 2) * 128, :]))
            vch[h + 1] = emit_vchain(h + 1)
        cast_pump(2)
        b.waite("vector", st_ld)
        b.wait("vector", S["vE"], vch[h])
        add_ev = b.op("vector", lambda e, sl=sl: e.tensor_tensor(
            SST[:, sl * T:(sl + 1) * T], SST[:, sl * T:(sl + 1) * T],
            VB2[sl][:, 0:T], ALU.add))
        vb_used[sl] = (S["vE"], add_ev)
        for tau in range(2):
            bk = b.bank(); b.waite("tensor", bk.free_evt)
            b.wait("tensor", S["vE"], add_ev)
            b.waite("tensor", u_ev)
            b.run("tensor", lambda e, bk=bk, sl=sl, tau=tau, pw=pw: e.matmul(
                bk.ap[:, :], pw(3), SST[:, sl * T + tau * TT: sl * T + (tau + 1) * TT],
                start=True, stop=False))
            g = b.inc("tensor", S["tE"], lambda e, bk=bk, usl=usl, tau=tau, pw=pw: e.matmul(
                bk.ap[:, :], pw(4), UH[:, usl * T + tau * TT: usl * T + (tau + 1) * TT],
                start=False, stop=True))
            if tau == 0:
                b.wait("vector", S["tE"], g)
                yev = b.op("vector", lambda e, bk=bk, h=h, tau=tau: e.tensor_copy(
                    St[:, h * T + tau * TT: h * T + (tau + 1) * TT], bk.ap[:, :]))
                bk.free_evt = (S["vE"], yev)
            else:
                b.wait("scalar", S["tE"], g)
                yev = b.op("scalar", lambda e, bk=bk, h=h, tau=tau: e.activation(
                    St[:, h * T + tau * TT: h * T + (tau + 1) * TT], bk.ap[:, :], AF.Identity))
                bk.free_evt = (S["sE"], yev)
        ssmw_used[par] = S["tE"].n
        sst_used[0] = S["tE"].n
    b.ln_slot = [S["tE"].n, S["tE"].n]
    y_done = (S["vE"], S["vE"].n)
    y_done_s = (S["sE"], S["sE"].n)
    if stop == "yss":
        dump_f32_full(St, y_done)
        return finish()

    b.scope('p5_outproj')
    z_done = None
    for tau in range(2):
        cast_ev = None
        for c in range(NCH):
            b.waite("vector", y_done)
            b.wait("vector", S["sE"], y_done_s[1])
            cast_ev = b.op("vector", lambda e, c=c, tau=tau: e.tensor_copy(
                NB[:, c * TT:(c + 1) * TT], St[:, c * T + tau * TT: c * T + (tau + 1) * TT]))
        def evac_z(j, t2, bk, gend, tau=tau):
            b.wait("vector", S["tE"], gend)
            ev = b.op("vector", lambda e, bk=bk, j=j, tau=tau:
                       e.scalar_tensor_tensor(
                           St[:, j * T + tau * TT: j * T + (tau + 1) * TT],
                           bk.ap[:, :], vcol("out_b", j),
                           BB1[:, j * T + tau * TT: j * T + (tau + 1) * TT],
                           ALU.add, ALU.add))
            bk.free_evt = (S["vE"], ev)
            return ev
        evz = []
        def evac_z_wrap(j, t2, bk, gend):
            evz.append(evac_z(j, t2, bk, gend))
        emit_proj(dt["out_w"], lambda k, t2, tau=tau: NB[:, k * TT:(k + 1) * TT],
                  evac_z_wrap, src_evt=(S["vE"], cast_ev), taus=(tau,),
                  shared_tau=False, mode="f32cast")
        z_done = (S["vE"], evz[-1])
    if stop == "z":
        dump_f32_full(St, z_done)
        return finish()

    b.scope('p68_ffnA')
    def hidtile(c):
        buf = BB1 if c < 32 else TBu
        return buf[:, (c % 32) * TT:(c % 32 + 1) * TT]

    xt_used = [None, None]

    def emit_x0_add(tau):
        ev = None
        for c in range(NCH):
            sl = c % 2
            xt = XT[:, sl * TT:(sl + 1) * TT]
            b.waite("sync", xt_used[sl])
            ld = b.dma("sync", "xt%d" % sl, lambda e, c=c, tau=tau, xt=xt:
                       e.dma_start(out=xt, in_=x0h[c * 128:(c + 1) * 128,
                                                   tau * TT:(tau + 1) * TT]))
            b.waite("vector", ld)
            ev = b.op("vector", lambda e, c=c, tau=tau, xt=xt: e.tensor_tensor(
                St[:, c * T + tau * TT: c * T + (tau + 1) * TT],
                St[:, c * T + tau * TT: c * T + (tau + 1) * TT], xt, ALU.add))
            xt_used[sl] = (S["vE"], ev)
        return ev

    ln2_evs = {}
    ln2_evs[0] = emit_ln(
        src_ch=lambda c: St[:, c * T: c * T + TT],
        dst_ch=lambda c: NB[:, c * TT:(c + 1) * TT],
        w_key=("ln2w", "ln2b"), src_f32=True)
    emit_x0_add(0)
    x1_ev = [None]
    for tau in range(2):
        gelu_ev = {}
        def evac_gelu(j, t2, bk, gend):
            b.wait("scalar", S["tE"], gend)
            ev = b.op("scalar", lambda e, bk=bk, j=j: e.activation(
                hidtile(j), bk.ap[:, :], AF.Gelu, bias=b1col(j)))
            bk.free_evt = (S["sE"], ev)
            gelu_ev[j] = ev
        w1a_gend = emit_proj(dt["w1a"], lambda k, t2: NB[:, k * TT:(k + 1) * TT],
                  evac_gelu, n_k=NCH, n_j=HCH, src_evt=ln2_evs[tau],
                  taus=(tau,), shared_tau=False, mode="f32cast")
        if tau == 0:
            ln2_evs[1] = emit_ln(
                src_ch=lambda c: St[:, c * T + TT: c * T + 2 * TT],
                dst_ch=lambda c: NB[:, c * TT:(c + 1) * TT],
                w_key=("ln2w", "ln2b"), src_f32=True,
                dst_gate=(S["tE"], w1a_gend))
            emit_x0_add(1)
        def evac_w2a(j, t2, bk, gend, tau=tau):
            b.wait("vector", S["tE"], gend)
            ev = b.op("vector", lambda e, bk=bk, j=j, tau=tau:
                      e.scalar_tensor_tensor(
                          St[:, j * T + tau * TT: j * T + (tau + 1) * TT],
                          bk.ap[:, :], vcol("b2", j),
                          St[:, j * T + tau * TT: j * T + (tau + 1) * TT],
                          ALU.add, ALU.add))
            bk.free_evt = (S["vE"], ev)
            x1_ev[0] = ev
        emit_proj(dt["w2a"], lambda k, t2: hidtile(k),
                  evac_w2a, n_k=HCH, n_j=NCH, src_evt=(S["sE"], gelu_ev[HCH - 1]),
                  taus=(tau,), shared_tau=False, mode="f32cast")
    x1_done = (S["vE"], x1_ev[0])
    if stop == "x1":
        dump_f32_full(St, x1_done)
        return finish()

    b.scope('p_w2prep')
    prep_in_used = [None, None, None, None]
    prep_out_used = [None, None, None, None]
    PREP_OUT = [WA[:, 0:2048], WA[:, 2048:4096], UH[:, 0:2048], SST[:, 0:2048]]

    def prep_pass():
        st_ev = None
        for i in range(HCH):
            sl = i % 4
            inslot = NB[:, sl * 2048:(sl + 1) * 2048]
            chn = "prep_in%d" % sl
            b.waite("gpsimd", prep_in_used[sl])
            ld = b.dma("gpsimd", chn, lambda e, i=i, inslot=inslot: e.dma_start(
                out=inslot, in_=dt["fw2"][i * 128:(i + 1) * 128, :]))
            b.waite("tensor", ld)
            out_wa = PREP_OUT[sl]
            ev_ev = None
            for dtile in range(4):
                bk = b.bank(); b.waite("tensor", bk.free_evt)
                g = b.inc("tensor", S["tE"], lambda e, bk=bk, inslot=inslot,
                          dtile=dtile: e.matmul(
                    bk.ap[:, :], H128, inslot[:, dtile * TT:(dtile + 1) * TT],
                    start=True, stop=True))
                eng = "vector" if dtile % 2 == 0 else "scalar"
                b.wait(eng, S["tE"], g)
                if dtile == 0:
                    b.waite("vector", prep_out_used[sl])
                if dtile == 1:
                    b.waite("scalar", prep_out_used[sl])
                if eng == "vector":
                    ev_ev = b.op(eng, lambda e, bk=bk, dtile=dtile, out_wa=out_wa:
                                 e.tensor_copy(out_wa[:, dtile * TT:(dtile + 1) * TT],
                                               bk.ap[:, :]))
                    bk.free_evt = (S["vE"], ev_ev)
                else:
                    ev_s = b.op(eng, lambda e, bk=bk, dtile=dtile, out_wa=out_wa:
                                e.activation(out_wa[:, dtile * TT:(dtile + 1) * TT],
                                             bk.ap[:, :], AF.Identity))
                    bk.free_evt = (S["sE"], ev_s)
            prep_in_used[sl] = (S["tE"], S["tE"].n)
            b.wait("scalar", S["vE"], ev_ev)
            b.wait("scalar", S["sE"], ev_s)
            st = b.dma("scalar", "prep_st%d" % sl, lambda e, i=i, out_wa=out_wa:
                       e.dma_start(
                out=bass.AP(w2t3, i * D, [[128 * D, 64], [64 * D, 2], [1, D]]),
                in_=out_wa))
            prep_out_used[sl] = st
            st_ev = st
        return st_ev

    prep_bar = (S["tE"], S["tE"].n)
    for _q in range(4):
        prep_in_used[_q] = prep_bar
    b.wait("gpsimd", S["tE"], S["tE"].n)
    b.wait("vector", S["tE"], S["tE"].n)
    b.wait("scalar", S["tE"], S["tE"].n)
    prep_pass()
    for _q in range(4):
        b.waite("sync", prep_out_used[_q])
    p2_end = prep_out_used[3]
    b.wait("vector", S["tE"], S["tE"].n)
    b.wait("scalar", S["tE"], S["tE"].n)
    proj_state["wa_evt"] = [prep_out_used[0], prep_out_used[1]]

    def emit_ffn(x_stop=None):
        out_ev = [None]
        ln_evs = {}
        ln_evs[0] = emit_ln(
            src_ch=lambda c: St[:, c * T: c * T + TT],
            dst_ch=lambda c: NB[:, c * TT:(c + 1) * TT],
            w_key=None, src_f32=True)
        for tau in range(2):
            silu_ev = {}
            h64_ev = {}
            def evac_silu(j, t2, bk, gend):
                b.wait("scalar", S["tE"], gend)
                ev = b.op("scalar", lambda e, bk=bk, j=j: e.activation(
                    hidtile(j), bk.ap[:, :], AF.Silu))
                bk.free_evt = (S["sE"], ev)
                silu_ev[j] = ev
            emit_proj(w1T, lambda k, t2: NB[:, k * TT:(k + 1) * TT],
                      evac_silu, n_k=NCH, n_j=HCH, src_evt=ln_evs[tau],
                      taus=(tau,), shared_tau=False, mode="bf16",
                      w_ready="w1")
            def evac_lin(j, t2, bk, gend):
                b.wait("vector", S["tE"], gend)
                b.wait("vector", S["sE"], silu_ev[j])
                ev = b.op("vector", lambda e, bk=bk, j=j: e.tensor_tensor(
                    hidtile(j), hidtile(j), bk.ap[:, :], ALU.mult))
                bk.free_evt = (S["vE"], ev)
                silu_ev[j] = ev
                bh = b.bank(); b.waite("tensor", bh.free_evt)
                b.wait("tensor", S["vE"], ev)
                g2 = b.inc("tensor", S["tE"], lambda e, bh=bh, j=j: e.matmul(
                    bh.ap[:, :], H64BD, hidtile(j), start=True, stop=True))
                b.wait("scalar", S["tE"], g2)
                ev2 = b.op("scalar", lambda e, bh=bh, j=j: e.activation(
                    hidtile(j), bh.ap[:, :], AF.Identity))
                bh.free_evt = (S["sE"], ev2)
                h64_ev[j] = ev2
            w3_gend = emit_proj(w3T, lambda k, t2: NB[:, k * TT:(k + 1) * TT],
                      evac_lin, n_k=NCH, n_j=HCH, src_evt=ln_evs[tau],
                      taus=(tau,), shared_tau=False, mode="bf16",
                      w_ready="w3")
            if tau == 0:
                ln_evs[1] = emit_ln(
                    src_ch=lambda c: St[:, c * T + TT: c * T + 2 * TT],
                    dst_ch=lambda c: NB[:, c * TT:(c + 1) * TT],
                    w_key=None, src_f32=True,
                    dst_gate=(S["tE"], w3_gend))
            def evac_out(j, t2, bk, gend, tau=tau):
                b.wait("vector", S["tE"], gend)
                ev = b.op("vector", lambda e, bk=bk, j=j, tau=tau: e.tensor_tensor(
                    St[:, j * T + tau * TT: j * T + (tau + 1) * TT],
                    St[:, j * T + tau * TT: j * T + (tau + 1) * TT],
                    bk.ap[:, :], ALU.add))
                bk.free_evt = (S["vE"], ev)
                out_ev[0] = ev
            emit_proj(w2t3, lambda k, t2: hidtile(k),
                      evac_out, n_k=HCH, n_j=NCH,
                      src_evt=(S["sE"], h64_ev[HCH - 1]),
                      taus=(tau,), shared_tau=False, mode="bf16",
                      w_ready=p2_end)
        return (S["vE"], out_ev[0])

    b.scope('p_ffn1')
    x2_done = emit_ffn()
    if stop == "x2":
        dump_f32_full(St, x2_done)
        return finish()

    b.scope('p_attnb_proj')
    ones_ev = b.op("vector", lambda e: e.memset(VBt[:, :], 1.0))

    qr_ev = [None]
    for tau in range(2):
        ln_ev = emit_ln(
            src_ch=lambda c, tau=tau: St[:, c * T + tau * TT: c * T + (tau + 1) * TT],
            dst_ch=lambda c: NB[:, c * TT:(c + 1) * TT],
            w_key=None, src_f32=True)
        for (whbm, dstbuf) in ((dt["wq"], TBu), (dt["wkv"], BB1)):
            def evac_raw(j, t2, bk, gend, dstbuf=dstbuf, tau=tau):
                b.wait("vector", S["tE"], gend)
                ev = b.op("vector", lambda e, bk=bk, j=j, tau=tau, dstbuf=dstbuf:
                          e.tensor_copy(dstbuf[:, j * T + tau * TT: j * T + (tau + 1) * TT],
                                        bk.ap[:, :]))
                bk.free_evt = (S["vE"], ev)
                qr_ev[0] = ev
            emit_proj(whbm, lambda k, t2: NB[:, k * TT:(k + 1) * TT],
                      evac_raw, n_k=NCH, n_j=NCH, src_evt=ln_ev,
                      taus=(tau,), shared_tau=False, mode="f32cast")

    b.scope('p_attnb_norm')
    nrm_state = {"i": 0, "sq": [0, 0], "srow": [0, 0, 0, 0],
                 "rb": [0, 0], "xb": [0, 0]}

    def emit_normalize(buf, h, tau):
        ns = nrm_state
        i = ns["i"]; ns["i"] += 1
        tile = buf[:, h * T + tau * TT: h * T + (tau + 1) * TT]
        sq = XQs[:, (i % 2) * TT:(i % 2 + 1) * TT]
        b.wait("gpsimd", S["tE"], ns["sq"][i % 2])
        if ns.get("src_ev") is not None:
            b.wait("gpsimd", S["vE"], ns["src_ev"])
        sq_ev = b.op("gpsimd", lambda e, sq=sq, tile=tile:
                     e.tensor_tensor(sq, tile, tile, ALU.mult))
        bk = b.bank(); b.waite("tensor", bk.free_evt)
        b.wait("tensor", S["gE"], sq_ev)
        g = b.inc("tensor", S["tE"], lambda e, bk=bk, sq=sq: e.matmul(
            bk.ap[0:1, :], ONESCOL_B, sq, start=True, stop=True))
        ns["sq"][i % 2] = g
        s_row = ROWS[0:1, (i % 4) * TT:(i % 4 + 1) * TT]
        b.wait("scalar", S["tE"], g)
        b.wait("scalar", S["vE"], ns["srow"][i % 4])
        sc_ev = b.op("scalar", lambda e, bk=bk, s_row=s_row: e.activation(
            s_row, bk.ap[0:1, :], AF.Sqrt, scale=128.0))
        bk.free_evt = (S["sE"], sc_ev)
        b.wait("vector", S["sE"], sc_ev)
        b.op("vector", lambda e, s_row=s_row: e.reciprocal(s_row, s_row))
        rb_row = RB[0:1, (i % 2) * TT:(i % 2 + 1) * TT]
        b.wait("vector", S["tE"], ns["rb"][i % 2])
        rb_ev = b.op("vector", lambda e, rb_row=rb_row, s_row=s_row:
                     e.tensor_copy(rb_row, s_row))
        ns["srow"][i % 4] = rb_ev
        bc = b.bank(); b.waite("tensor", bc.free_evt)
        b.wait("tensor", S["vE"], rb_ev)
        g2 = b.inc("tensor", S["tE"], lambda e, bc=bc, rb_row=rb_row: e.matmul(
            bc.ap[:, :], ONESROW_B, rb_row, start=True, stop=True))
        ns["rb"][i % 2] = g2
        b.wait("scalar", S["tE"], g2)
        xb = XBs[:, (i % 2) * TT:(i % 2 + 1) * TT]
        b.waite("scalar", ns["xb_used"][i % 2])
        cp_ev = b.op("scalar", lambda e, bc=bc, xb=xb: e.activation(
            xb, bc.ap[:, :], AF.Identity))
        bc.free_evt = (S["sE"], cp_ev)
        b.wait("vector", S["sE"], cp_ev)
        nrm_ev = b.op("vector", lambda e, tile=tile, xb=xb:
                      e.tensor_tensor(tile, tile, xb, ALU.mult))
        ns["xb_used"][i % 2] = (S["vE"], nrm_ev)
        return nrm_ev

    nrm_state["sq"] = [b.ln_slot[0], b.ln_slot[1]]
    nrm_state["rb"] = [b.rb_consumed, b.rb_consumed]
    nrm_state["srow"] = [S["vE"].n] * 4
    nrm_state["xb_used"] = [(S["vE"], S["vE"].n), (S["vE"], S["vE"].n)]
    nrm_state["src_ev"] = qr_ev[0]
    kv_norm_ev = None
    for h in range(NH):
        for tau in range(2):
            emit_normalize(TBu, h, tau)
            kv_norm_ev = emit_normalize(BB1, h, tau)
    b.ln_slot = [max(nrm_state["sq"]), max(nrm_state["sq"])]
    b.rb_consumed = max(nrm_state["rb"])

    b.waite("vector", cin_ev)
    scan_evs = {}
    for h in range(NH):
        sc_ev = b.op("vector", lambda e, h=h: e.tensor_tensor_scan(
            BB1[:, h * T:(h + 1) * T], VBt[:, :], BB1[:, h * T:(h + 1) * T],
            0.0, ALU.mult, ALU.add))
        car_ev = b.op("vector", lambda e, h=h: e.tensor_copy(
            CAR[:, h:h + 1], BB1[:, h * T + T - 1: h * T + T]))
        scan_evs[h] = car_ev

    b.wait("gpsimd", S["vE"], scan_evs[NH - 1])
    b.wait("gpsimd", S["vE"], hinb_ev)
    b.wait("sync", S["vE"], scan_evs[NH - 1])
    cin2 = b.dma("sync", "ccio", lambda e: e.dma_start(
        out=cc2_in.ap(), in_=CAR[:, :]))
    b.waite("gpsimd", cin2)
    cc2_ev = b.inc("gpsimd", S["cc"], lambda e: e.collective_compute(
        "AllGather", ALU.bypass, replica_groups=[list(range(8))],
        ins=[cc2_in.ap().opt()], outs=[cc2_out.ap().opt()]))
    b.wait("gpsimd", S["cc"], cc2_ev)
    b.wait("sync", S["cc"], cc2_ev)
    b.wait("sync", S["vE"], hinb_ev)
    gg2 = b.dma("sync", "ccio", lambda e: e.dma_start(
        out=GG[:, :].rearrange("p (r h) -> p r h", r=8),
        in_=cc2_out.ap().rearrange("(r i) h -> i r h", i=128)))
    b.waite("vector", gg2)
    b.op("vector", lambda e: e.tensor_scalar(
        HIN[:, :], GG[:, 0:NH], MASK[:, 0:1], None, ALU.mult))
    for r in range(1, 8):
        b.op("vector", lambda e, r=r: e.scalar_tensor_tensor(
            HIN[:, :], GG[:, r * NH:(r + 1) * NH], MASK[:, r:r + 1], HIN[:, :],
            ALU.mult, ALU.add))
    addc_ev = {}
    for h in range(NH):
        addc_ev[h] = b.op("vector", lambda e, h=h: e.tensor_scalar(
            BB1[:, h * T:(h + 1) * T], BB1[:, h * T:(h + 1) * T],
            HIN[:, h:h + 1], None, ALU.add))

    b.scope('p_attnb_fft')
    val_ev = None
    for h in range(NH):
        for tau in range(2):
            qt = TBu[:, h * T + tau * TT: h * T + (tau + 1) * TT]
            kt = BB1[:, h * T + tau * TT: h * T + (tau + 1) * TT]
            fs = [XS0[:, 0:TT], XS0[:, TT:2 * TT], XS0[:, 2 * TT:3 * TT],
                  XS1[:, 0:TT], XS1[:, TT:2 * TT]]
            banks = []
            b.wait("tensor", S["vE"], addc_ev[h])
            for (lhs, src) in ((DFTC, qt), (DFTS, qt), (DFTC, kt), (DFTS, kt)):
                bk = b.bank(); b.waite("tensor", bk.free_evt)
                g = b.inc("tensor", S["tE"], lambda e, bk=bk, lhs=lhs, src=src:
                          e.matmul(bk.ap[:, :], lhs, src, start=True, stop=True))
                banks.append((bk, g))
            for i, (bk, g) in enumerate(banks):
                b.wait("vector", S["tE"], g)
                ev = b.op("vector", lambda e, bk=bk, i=i: e.tensor_copy(
                    fs[i], bk.ap[:, :]))
                bk.free_evt = (S["vE"], ev)
            b.op("vector", lambda e: e.tensor_tensor(fs[4], fs[1], fs[2], ALU.mult))
            b.op("vector", lambda e: e.tensor_tensor(fs[2], fs[0], fs[2], ALU.mult))
            b.op("vector", lambda e: e.tensor_tensor(fs[0], fs[0], fs[3], ALU.mult))
            b.op("vector", lambda e: e.tensor_tensor(fs[3], fs[1], fs[3], ALU.mult))
            b.op("vector", lambda e: e.tensor_tensor(fs[2], fs[2], fs[3], ALU.subtract))
            pr_ev = b.op("vector", lambda e: e.tensor_tensor(fs[0], fs[0], fs[4], ALU.add))
            bk = b.bank(); b.waite("tensor", bk.free_evt)
            b.wait("tensor", S["vE"], pr_ev)
            b.run("tensor", lambda e, bk=bk: e.matmul(
                bk.ap[:, :], IDFTC, fs[2], start=True, stop=False))
            g = b.inc("tensor", S["tE"], lambda e, bk=bk: e.matmul(
                bk.ap[:, :], IDFTS, fs[0], start=False, stop=True))
            b.wait("vector", S["tE"], g)
            val_ev = b.op("vector", lambda e, bk=bk, qt=qt: e.tensor_copy(
                qt, bk.ap[:, :]))
            bk.free_evt = (S["vE"], val_ev)

    b.scope('p_attnb_wo')
    x3_ev = [None]
    for tau in range(2):
        def evac_wo(j, t2, bk, gend, tau=tau):
            b.wait("vector", S["tE"], gend)
            ev = b.op("vector", lambda e, bk=bk, j=j, tau=tau: e.tensor_tensor(
                St[:, j * T + tau * TT: j * T + (tau + 1) * TT],
                St[:, j * T + tau * TT: j * T + (tau + 1) * TT],
                bk.ap[:, :], ALU.add))
            bk.free_evt = (S["vE"], ev)
            x3_ev[0] = ev
        emit_proj(dt["wo"], lambda k, t2, tau=tau: TBu[:, k * T + tau * TT: k * T + (tau + 1) * TT],
                  evac_wo, n_k=NCH, n_j=NCH, src_evt=(S["vE"], val_ev),
                  taus=(tau,), shared_tau=False, mode="f32cast")
    x3_done = (S["vE"], x3_ev[0])
    if stop == "x3":
        dump_f32_full(St, x3_done)
        return finish()

    b.scope('p_ffn2')
    x4_done = emit_ffn()
    if stop == "x4":
        dump_f32_full(St, x4_done)
        return finish()

    b.scope('p_out')
    b.waite("sync", x4_done)
    out_ev = b.dma("sync", "out", lambda e: e.dma_start(
        out=y_out.ap().rearrange("(c p) t -> p c t", p=128),
        in_=St[:, :].rearrange("p (c t) -> p c t", c=NCH)))
    b.waite("sync", out_ev)
    return finish()


def _consts_b():
    blocks = []
    blocks.append(np.ones((128, 128), np.float32))
    blocks.append(np.eye(128, dtype=np.float32))
    def hadamard(n):
        h = np.array([[1.0]], np.float32)
        while h.shape[0] < n:
            h = np.block([[h, h], [h, -h]])
        return h
    blocks.append(hadamard(128))
    h64 = hadamard(64) / np.sqrt(8192.0)
    bd = np.zeros((128, 128), np.float32)
    bd[:64, :64] = h64; bd[64:, 64:] = h64
    blocks.append(bd)
    j = np.arange(128)
    ang = 2.0 * np.pi * np.outer(j, j) / 128.0
    C = np.cos(ang).astype(np.float32); Sm = np.sin(ang).astype(np.float32)
    blocks.append(C); blocks.append(Sm)
    blocks.append(C / 128.0); blocks.append(Sm / 128.0)
    blocks.append(np.zeros((128, 128), np.float32))
    blocks.append(np.zeros((128, 128), np.float32))
    return np.concatenate(blocks, 1).astype(ml_dtypes.bfloat16)


def _hid_perm():
    j = np.arange(HID)
    return (j % 64) * 128 + 2 * (j // 128) + (j % 128) // 64


def make_inputs(full):
    g = lambda k: np.asarray(full[k])
    x = g("L0_attn_a_x")
    vkeys = [g("L0_attn_a_ln1_w"), g("L0_attn_a_ln1_b"),
             g("L0_attn_a_ln2_w"), g("L0_attn_a_ln2_b"),
             g("L0_attn_a_in_b"), g("L0_attn_a_out_b"), g("L0_attn_a_b2")]
    vcols = np.concatenate(
        [np.asarray(v, np.float32).reshape(16, 128).T for v in vkeys], 1)
    b1cols = np.ascontiguousarray(
        g("L0_attn_a_b1").astype(np.float32).reshape(64, 128).T)
    A_ = g("L0_attn_a_A"); Bm_ = g("L0_attn_a_Bm")
    Cm_ = g("L0_attn_a_Cm"); Dm_ = g("L0_attn_a_Dm")
    blocks = []
    for h in range(NH):
        blocks += [A_[h], A_[h].T, Bm_[h].T, Cm_[h].T, Dm_[h].T]
    ssm = np.ascontiguousarray(np.concatenate(blocks, 1), np.float32)
    shared = dict(
        in_w=np.ascontiguousarray(g("L0_attn_a_in_w"), np.float32),
        out_w=np.ascontiguousarray(g("L0_attn_a_out_w"), np.float32),
        w1a=np.ascontiguousarray(g("L0_attn_a_w1"), np.float32),
        w2a=np.ascontiguousarray(g("L0_attn_a_w2"), np.float32),
        ssm_mats=ssm,
        wq=np.ascontiguousarray(g("L0_attn_b_W_q").T, np.float32),
        wkv=np.ascontiguousarray(g("L0_attn_b_W_kv").T, np.float32),
        wo=np.ascontiguousarray(g("L0_attn_b_W_o").T, np.float32),
        fw1=np.ascontiguousarray(g("L0_ffn_W1").T[:, _hid_perm()], np.float32),
        fw2=np.ascontiguousarray(g("L0_ffn_W2").T, np.float32),
        fw3=np.ascontiguousarray(g("L0_ffn_W3").T[:, _hid_perm()], np.float32),
        vcolsin=np.ascontiguousarray(vcols, np.float32),
        b1colsin=b1cols, consts_b=_consts_b(),
        onescol_f=np.ones((128, 1), np.float32),
    )
    in_maps = []
    for core in range(8):
        bb, half = core // 2, core % 2
        xs = np.ascontiguousarray(x[bb, half * T:(half + 1) * T, :].T.astype(np.float32))
        maskc = np.zeros((128, 8), np.float32)
        if half == 1:
            maskc[:, core - 1] = 1.0
        m = dict(shared)
        m["x0"] = xs
        m["maskc"] = maskc
        in_maps.append(m)
    return in_maps


def kernel(**inputs):
    in_maps = make_inputs(inputs)
    nc = build_program(stop=None)
    res = run_bass_kernel_spmd(nc, in_maps, core_ids=list(range(8)))
    out = np.empty((4, 2048, 2048), np.float32)
    for core in range(8):
        bb, half = core // 2, core % 2
        out[bb, half * T:(half + 1) * T, :] = res.results[core]["y"].T
    return out
